# version 07
# baseline (speedup 1.0000x reference)
import numpy as np
from contextlib import ExitStack

import concourse.bass as bass
import concourse.mybir as mybir
from concourse.bass_utils import run_bass_kernel_spmd

F32 = mybir.dt.float32
BF16 = mybir.dt.bfloat16
I32 = mybir.dt.int32
AF = mybir.ActivationFunctionType
ALU = mybir.AluOpType
AX = mybir.AxisListType

D = 1024
NTOK = 4096
NG = 8
TG = 512
NC = 8
CI = 2816
NIC = 22
NH = 11
NTILE = 4
EPS_LN = 1e-5
EPS_DEN = 1e-6
NCORES = 8

ENGS = ("sync", "scalar", "vector", "tensor", "gpsimd")


class Sem:
    def __init__(self, h):
        self.h = h
        self.n = 0


class Prog:
    def __init__(self, nc, ctx):
        self.nc = nc
        self.ctx = ctx
        self.items = {e: [] for e in ENGS}

    def sem(self, name):
        return Sem(self.ctx.enter_context(self.nc.semaphore(name)))

    def pool(self, name, k):
        return [self.sem(f"{name}{i}") for i in range(k)]

    def op(self, eng, fn, sig=None):
        if sig is None:
            self.items[eng].append(fn)
            return None
        sig.n += 1
        self.items[eng].append(lambda e, f=fn, s=sig: f(e).then_inc(s.h, 1))
        return sig.n

    def dma(self, eng, pool, idx, fn):
        s = pool[idx % len(pool)]
        s.n += 16
        self.items[eng].append(lambda e, f=fn, ss=s: f(e).then_inc(ss.h, 16))
        return (s, s.n)

    def wait(self, eng, sem, n=None):
        v = sem.n if n is None else n
        if v > 0:
            self.items[eng].append(lambda e, s=sem, vv=v: e.wait_ge(s.h, vv))

    def wait_m(self, eng, mark):
        s, v = mark
        if v > 0:
            self.items[eng].append(lambda e, ss=s, vv=v: e.wait_ge(ss.h, vv))

    def snap(self, pool):
        return [(s, s.n) for s in pool]

    def wait_snap(self, eng, snap):
        for s, v in snap:
            if v > 0:
                self.items[eng].append(
                    lambda e, ss=s, vv=v: e.wait_ge(ss.h, vv))

    def drain(self, eng, pool):
        self.wait_snap(eng, self.snap(pool))

    def fence(self, eng, sem):
        self.op(eng, lambda ee: ee.nop(), sig=sem)
        self.wait(eng, sem, sem.n)

    def barrier(self, engsems):
        snap = []
        for e, s in engsems:
            if e != "tensor":
                self.op(e, lambda ee: ee.nop(), sig=s)
            snap.append((e, s, s.n))
        for e, _, _ in snap:
            for o, s, v in snap:
                if o != e:
                    self.wait(e, s, v)
        return [(s, v) for _, s, v in snap]




def _finalize(p, bar_engs):
    nc = p.nc
    with nc.Block() as block:
        block.sync(lambda e: [f(e) for f in p.items["sync"]])
        block.scalar(lambda e: [f(e) for f in p.items["scalar"]])
        block.vector(lambda e: [f(e) for f in p.items["vector"]])
        block.tensor(lambda e: [f(e) for f in p.items["tensor"]])
        block.gpsimd(lambda e: [f(e) for f in p.items["gpsimd"]])

def build(passes=("A", "B1", "B2", "C")):
    nc = bass.Bass("TRN2", target_bir_lowering=False,
                   detect_race_conditions=False)

    def din(name, shape, dtype=F32):
        return nc.dram_tensor(name, shape, dtype, kind="ExternalInput").ap()

    xT = din("xT", [D, NTOK])
    wqT_d = din("wqT", [D, D]); wkT_d = din("wkT", [D, D]); wvT_d = din("wvT", [D, D])
    bq_pack = din("bq_pack", [128, NC])
    bk_row = din("bk_row", [1, D]); bv_row = din("bv_row", [1, D])
    gamma_pack = din("gamma_pack", [128, NC]); beta_pack = din("beta_pack", [128, NC])
    clip2 = din("clip2", [1, 2])
    wgT_d = din("wgT", [D, CI]); wuT_d = din("wuT", [D, CI])
    wg_raw = din("wg_raw", [CI, D]); wu_raw = din("wu_raw", [CI, D])
    wd_raw = din("wd_raw", [D, CI])
    wdT_d = din("wdT", [CI, D])
    ident_d = din("ident", [128, 128])
    ec_d = din("ec", [16, 1024])
    emask_d = din("emask", [128, 128])

    outT = nc.dram_tensor("outT", [128, NC * NTOK], F32, kind="ExternalOutput").ap()

    _k = "ExternalOutput" if "DBG" in passes else "Internal"
    xn_s = nc.dram_tensor("xn_s", [128, NC * NTOK], F32, kind=_k).ap()
    xnh_s = nc.dram_tensor("xnh_s", [128, NC * NTOK], BF16, kind=_k).ap()
    xnl_s = nc.dram_tensor("xnl_s", [128, NC * NTOK], BF16, kind=_k).ap()
    s_s = nc.dram_tensor("s_s", [128, NC * NTOK], F32, kind=_k).ap()
    xq_s = nc.dram_tensor("xq_s", [128, NC * NTOK], BF16, kind=_k).ap()
    hp_s = nc.dram_tensor("hp_s", [128, NIC * NTOK], F32, kind=_k).ap()
    dbg_s = nc.dram_tensor("dbg_s", [128, 8192], F32, kind=_k).ap()
    dbg_s2 = nc.dram_tensor("dbg_s2", [128, 8192], F32, kind=_k).ap()
    cc_in1 = nc.dram_tensor("cc_in1", [1, 1], F32, kind="Internal").ap()
    cc_out1 = nc.dram_tensor("cc_out1", [1, 1], F32, kind="Internal").ap()
    cc_in2 = nc.dram_tensor("cc_in2", [1, 1], F32, kind="Internal").ap()
    cc_out2 = nc.dram_tensor("cc_out2", [1, 1], F32, kind="Internal").ap()

    with ExitStack() as ctx:
        def sb(name, shape, dtype=F32):
            return ctx.enter_context(nc.sbuf_tensor(name, shape, dtype))

        p = Prog(nc, ctx)

        s_ident = sb("s_ident", [128, 128])
        s_ec = sb("s_ec", [16, 1024])
        ones_col = sb("ones_col", [128, 1])
        ones_col_b = sb("ones_col_b", [128, 1], BF16)
        ones_row = sb("ones_row", [1, 128])
        ones_row_b = sb("ones_row_b", [1, 128], BF16)
        s_bq = sb("s_bq", [128, NC])
        s_gamma = sb("s_gamma", [128, NC])
        s_beta = sb("s_beta", [128, NC])
        s_clip = sb("s_clip", [1, 2])
        s_sig = sb("s_sig", [1, 2])
        s_eps = sb("s_eps", [1, 1])
        smalls = sb("smalls", [1, 16])
        gmax1 = sb("gmax1", [1, 1]); gmax2 = sb("gmax2", [1, 1])
        mxcol = sb("mxcol", [128, 1]); mxcol2 = sb("mxcol2", [128, 1])
        rsx_bc = sb("rsx_bc", [128, 1]); rsh_bc = sb("rsh_bc", [128, 1])
        sx_bc = sb("sx_bc", [128, 1]); sh_bc = sb("sh_bc", [128, 1])
        kvblk = sb("kvblk", [128, NC * 128])
        ksum_fm = sb("ksum_fm", [128, NC])
        ksum_blk = sb("ksum_blk", [128, NC * 16])
        ksum_ex = sb("ksum_ex", [128, NC * 128])
        s_emask = sb("s_emask", [128, 128])
        k_acc = sb("k_acc", [128, D])
        swg = sb("swg", [128, NIC]); swu = sb("swu", [128, NIC])
        swd = sb("swd", [128, NC])
        dq_g = sb("dq_g", [128, NIC]); dq_u = sb("dq_u", [128, NIC])
        dq_d = sb("dq_d", [128, NC])
        row_a = sb("row_a", [1, TG]); row_b = sb("row_b", [1, TG])
        row_c = sb("row_c", [1, TG]); row_d = sb("row_d", [1, TG])

        if "SEMSHIFT" in passes:
            _dummy = [p.sem(f"dummy{i}") for i in range(10)]
        VE = p.sem("ve"); SC = p.sem("sc"); TE = p.sem("te"); GP = p.sem("gp")
        CC = p.sem("cc")
        PCONST = p.pool("pconst", 1)
        PWLD = p.pool("pwld", 2)
        PLDX = p.pool("pldx", 4)
        PWR = p.pool("pwr", 2)
        PSTX = p.pool("pstx", 4)
        PXQ = p.pool("pxq", 2)
        PHP = p.pool("php", 4)
        PHPL = p.pool("phpl", 4)
        SWQ = p.pool("swq", 1)
        POUT = p.pool("pout", 4)
        SCC = p.pool("scc", 4)

        bar_engs = (("scalar", SC), ("vector", VE), ("tensor", TE),
                    ("gpsimd", GP))

        for t_, s_ in ((ident_d, s_ident), (ec_d, s_ec), (emask_d, s_emask),
                       (bq_pack, s_bq),
                       (gamma_pack, s_gamma), (beta_pack, s_beta),
                       (clip2, s_clip)):
            mconst = p.dma("sync", PCONST, 0, lambda e, t_=t_, s_=s_:
                           e.dma_start(out=s_[:], in_=t_))

        p.op("vector", lambda e: e.memset(ones_col[:], 1.0))
        p.op("vector", lambda e: e.memset(ones_col_b[:], 1.0))
        p.op("vector", lambda e: e.memset(ones_row[:], 1.0))
        p.op("vector", lambda e: e.memset(ones_row_b[:], 1.0))
        p.op("vector", lambda e: e.memset(s_eps[:], EPS_LN))
        p.op("vector", lambda e: e.memset(mxcol[:], 0.0))
        p.op("vector", lambda e: e.memset(ksum_blk[:], 0.0))
        p.op("vector", lambda e: e.memset(kvblk[:], 0.0))
        p.op("gpsimd", lambda e: e.memset(k_acc[:], 0.0), sig=GP)

        p.wait_m("scalar", mconst)
        p.op("scalar", lambda e: e.activation(out=s_sig[:], in_=s_clip[:],
                                              func=AF.Sigmoid), sig=SC)
        p.wait_m("vector", mconst)

        _a_stage = "A"
        for _s in ("A0", "A1", "A2"):
            if _s in passes:
                _a_stage = _s
        _do_groups = _a_stage != "A0"
        _do_kvproj = _a_stage in ("A2", "A")
        _do_kv = _a_stage == "A"
        _ngroups = NG
        for _s in passes:
            if _s.startswith("G"):
                _ngroups = int(_s[1:])
        _no_store = "NOSTORE" in passes
        _no_stats = "NOSTATS" in passes
        _no_rowmax = "NORM" in passes
        with ExitStack() as actx:
            def sba(name, shape, dtype=F32):
                return actx.enter_context(nc.sbuf_tensor(name, shape, dtype))

            s_bkh = sba("s_bkh", [1, D], BF16); s_bkl = sba("s_bkl", [1, D], BF16)
            s_bvh = sba("s_bvh", [1, D], BF16); s_bvl = sba("s_bvl", [1, D], BF16)
            wk_h = sba("wk_h", [128, NC * D], BF16); wk_l = sba("wk_l", [128, NC * D], BF16)
            wv_h = sba("wv_h", [128, NC * D], BF16); wv_l = sba("wv_l", [128, NC * D], BF16)
            wld_t = sba("wld_t", [128, 2 * D])
            wraw_t = sba("wraw_t", [128, 2 * D])
            xg = sba("xg", [128, 2 * NC * TG])
            xh_t = sba("xh_t", [128, 2 * TG], BF16)
            xl_t = sba("xl_t", [128, 2 * TG], BF16)
            xsq_t = sba("xsq_t", [128, 2 * TG])
            xn_t = sba("xn_t", [128, 2 * TG])
            xnh_t = sba("xnh_t", [128, NC * TG], BF16)
            xnl_t = sba("xnl_t", [128, NC * TG], BF16)
            k_tile = sba("k_tile", [128, D]); v_tile = sba("v_tile", [128, D])

            ps_kv = actx.enter_context(nc.psum_tensor("ps_kv", [128, NC * 128], F32))
            ps_k = actx.enter_context(nc.psum_tensor("ps_k", [128, D], F32))
            ps_v = actx.enter_context(nc.psum_tensor("ps_v", [128, D], F32))
            ps_st = actx.enter_context(nc.psum_tensor("ps_st", [128, 2 * TG], F32))

            p.dma("sync", PWLD, 0, lambda e: e.dma_start(
                out=wld_t[0:1, 0:D], in_=bk_row))
            mbias = p.dma("sync", PWLD, 0, lambda e: e.dma_start(
                out=wld_t[0:1, D:2 * D], in_=bv_row))
            p.wait_m("vector", mbias)
            p.op("vector", lambda e: e.tensor_copy(out=s_bkh[:],
                                                   in_=wld_t[0:1, 0:D]))
            p.op("vector", lambda e: e.tensor_tensor(
                out=s_bkl[:], in0=wld_t[0:1, 0:D], in1=s_bkh[:],
                op=ALU.subtract))
            p.op("vector", lambda e: e.tensor_copy(out=s_bvh[:],
                                                   in_=wld_t[0:1, D:2 * D]))
            ve_bias = p.op("vector", lambda e: e.tensor_tensor(
                out=s_bvl[:], in0=wld_t[0:1, D:2 * D], in1=s_bvh[:],
                op=ALU.subtract), sig=VE)
            p.wait("sync", VE, ve_bias)

            wsplit_marks = []
            idx = 0
            for wi, (src, dh, dl) in enumerate(
                    ((wkT_d, wk_h, wk_l), (wvT_d, wv_h, wv_l))):
                for c in range(NC):
                    slot = idx % 2
                    if idx >= 2:
                        p.wait("sync", VE, wsplit_marks[idx - 2])
                    m = p.dma("sync", PWLD, idx, lambda e, src=src, c=c,
                              slot=slot: e.dma_start(
                                  out=wld_t[:, slot * D:(slot + 1) * D],
                                  in_=src[128 * c:128 * (c + 1), :]))
                    p.wait_m("vector", m)
                    p.op("vector", lambda e, dh=dh, c=c, slot=slot:
                         e.tensor_copy(
                             out=dh[:, D * c:D * (c + 1)],
                             in_=wld_t[:, slot * D:(slot + 1) * D]))
                    wsplit_marks.append(p.op(
                        "vector", lambda e, dh=dh, dl=dl, c=c, slot=slot:
                        e.tensor_tensor(
                            out=dl[:, D * c:D * (c + 1)],
                            in0=wld_t[:, slot * D:(slot + 1) * D],
                            in1=dh[:, D * c:D * (c + 1)], op=ALU.subtract),
                        sig=VE))
                    idx += 1

            rowmax_jobs = []
            for wi, (raw, dst) in enumerate(((wg_raw, swg), (wu_raw, swu))):
                for j in range(NIC):
                    rowmax_jobs.append((raw, dst, j))
            ve_rm_marks = []

            def emit_rowmax(job_idx):
                raw, dst, j = rowmax_jobs[job_idx]
                slot = job_idx % 2
                if job_idx >= 2:
                    p.wait("scalar", VE, ve_rm_marks[job_idx - 2])
                m = p.dma("scalar", PWR, job_idx, lambda e, raw=raw, j=j,
                          slot=slot: e.dma_start(
                              out=wraw_t[:, slot * D:(slot + 1) * D],
                              in_=raw[128 * j:128 * (j + 1), :]))
                p.wait_m("vector", m)
                ve_rm_marks.append(p.op(
                    "vector", lambda e, dst=dst, j=j, slot=slot:
                    e.reduce_max(out=dst[:, j:j + 1],
                                 in_=wraw_t[:, slot * D:(slot + 1) * D],
                                 axis=AX.X, apply_absolute_value=True),
                    sig=VE))

            ldx_marks = []
            bar_marks = [0] * (NG + 2)
            stx_snaps = [None] * (NG + 2)
            for g in range(_ngroups if _do_groups else 0):
                slot = g % 2
                gbase = slot * NC * TG
                if g >= 2:
                    p.wait_snap("sync", bar_marks[g - 2])
                for c in range(NC):
                    m = p.dma("sync", PLDX, g, lambda e, g=g, c=c, gbase=gbase:
                              e.dma_start(
                                  out=xg[:, gbase + TG * c:gbase + TG * (c + 1)],
                                  in_=xT[128 * c:128 * (c + 1),
                                         TG * g:TG * (g + 1)]))
                ldx_marks.append(m)
                if g >= 1:
                    p.wait_snap("scalar", stx_snaps[g - 1])
                    p.wait_snap("vector", stx_snaps[g - 1])

                lo = (len(rowmax_jobs) * g) // NG
                hi = (len(rowmax_jobs) * (g + 1)) // NG
                for ji in range(lo, hi):
                    if not _no_rowmax:
                        emit_rowmax(ji)

                p.wait_m("vector", ldx_marks[g])
                p.wait_m("scalar", ldx_marks[g])
                if _no_stats:
                    if "BARNOP" in passes:
                        for _e, _s in bar_engs:
                            p.op(_e, lambda ee: ee.nop())
                    elif "BARINC" in passes:
                        for _e, _s in bar_engs:
                            p.op(_e, lambda ee: ee.nop(), sig=_s)
                    elif any(_x.startswith("BW") for _x in passes):
                        sel = [_x for _x in passes if _x.startswith("BW")][0][2:]
                        snap = []
                        for _e, _s in bar_engs:
                            p.op(_e, lambda ee: ee.nop(), sig=_s)
                            snap.append((_e, _s, _s.n))
                        pairs = {"sv": ("scalar", "vector"),
                                 "st": ("scalar", "tensor"),
                                 "sg": ("scalar", "gpsimd"),
                                 "vs": ("vector", "scalar"),
                                 "vt": ("vector", "tensor"),
                                 "vg": ("vector", "gpsimd"),
                                 "ts": ("tensor", "scalar"),
                                 "tv": ("tensor", "vector"),
                                 "tg": ("tensor", "gpsimd"),
                                 "gs": ("gpsimd", "scalar"),
                                 "gv": ("gpsimd", "vector"),
                                 "gt": ("gpsimd", "tensor")}
                        for code in sel.split("-"):
                            we, oe = pairs[code]
                            for _e, _s, _v in snap:
                                if _e == oe:
                                    p.wait(we, _s, _v)
                    elif "NOBAR" not in passes:
                        bar_marks[g] = p.barrier(bar_engs)
                    stx_snaps[g] = p.snap(PSTX)
                    continue
                ve_sp = []
                sc_sq = []
                te_sum = []
                te_sq = []
                for c in range(NC):
                    cs = c % 2
                    if c >= 2:
                        p.wait("vector", TE, te_sum[c - 2])
                        p.wait("scalar", TE, te_sq[c - 2])
                    p.op("vector", lambda e, c=c, cs=cs, gbase=gbase:
                         e.tensor_copy(
                             out=xh_t[:, TG * cs:TG * (cs + 1)],
                             in_=xg[:, gbase + TG * c:gbase + TG * (c + 1)]))
                    ve_sp.append(p.op(
                        "vector", lambda e, c=c, cs=cs, gbase=gbase:
                        e.tensor_tensor(
                            out=xl_t[:, TG * cs:TG * (cs + 1)],
                            in0=xg[:, gbase + TG * c:gbase + TG * (c + 1)],
                            in1=xh_t[:, TG * cs:TG * (cs + 1)],
                            op=ALU.subtract), sig=VE))
                    sc_sq.append(p.op(
                        "scalar", lambda e, c=c, cs=cs, gbase=gbase:
                        e.activation(
                            out=xsq_t[:, TG * cs:TG * (cs + 1)],
                            in_=xg[:, gbase + TG * c:gbase + TG * (c + 1)],
                            func=AF.Square), sig=SC))
                    p.wait("tensor", VE, ve_sp[c])
                    p.op("tensor", lambda e, c=c, cs=cs: e.matmul(
                        ps_st[0:1, 0:TG], ones_col_b[:],
                        xh_t[:, TG * cs:TG * (cs + 1)],
                        start=(c == 0), stop=False))
                    te_sum.append(p.op("tensor", lambda e, c=c, cs=cs: e.matmul(
                        ps_st[0:1, 0:TG], ones_col_b[:],
                        xl_t[:, TG * cs:TG * (cs + 1)],
                        start=False, stop=(c == NC - 1)), sig=TE))
                    p.wait("tensor", SC, sc_sq[c])
                    te_sq.append(p.op("tensor", lambda e, c=c, cs=cs: e.matmul(
                        ps_st[0:1, TG:2 * TG], ones_col[:],
                        xsq_t[:, TG * cs:TG * (cs + 1)],
                        start=(c == 0), stop=(c == NC - 1)), sig=TE))
                te_stats = TE.n

                p.wait("vector", TE, te_stats)
                p.op("vector", lambda e: e.tensor_scalar(
                    out=row_a[:], in0=ps_st[0:1, 0:TG], scalar1=-1.0 / D,
                    scalar2=None, op0=ALU.mult, op1=ALU.bypass))
                p.op("vector", lambda e: e.tensor_scalar(
                    out=row_b[:], in0=ps_st[0:1, TG:2 * TG], scalar1=1.0 / D,
                    scalar2=None, op0=ALU.mult, op1=ALU.bypass))
                p.op("vector", lambda e: e.tensor_tensor(
                    out=row_c[:], in0=row_a[:], in1=row_a[:], op=ALU.mult))
                ve_sm1 = p.op("vector", lambda e: e.tensor_tensor(
                    out=row_b[:], in0=row_b[:], in1=row_c[:], op=ALU.subtract),
                    sig=VE)
                p.wait("scalar", VE, ve_sm1)
                sc_sqrt = p.op("scalar", lambda e: e.activation(
                    out=row_c[:], in_=row_b[:], func=AF.Sqrt,
                    bias=s_eps[0:1, 0:1]), sig=SC)
                p.wait("vector", SC, sc_sqrt)
                p.op("vector", lambda e: e.reciprocal(
                    out=row_d[:], in_=row_c[:]))
                p.fence("vector", VE)
                p.fence("vector", VE)
                ve_sm2 = VE.n

                p.wait("tensor", VE, ve_sm2)
                p.op("tensor", lambda e: e.matmul(
                    ps_st[0:128, 0:TG], ones_row[:], row_a[:],
                    start=True, stop=True))
                te_exp = p.op("tensor", lambda e: e.matmul(
                    ps_st[0:128, TG:2 * TG], ones_row[:], row_d[:],
                    start=True, stop=True), sig=TE)

                p.wait("vector", TE, te_exp)
                ve_xn = []
                stx_marks = []
                for c in range(NC):
                    cs = c % 2
                    if c >= 2:
                        p.wait_m("vector", stx_marks[c - 2])
                    p.op("vector", lambda e, c=c, cs=cs, gbase=gbase:
                         e.tensor_tensor(
                             out=xsq_t[:, TG * cs:TG * (cs + 1)],
                             in0=xg[:, gbase + TG * c:gbase + TG * (c + 1)],
                             in1=ps_st[0:128, 0:TG], op=ALU.add))
                    p.op("vector", lambda e, cs=cs: e.tensor_tensor(
                        out=xn_t[:, TG * cs:TG * (cs + 1)],
                        in0=xsq_t[:, TG * cs:TG * (cs + 1)],
                        in1=ps_st[0:128, TG:2 * TG], op=ALU.mult))
                    p.op("vector", lambda e, c=c, cs=cs: e.tensor_copy(
                        out=xnh_t[:, TG * c:TG * (c + 1)],
                        in_=xn_t[:, TG * cs:TG * (cs + 1)]))
                    ve_xn.append(p.op(
                        "vector", lambda e, c=c, cs=cs: e.tensor_tensor(
                            out=xnl_t[:, TG * c:TG * (c + 1)],
                            in0=xn_t[:, TG * cs:TG * (cs + 1)],
                            in1=xnh_t[:, TG * c:TG * (c + 1)],
                            op=ALU.subtract), sig=VE))
                    p.op("vector", lambda e, cs=cs: e.reduce_max(
                        out=mxcol2[:], in_=xn_t[:, TG * cs:TG * (cs + 1)],
                        axis=AX.X, apply_absolute_value=True))
                    p.fence("vector", VE)
                    p.op("vector", lambda e: e.tensor_tensor(
                        out=mxcol[:], in0=mxcol[:], in1=mxcol2[:],
                        op=ALU.max))
                    p.fence("vector", VE)
                    col = NC * TG * g + TG * c
                    p.wait("scalar", VE, ve_xn[c])
                    if not _no_store:
                        p.dma("scalar", PSTX, c, lambda e, cs=cs, col=col:
                              e.dma_start(out=xn_s[:, col:col + TG],
                                          in_=xn_t[:, TG * cs:TG * (cs + 1)]))
                        p.dma("scalar", PSTX, c, lambda e, c=c, col=col:
                              e.dma_start(out=xnh_s[:, col:col + TG],
                                          in_=xnh_t[:, TG * c:TG * (c + 1)]))
                        stx_marks.append(p.dma(
                            "scalar", PSTX, c, lambda e, c=c, col=col:
                            e.dma_start(out=xnl_s[:, col:col + TG],
                                        in_=xnl_t[:, TG * c:TG * (c + 1)])))
                    else:
                        stx_marks.append((PSTX[0], 0))
                stx_snaps[g] = p.snap(PSTX)

                p.wait("tensor", VE, ve_xn[NC - 1])
                te_k = []; te_v = []; te_kv = []
                ve_kev = []; sc_vev = []; gp_acc = []
                for t in range(NTILE if _do_kvproj else 0):
                    toff = 128 * t
                    if t >= 1:
                        p.wait("tensor", VE, ve_kev[t - 1])
                    for nh in range(2):
                        first = True
                        for cc in range(NC):
                            for (la, rb) in ((xnh_t, wk_h), (xnh_t, wk_l),
                                             (xnl_t, wk_h)):
                                p.op("tensor", lambda e, cc=cc, la=la, rb=rb,
                                     nh=nh, toff=toff, first=first: e.matmul(
                                         ps_k[:, 512 * nh:512 * (nh + 1)],
                                         la[:, TG * cc + toff:TG * cc + toff + 128],
                                         rb[:, D * cc + 512 * nh:D * cc + 512 * (nh + 1)],
                                         start=first, stop=False))
                                first = False
                        p.op("tensor", lambda e, nh=nh: e.matmul(
                            ps_k[:, 512 * nh:512 * (nh + 1)], ones_row_b[:],
                            s_bkh[0:1, 512 * nh:512 * (nh + 1)],
                            start=False, stop=False))
                        if nh == 1:
                            te_k.append(p.op("tensor", lambda e, nh=nh:
                                             e.matmul(
                                                 ps_k[:, 512 * nh:512 * (nh + 1)],
                                                 ones_row_b[:],
                                                 s_bkl[0:1, 512 * nh:512 * (nh + 1)],
                                                 start=False, stop=True),
                                             sig=TE))
                        else:
                            p.op("tensor", lambda e, nh=nh: e.matmul(
                                ps_k[:, 512 * nh:512 * (nh + 1)], ones_row_b[:],
                                s_bkl[0:1, 512 * nh:512 * (nh + 1)],
                                start=False, stop=True))
                    if t >= 1:
                        p.wait("tensor", SC, sc_vev[t - 1])
                    for nh in range(2):
                        first = True
                        for cc in range(NC):
                            for (la, rb) in ((xnh_t, wv_h), (xnh_t, wv_l),
                                             (xnl_t, wv_h)):
                                p.op("tensor", lambda e, cc=cc, la=la, rb=rb,
                                     nh=nh, toff=toff, first=first: e.matmul(
                                         ps_v[:, 512 * nh:512 * (nh + 1)],
                                         la[:, TG * cc + toff:TG * cc + toff + 128],
                                         rb[:, D * cc + 512 * nh:D * cc + 512 * (nh + 1)],
                                         start=first, stop=False))
                                first = False
                        p.op("tensor", lambda e, nh=nh: e.matmul(
                            ps_v[:, 512 * nh:512 * (nh + 1)], ones_row_b[:],
                            s_bvh[0:1, 512 * nh:512 * (nh + 1)],
                            start=False, stop=False))
                        if nh == 1:
                            te_v.append(p.op("tensor", lambda e, nh=nh:
                                             e.matmul(
                                                 ps_v[:, 512 * nh:512 * (nh + 1)],
                                                 ones_row_b[:],
                                                 s_bvl[0:1, 512 * nh:512 * (nh + 1)],
                                                 start=False, stop=True),
                                             sig=TE))
                        else:
                            p.op("tensor", lambda e, nh=nh: e.matmul(
                                ps_v[:, 512 * nh:512 * (nh + 1)], ones_row_b[:],
                                s_bvl[0:1, 512 * nh:512 * (nh + 1)],
                                start=False, stop=True))

                    p.wait("vector", TE, te_k[t])
                    if te_kv:
                        p.wait("vector", TE, te_kv[-1])
                    if gp_acc:
                        p.wait("vector", GP, gp_acc[-1])
                    ve_kev.append(p.op("vector", lambda e: e.tensor_relu(
                        out=k_tile[:], in_=ps_k[:]), sig=VE))
                    p.wait("scalar", TE, te_v[t])
                    if te_kv:
                        p.wait("scalar", TE, te_kv[-1])
                    sc_vev.append(p.op("scalar", lambda e: e.activation(
                        out=v_tile[:], in_=ps_v[:], func=AF.Identity), sig=SC))

                    if _do_kv:
                        p.wait("tensor", VE, ve_kev[t])
                        p.wait("tensor", SC, sc_vev[t])
                        for cc in range(NC):
                            _sig = TE if cc == NC - 1 else None
                            _mk = p.op("tensor", lambda e, cc=cc, g=g, t=t:
                                       e.matmul(
                                           ps_kv[:, 128 * cc:128 * (cc + 1)],
                                           k_tile[:, 128 * cc:128 * (cc + 1)],
                                           v_tile[:, 128 * cc:128 * (cc + 1)],
                                           start=(g == 0 and t == 0
                                                  and cc % 4 == 0),
                                           stop=(g == NG - 1
                                                 and t == NTILE - 1
                                                 and cc % 4 == 3)), sig=_sig)
                        te_kv.append(_mk)
                        p.wait("gpsimd", VE, ve_kev[t])
                        gp_acc.append(p.op("gpsimd", lambda e: e.tensor_tensor(
                            out=k_acc[:], in0=k_acc[:], in1=k_tile[:],
                            op=ALU.add), sig=GP))

                bar_marks[g] = p.barrier(bar_engs)

            p.wait("tensor", GP, GP.n)
            te_ks = TE.n
            for c in range(NC if _do_kv else 0):
                te_ks = p.op("tensor", lambda e, c=c: e.matmul(
                    ps_st[0:128, c:c + 1], k_acc[:, 128 * c:128 * (c + 1)],
                    ones_col[:], start=(c == 0), stop=(c == NC - 1)),
                    sig=TE if c == NC - 1 else None) or te_ks
            p.wait("vector", TE, te_ks)
            if _do_kv:
                p.op("vector", lambda e: e.tensor_copy(out=ksum_fm[:],
                                                       in_=ps_st[0:128, 0:NC]))
                for c in range(NC):
                    p.op("vector", lambda e, c=c: e.tensor_copy(
                        out=ksum_blk[0:64, 16 * c + 2 * c:16 * c + 2 * c + 1],
                        in_=ps_st[0:64, c:c + 1]))
                    p.op("vector", lambda e, c=c: e.tensor_copy(
                        out=ksum_blk[64:128, 16 * c + 2 * c + 1:16 * c + 2 * c + 2],
                        in_=ps_st[64:128, c:c + 1]))
                for c in range(NC):
                    p.op("vector", lambda e, c=c: e.tensor_copy(
                        out=kvblk[0:64, 128 * c:128 * c + 64],
                        in_=ps_kv[0:64, 128 * c:128 * c + 64]))
                    p.op("vector", lambda e, c=c: e.tensor_copy(
                        out=kvblk[64:128, 128 * c + 64:128 * (c + 1)],
                        in_=ps_kv[64:128, 128 * c + 64:128 * (c + 1)]))
            if _do_kv:
                p.fence("vector", VE)
                for c in range(NC):
                    p.op("vector", lambda e, c=c: e.tensor_scalar(
                        out=ksum_ex[:, 128 * c:128 * (c + 1)], in0=s_emask[:],
                        scalar1=ksum_fm[:, c:c + 1], scalar2=None,
                        op0=ALU.mult, op1=ALU.bypass))
            ve_fin = p.op("vector", lambda e: e.nop(), sig=VE)

            if not _do_groups:
                p.op("vector", lambda e: e.memset(smalls[0:1, 0:1], 1.0))
            te_gm = p.op("tensor", lambda e: e.matmul(
                ps_st[0:1, TG:TG + 128], mxcol[:], s_ident[:],
                start=True, stop=True), sig=TE)
            p.wait("vector", TE, te_gm)
            p.op("vector", lambda e: e.tensor_copy(
                out=row_a[0:1, 0:128], in_=ps_st[0:1, TG:TG + 128]))
            p.fence("vector", VE)
            if "DBG" in passes:
                p.dma("scalar", PSTX, 3, lambda e: e.dma_start(
                    out=dbg_s[0:1, 4200:4328], in_=row_a[0:1, 0:128]))
                p.dma("scalar", PSTX, 3, lambda e: e.dma_start(
                    out=dbg_s[:, 4330:4331], in_=mxcol[:],
                    allow_slow_non_contiguous=True))
            p.op("vector", lambda e: e.reduce_max(
                out=smalls[0:1, 0:1], in_=row_a[0:1, 0:128], axis=AX.X))
            ve_gm = p.fence("vector", VE) or VE.n
            if "DBG" in passes:
                p.wait("scalar", VE, VE.n)
                p.dma("scalar", PSTX, 3, lambda e: e.dma_start(
                    out=dbg_s[0:1, 4340:4341], in_=smalls[0:1, 0:1]))
            p.wait("gpsimd", VE, VE.n)
            mcc = p.dma("gpsimd", SCC, 0, lambda e: e.dma_start(
                out=cc_in1, in_=smalls[0:1, 0:1]))
            p.wait_m("gpsimd", mcc)
            p.op("gpsimd", lambda e: e.collective_compute(
                "AllReduce", ALU.max, replica_groups=[list(range(NCORES))],
                ins=[cc_in1.opt()], outs=[cc_out1.opt()]), sig=CC)

            if "DBG" in passes:
                p.wait("scalar", VE, VE.n)
                p.dma("scalar", PSTX, 0, lambda e: e.dma_start(
                    out=dbg_s[:, 0:NC * 128], in_=kvblk[:]))
                p.dma("scalar", PSTX, 0, lambda e: e.dma_start(
                    out=dbg_s[:, 1024:1024 + NC], in_=ksum_fm[:]))
                p.dma("scalar", PSTX, 0, lambda e: e.dma_start(
                    out=dbg_s[:, 1040:1041], in_=mxcol[:],
                    allow_slow_non_contiguous=True))
                p.dma("scalar", PSTX, 0, lambda e: e.dma_start(
                    out=dbg_s[0:1, 1050:1066], in_=smalls[:]))
                p.dma("scalar", PSTX, 0, lambda e: e.dma_start(
                    out=dbg_s[:, 1100:1100 + NC * 16], in_=ksum_blk[:]))
            p.drain("sync", PWLD)
            p.drain("sync", PLDX)
            p.drain("scalar", PWR)
            p.drain("scalar", PSTX)
            p.wait("vector", VE, ve_fin)
            _bs = p.barrier(bar_engs)
            p.wait_snap("sync", _bs)

        if "B1" not in passes:
            _finalize(p, bar_engs)
            return nc
        with ExitStack() as bctx:
            def sbb(name, shape, dtype=F32):
                return bctx.enter_context(nc.sbuf_tensor(name, shape, dtype))

            wq_h = sbb("wq_h", [128, NC * D], BF16)
            wq_l = sbb("wq_l", [128, NC * D], BF16)
            wld_b = sbb("wld_b", [128, 2 * D])
            xng = sbb("xng", [128, 2 * NC * TG])
            xnh_b = sbb("xnh_b", [128, NC * TG], BF16)
            xnl_b = sbb("xnl_b", [128, NC * TG], BF16)
            qT_t = sbb("qT_t", [128, NC * TG])
            y_t = sbb("y_t", [128, NC * TG])

            rbc_t = sbb("rbc_t", [128, 2 * TG])
            t2_t = sbb("t2_t", [128, 2 * TG])
            s_t = sbb("s_t", [128, 2 * TG])
            ysq_t = sbb("ysq_t", [128, 2 * TG])
            wraw_d = sbb("wraw_d", [128, 2 * CI])
            dbg2_s = (sbb("dbg2_s", [128, 2 * TG]) if "DBG" in passes else None)

            ps_q = bctx.enter_context(nc.psum_tensor("ps_q", [128, 2 * TG], F32))
            ps_dbc = bctx.enter_context(nc.psum_tensor("ps_dbc", [128, TG], F32))
            ps_ctx = bctx.enter_context(nc.psum_tensor("ps_ctx", [128, TG], F32))
            ps_stb = bctx.enter_context(nc.psum_tensor("ps_stb", [128, 2 * TG], F32))

            wq_marks = []
            for c in range(NC):
                slot = c % 2
                if c >= 2:
                    p.wait("sync", VE, wq_marks[c - 2])
                m = p.dma("sync", PWLD, c, lambda e, c=c, slot=slot:
                          e.dma_start(out=wld_b[:, slot * D:(slot + 1) * D],
                                      in_=wqT_d[128 * c:128 * (c + 1), :]))
                p.wait_m("vector", m)
                p.op("vector", lambda e, c=c, slot=slot: e.tensor_copy(
                    out=wq_h[:, D * c:D * (c + 1)],
                    in_=wld_b[:, slot * D:(slot + 1) * D]))
                wq_marks.append(p.op(
                    "vector", lambda e, c=c, slot=slot: e.tensor_tensor(
                        out=wq_l[:, D * c:D * (c + 1)],
                        in0=wld_b[:, slot * D:(slot + 1) * D],
                        in1=wq_h[:, D * c:D * (c + 1)], op=ALU.subtract),
                    sig=VE))
            ve_wq = VE.n

            wd_marks = []

            def emit_wd_rowmax(j):
                slot = j % 2
                if j >= 2:
                    p.wait("scalar", VE, wd_marks[j - 2])
                m = p.dma("scalar", PWR, j, lambda e, j=j, slot=slot:
                          e.dma_start(
                              out=wraw_d[:, slot * CI:(slot + 1) * CI],
                              in_=wd_raw[128 * j:128 * (j + 1), :]))
                p.wait_m("vector", m)
                wd_marks.append(p.op(
                    "vector", lambda e, j=j, slot=slot: e.reduce_max(
                        out=swd[:, j:j + 1],
                        in_=wraw_d[:, slot * CI:(slot + 1) * CI],
                        axis=AX.X, apply_absolute_value=True), sig=VE))

            bar_marks_b = [0] * (NG + 2)
            stx_snaps_b = [None] * (NG + 2)
            ldb_marks = []
            for g in range(NG):
                slot = g % 2
                gbase = slot * NC * TG
                if g >= 2:
                    p.wait_snap("sync", bar_marks_b[g - 2])
                p.dma("sync", PLDX, g, lambda e, g=g, gbase=gbase: e.dma_start(
                    out=xng[:, gbase:gbase + NC * TG],
                    in_=xn_s[:, NC * TG * g:NC * TG * (g + 1)]))
                if g >= 1:
                    p.wait_snap("sync", bar_marks_b[g - 1])
                p.dma("sync", PLDX, g, lambda e, g=g: e.dma_start(
                    out=xnh_b[:], in_=xnh_s[:, NC * TG * g:NC * TG * (g + 1)]))
                m = p.dma("sync", PLDX, g, lambda e, g=g: e.dma_start(
                    out=xnl_b[:], in_=xnl_s[:, NC * TG * g:NC * TG * (g + 1)]))
                ldb_marks.append(m)
                if g >= 1:
                    p.wait_snap("scalar", stx_snaps_b[g - 1])

                if g < NC:
                    emit_wd_rowmax(g)

                p.wait_m("tensor", ldb_marks[g])
                if g == 0:
                    p.wait("tensor", VE, ve_wq)
                sc_qe = []
                te_qm = TE.n
                for mch in range(NC):
                    qb = (mch % 2) * TG
                    if mch >= 2:
                        p.wait("tensor", SC, sc_qe[mch - 2])
                    first = True
                    for cc in range(NC):
                        for pi, (la, rb) in enumerate(
                                ((xnh_b, wq_h), (xnh_b, wq_l),
                                 (xnl_b, wq_h))):
                            last = (cc == NC - 1 and pi == 2)
                            te_qm = p.op("tensor", lambda e, cc=cc, la=la,
                                          rb=rb, mch=mch, qb=qb, first=first,
                                          last=last: e.matmul(
                                              ps_q[:, qb:qb + TG],
                                              rb[:, D * cc + 128 * mch:D * cc + 128 * (mch + 1)],
                                              la[:, TG * cc:TG * (cc + 1)],
                                              start=first, stop=last),
                                          sig=TE if last else None) or te_qm
                            first = False
                    p.wait("scalar", TE, te_qm)
                    sc_qe.append(p.op("scalar", lambda e, mch=mch, qb=qb:
                                      e.activation(
                                          out=qT_t[:, TG * mch:TG * (mch + 1)],
                                          in_=ps_q[:, qb:qb + TG],
                                          func=AF.Relu,
                                          bias=s_bq[:, mch:mch + 1]), sig=SC))

                p.wait_m("vector", ldb_marks[g])
                ve_y = []
                sc_ysq = []
                te_ysq = []
                for cc in range(NC):
                    cs = cc % 2
                    p.wait("tensor", SC, sc_qe[cc])
                    if cc >= 1:
                        p.wait("tensor", VE, ve_y[cc - 1])
                    te_dbc = p.op("tensor", lambda e, cc=cc: e.matmul(
                        ps_dbc[:], ksum_ex[:, 128 * cc:128 * (cc + 1)],
                        qT_t[:, TG * cc:TG * (cc + 1)], start=True, stop=True),
                        sig=TE)
                    te_ctx = p.op("tensor", lambda e, cc=cc: e.matmul(
                        ps_ctx[:], kvblk[:, 128 * cc:128 * (cc + 1)],
                        qT_t[:, TG * cc:TG * (cc + 1)], start=True, stop=True),
                        sig=TE)
                    p.wait("vector", TE, te_dbc)
                    p.op("vector", lambda e, cs=cs: e.reciprocal(
                        out=rbc_t[:, TG * cs:TG * (cs + 1)], in_=ps_dbc[:]))
                    p.wait("vector", TE, te_ctx)
                    if cc >= 2:
                        p.wait("vector", TE, te_ysq[cc - 2])
                    if "DBG" in passes and g in (0, 1) and cc in (0, 3):
                        _off = 4928 + (g * 2 + (0 if cc == 0 else 1)) * TG
                        p.op("vector", lambda e: e.tensor_copy(
                            out=dbg2_s[:, 0:TG], in_=ps_ctx[:]))
                        p.op("vector", lambda e, cs=cs: e.tensor_copy(
                            out=dbg2_s[:, TG:2 * TG],
                            in_=rbc_t[:, TG * cs:TG * (cs + 1)]))
                        p.fence("vector", VE)
                        p.wait("scalar", VE, VE.n)
                        _m1 = p.dma("scalar", PSTX, 3, lambda e, _off=_off:
                                    e.dma_start(out=dbg_s[:, _off:_off + TG],
                                                in_=dbg2_s[:, 0:TG]))
                        _m2 = p.dma("scalar", PSTX, 3, lambda e, _off=_off:
                                    e.dma_start(out=dbg_s2[:, _off:_off + TG],
                                                in_=dbg2_s[:, TG:2 * TG]))
                        p.wait_m("vector", _m1)
                        p.wait_m("vector", _m2)
                    p.op("vector", lambda e, cs=cs: e.scalar_tensor_tensor(
                        out=ysq_t[:, TG * cs:TG * (cs + 1)],
                        in0=rbc_t[:, TG * cs:TG * (cs + 1)],
                        scalar=1.0 / EPS_DEN, in1=ps_ctx[:],
                        op0=ALU.min, op1=ALU.mult))
                    ve_y.append(p.op(
                        "vector", lambda e, cc=cc, cs=cs, gbase=gbase:
                        e.tensor_tensor(
                            out=y_t[:, TG * cc:TG * (cc + 1)],
                            in0=ysq_t[:, TG * cs:TG * (cs + 1)],
                            in1=xng[:, gbase + TG * cc:gbase + TG * (cc + 1)],
                            op=ALU.add), sig=VE))
                    p.wait("tensor", VE, ve_y[cc])
                    p.op("tensor", lambda e, cc=cc: e.matmul(
                        ps_stb[0:1, 0:TG], ones_col[:],
                        y_t[:, TG * cc:TG * (cc + 1)],
                        start=(cc == 0), stop=(cc == NC - 1)))
                    p.wait("scalar", VE, ve_y[cc])
                    sc_ysq.append(p.op(
                        "scalar", lambda e, cc=cc, cs=cs: e.activation(
                            out=ysq_t[:, TG * cs:TG * (cs + 1)],
                            in_=y_t[:, TG * cc:TG * (cc + 1)],
                            func=AF.Square), sig=SC))
                    p.wait("tensor", SC, sc_ysq[cc])
                    te_ysq.append(p.op(
                        "tensor", lambda e, cc=cc, cs=cs: e.matmul(
                            ps_stb[0:1, TG:2 * TG], ones_col[:],
                            ysq_t[:, TG * cs:TG * (cs + 1)],
                            start=(cc == 0), stop=(cc == NC - 1)), sig=TE))
                te_yst = TE.n
                if "DBG" in passes and g == 1:
                    p.wait("scalar", VE, VE.n)
                    p.dma("scalar", PSTX, 3, lambda e: e.dma_start(
                        out=dbg_s2[:, 4096:4096 + NC * TG // 8],
                        in_=y_t[:, 0:NC * TG // 8]))
                    p.dma("scalar", PSTX, 3, lambda e: e.dma_start(
                        out=dbg_s2[:, 4608:4608 + NC * TG // 8],
                        in_=y_t[:, 7 * TG:7 * TG + NC * TG // 8]))

                p.wait("vector", TE, te_yst)
                p.op("vector", lambda e: e.tensor_scalar(
                    out=row_a[:], in0=ps_stb[0:1, 0:TG], scalar1=-1.0 / D,
                    scalar2=None, op0=ALU.mult, op1=ALU.bypass))
                p.op("vector", lambda e: e.tensor_scalar(
                    out=row_b[:], in0=ps_stb[0:1, TG:2 * TG], scalar1=1.0 / D,
                    scalar2=None, op0=ALU.mult, op1=ALU.bypass))
                p.op("vector", lambda e: e.tensor_tensor(
                    out=row_c[:], in0=row_a[:], in1=row_a[:], op=ALU.mult))
                ve_ysm = p.op("vector", lambda e: e.tensor_tensor(
                    out=row_b[:], in0=row_b[:], in1=row_c[:], op=ALU.subtract),
                    sig=VE)
                p.wait("scalar", VE, ve_ysm)
                sc_ysqrt = p.op("scalar", lambda e: e.activation(
                    out=row_c[:], in_=row_b[:], func=AF.Sqrt,
                    bias=s_eps[0:1, 0:1]), sig=SC)
                p.wait("vector", SC, sc_ysqrt)
                p.op("vector", lambda e: e.reciprocal(
                    out=row_d[:], in_=row_c[:]))
                p.fence("vector", VE)
                p.fence("vector", VE)
                ve_ysm2 = VE.n
                p.wait("tensor", VE, ve_ysm2)
                p.op("tensor", lambda e: e.matmul(
                    ps_stb[0:128, 0:TG], ones_row[:], row_a[:],
                    start=True, stop=True))
                te_yexp = p.op("tensor", lambda e: e.matmul(
                    ps_stb[0:128, TG:2 * TG], ones_row[:], row_d[:],
                    start=True, stop=True), sig=TE)

                p.wait("vector", TE, te_yexp)
                sc_s = []
                sts_marks = []
                for cc in range(NC):
                    cs = cc % 2
                    if cc >= 2:
                        p.wait("vector", SC, sc_s[cc - 2])
                    p.op("vector", lambda e, cc=cc, cs=cs: e.tensor_tensor(
                        out=ysq_t[:, TG * cs:TG * (cs + 1)],
                        in0=y_t[:, TG * cc:TG * (cc + 1)],
                        in1=ps_stb[0:128, 0:TG], op=ALU.add))
                    ve_t2 = p.op("vector", lambda e, cs=cs: e.tensor_tensor(
                        out=t2_t[:, TG * cs:TG * (cs + 1)],
                        in0=ysq_t[:, TG * cs:TG * (cs + 1)],
                        in1=ps_stb[0:128, TG:2 * TG], op=ALU.mult), sig=VE)
                    p.wait("scalar", VE, ve_t2)
                    if cc >= 2:
                        p.wait_m("scalar", sts_marks[cc - 2])
                    sc_s.append(p.op("scalar", lambda e, cc=cc, cs=cs:
                                     e.activation(
                                         out=s_t[:, TG * cs:TG * (cs + 1)],
                                         in_=t2_t[:, TG * cs:TG * (cs + 1)],
                                         func=AF.Identity,
                                         scale=s_gamma[:, cc:cc + 1],
                                         bias=s_beta[:, cc:cc + 1]), sig=SC))
                    col = NC * TG * g + TG * cc
                    sts_marks.append(p.dma(
                        "scalar", PSTX, cc, lambda e, cs=cs, col=col:
                        e.dma_start(out=s_s[:, col:col + TG],
                                    in_=s_t[:, TG * cs:TG * (cs + 1)])))
                stx_snaps_b[g] = p.snap(PSTX)

                bar_marks_b[g] = p.barrier(bar_engs)

            if "DBG" in passes:
                p.wait("scalar", VE, VE.n)
                p.wait("scalar", SC, SC.n)
                p.dma("scalar", PSTX, 1, lambda e: e.dma_start(
                    out=dbg_s[:, 1300:1300 + TG], in_=qT_t[:, 0:TG]))
                p.dma("scalar", PSTX, 1, lambda e: e.dma_start(
                    out=dbg_s[:, 3000:3000 + NC * TG // 8],
                    in_=y_t[:, 0:NC * TG // 8]))
            p.drain("sync", PWLD)
            p.drain("sync", PLDX)
            p.drain("scalar", PWR)
            p.drain("scalar", PSTX)
            _bs = p.barrier(bar_engs)
            p.wait_snap("sync", _bs)

        if "B2" not in passes:
            _finalize(p, bar_engs)
            return nc
        with ExitStack() as cctx:
            def sbc(name, shape, dtype=F32):
                return cctx.enter_context(nc.sbuf_tensor(name, shape, dtype))

            lvl_g = sbc("lvl_g", [128, NC * NH * 128], BF16)
            lvl_u = sbc("lvl_u", [128, NC * NH * 128], BF16)
            rbc_w = sbc("rbc_w", [128, NH * 128])
            rr_t = sbc("rr_t", [1, NH * 128])
            wqt_t = sbc("wqt_t", [128, NH * 128])
            wqi_t = sbc("wqi_t", [128, NH * 128], I32)
            wlv_t = sbc("wlv_t", [128, NH * 128])
            xng_c = sbc("xng_c", [128, 2 * NC * TG])
            xq_t = sbc("xq_t", [128, NC * TG], BF16)
            xqi_t = sbc("xqi_t", [128, 2 * TG], I32)
            sg_t = sbc("sg_t", [128, 2 * TG])
            up_t = sbc("up_t", [128, 2 * TG])
            hp_t = sbc("hp_t", [128, 4 * TG])

            ps_g = cctx.enter_context(nc.psum_tensor("ps_g", [128, 2 * TG], F32))
            ps_u = cctx.enter_context(nc.psum_tensor("ps_u", [128, 2 * TG], F32))
            ps_b = cctx.enter_context(nc.psum_tensor("ps_b", [128, 128], F32))
            ps_tr = cctx.enter_context(nc.psum_tensor("ps_tr", [1, 128], F32))

            p.op("vector", lambda e: e.memset(mxcol[:], 0.0))

            p.wait("gpsimd", CC, 1)
            mgf = p.dma("gpsimd", SCC, 1, lambda e: e.dma_start(
                out=gmax1[:], in_=cc_out1))
            p.wait_m("vector", mgf)
            p.op("vector", lambda e: e.tensor_tensor(
                out=smalls[0:1, 0:1], in0=s_sig[0:1, 0:1], in1=gmax1[:],
                op=ALU.mult))
            p.fence("vector", VE)
            p.op("vector", lambda e: e.tensor_scalar(
                out=smalls[0:1, 1:2], in0=smalls[0:1, 0:1], scalar1=1.0 / 127.0,
                scalar2=None, op0=ALU.mult, op1=ALU.bypass))
            p.fence("vector", VE)
            p.op("vector", lambda e: e.reciprocal(
                out=smalls[0:1, 2:3], in_=smalls[0:1, 1:2]))
            p.fence("vector", VE)
            p.fence("vector", VE)
            ve_sx = VE.n
            p.wait("tensor", VE, ve_sx)
            p.op("tensor", lambda e: e.matmul(
                ps_b[:, 0:1], ones_row[:], smalls[0:1, 2:3],
                start=True, stop=True))
            te_bc = p.op("tensor", lambda e: e.matmul(
                ps_b[:, 1:2], ones_row[:], smalls[0:1, 1:2],
                start=True, stop=True), sig=TE)
            p.wait("vector", TE, te_bc)
            p.op("vector", lambda e: e.tensor_copy(out=rsx_bc[:],
                                                   in_=ps_b[:, 0:1]))
            p.op("vector", lambda e: e.tensor_copy(out=sx_bc[:],
                                                   in_=ps_b[:, 1:2]))
            p.fence("vector", VE)
            p.op("vector", lambda e: e.tensor_scalar(
                out=dq_g[:], in0=swg[:], scalar1=sx_bc[:, 0:1],
                scalar2=1.0 / 127.0, op0=ALU.mult, op1=ALU.mult))
            ve_dq = p.op("vector", lambda e: e.tensor_scalar(
                out=dq_u[:], in0=swu[:], scalar1=sx_bc[:, 0:1],
                scalar2=1.0 / 127.0, op0=ALU.mult, op1=ALU.mult), sig=VE)
            p.fence("vector", VE)
            if "DBG" in passes:
                p.wait("scalar", VE, VE.n)
                p.dma("scalar", PSTX, 2, lambda e: e.dma_start(
                    out=dbg_s[0:1, 4000:4001], in_=gmax1[:]))
                p.dma("scalar", PSTX, 2, lambda e: e.dma_start(
                    out=dbg_s[:, 4010:4011], in_=rsx_bc[:],
                    allow_slow_non_contiguous=True))
                p.dma("scalar", PSTX, 2, lambda e: e.dma_start(
                    out=dbg_s[:, 4020:4021], in_=sx_bc[:],
                    allow_slow_non_contiguous=True))
                p.dma("scalar", PSTX, 2, lambda e: e.dma_start(
                    out=dbg_s[:, 4030:4030 + NIC], in_=dq_g[:]))
                p.dma("scalar", PSTX, 2, lambda e: e.dma_start(
                    out=dbg_s[:, 4060:4060 + NIC], in_=swg[:]))
                p.dma("scalar", PSTX, 2, lambda e: e.dma_start(
                    out=dbg_s[:, 4090:4090 + NIC], in_=swu[:]))
                p.dma("scalar", PSTX, 2, lambda e: e.dma_start(
                    out=dbg_s[0:1, 4120:4122], in_=s_sig[:]))

            bar_marks_c = [0] * (2 * NG + 4)
            xq_snaps = [None] * (NG + 2)
            hp_snaps = [None] * (2 * NG + 4)
            gidx = 0
            for half in range(2):
                ic0 = half * NH
                for wi, (sw_, lvl, wsrc) in enumerate(
                        ((swg, lvl_g, wgT_d), (swu, lvl_u, wuT_d))):
                    p.op("vector", lambda e, sw_=sw_: e.tensor_scalar(
                        out=wqt_t[:, 0:NIC], in0=sw_[:], scalar1=1.0 / 127.0,
                        scalar2=None, op0=ALU.mult, op1=ALU.bypass))
                    p.fence("vector", VE)
                    p.op("vector", lambda e: e.reciprocal(
                        out=wqt_t[:, NIC:2 * NIC], in_=wqt_t[:, 0:NIC]))
                    p.fence("vector", VE)
                    p.fence("vector", VE)
                    ve_r = VE.n
                    p.wait("tensor", VE, ve_r)
                    for jj in range(NH):
                        j = ic0 + jj
                        te_t = p.op("tensor", lambda e, j=j: e.matmul(
                            ps_tr[:], wqt_t[:, NIC + j:NIC + j + 1],
                            s_ident[:], start=True, stop=True), sig=TE)
                        p.wait("vector", TE, te_t)
                        p.op("vector", lambda e, jj=jj: e.tensor_copy(
                            out=rr_t[0:1, 128 * jj:128 * (jj + 1)],
                            in_=ps_tr[:]), sig=VE)
                        if jj < NH - 1:
                            p.wait("tensor", VE, VE.n)
                    p.fence("vector", VE)
                    p.fence("vector", VE)
                    ve_rr = VE.n
                    p.wait("scalar", VE, VE.n)
                    sc_e = []
                    for jj in range(NH):
                        p.wait("tensor", VE, ve_rr)
                        if jj >= 1:
                            p.wait("tensor", SC, sc_e[jj - 1])
                        te_e = p.op("tensor", lambda e, jj=jj: e.matmul(
                            ps_b[:], ones_row[:],
                            rr_t[0:1, 128 * jj:128 * (jj + 1)],
                            start=True, stop=True), sig=TE)
                        p.wait("scalar", TE, te_e)
                        sc_e.append(p.op(
                            "scalar", lambda e, jj=jj: e.activation(
                                out=rbc_w[:, 128 * jj:128 * (jj + 1)],
                                in_=ps_b[:], func=AF.Identity), sig=SC))
                    sc_exp = SC.n
                    for ck in range(NC):
                        m = p.dma("sync", SWQ, 0, lambda e, ck=ck, wsrc=wsrc,
                                  ic0=ic0: e.dma_start(
                                      out=wlv_t[:],
                                      in_=wsrc[128 * ck:128 * (ck + 1),
                                               128 * ic0:128 * (ic0 + NH)]))
                        p.wait_m("vector", m)
                        p.wait("vector", SC, sc_exp)
                        p.op("vector", lambda e: e.tensor_tensor(
                            out=wqt_t[:], in0=wlv_t[:], in1=rbc_w[:],
                            op=ALU.mult))
                        p.op("vector", lambda e: e.tensor_copy(
                            out=wqi_t[:], in_=wqt_t[:]))
                        ve_q = p.op("vector", lambda e, lvl=lvl, ck=ck:
                                    e.tensor_scalar(
                                        out=lvl[:, NH * 128 * ck:NH * 128 * (ck + 1)],
                                        in0=wqi_t[:], scalar1=-128,
                                        scalar2=127,
                                        op0=ALU.max, op1=ALU.min), sig=VE)
                        p.wait("sync", VE, ve_q)
                ve_wq_lvls = VE.n

                for g in range(NG):
                    slot = g % 2
                    gbase = slot * NC * TG
                    if gidx >= 2:
                        p.wait_snap("sync", bar_marks_c[gidx - 2])
                    if half == 0:
                        mld = p.dma("sync", PLDX, gidx, lambda e, g=g,
                                    gbase=gbase: e.dma_start(
                                        out=xng_c[:, gbase:gbase + NC * TG],
                                        in_=xn_s[:, NC * TG * g:NC * TG * (g + 1)]))
                    else:
                        if gidx >= 1:
                            p.wait_snap("sync", bar_marks_c[gidx - 1])
                        if xq_snaps[g] is not None:
                            p.wait_snap("sync", xq_snaps[g])
                        mld = p.dma("sync", PLDX, gidx, lambda e, g=g:
                                    e.dma_start(
                                        out=xq_t[:],
                                        in_=xq_s[:, NC * TG * g:NC * TG * (g + 1)]))
                    if gidx >= 1:
                        p.wait_snap("vector", hp_snaps[gidx - 1])

                    if half == 0:
                        p.wait_m("vector", mld)
                        if gidx == 0:
                            p.wait("vector", VE, ve_dq)
                        if g >= 1 and xq_snaps[g - 1] is not None:
                            p.wait_snap("vector", xq_snaps[g - 1])
                        ve_xq = []
                        for cc in range(NC):
                            cs = cc % 2
                            p.op("vector", lambda e, cc=cc, cs=cs,
                                 gbase=gbase: e.tensor_scalar(
                                     out=xqi_t[:, TG * cs:TG * (cs + 1)],
                                     in0=xng_c[:, gbase + TG * cc:gbase + TG * (cc + 1)],
                                     scalar1=rsx_bc[:, 0:1], scalar2=None,
                                     op0=ALU.mult, op1=ALU.bypass))
                            ve_xq.append(p.op(
                                "vector", lambda e, cc=cc, cs=cs:
                                e.tensor_scalar(
                                    out=xq_t[:, TG * cc:TG * (cc + 1)],
                                    in0=xqi_t[:, TG * cs:TG * (cs + 1)],
                                    scalar1=-128, scalar2=127,
                                    op0=ALU.max, op1=ALU.min), sig=VE))
                            col = NC * TG * g + TG * cc
                            p.wait("scalar", VE, ve_xq[cc])
                            p.dma("scalar", PXQ, g, lambda e, cc=cc, col=col:
                                  e.dma_start(
                                      out=xq_s[:, col:col + TG],
                                      in_=xq_t[:, TG * cc:TG * (cc + 1)]))
                        xq_snaps[g] = p.snap(PXQ)
                        ve_xq_all = ve_xq[-1]
                        p.wait("tensor", VE, ve_xq_all)
                    else:
                        p.wait_m("tensor", mld)
                    if gidx == 0 or (half == 1 and g == 0):
                        p.wait("tensor", VE, ve_wq_lvls)

                    sc_ge = []; sc_ue = []; ve_hp = []
                    hp_marks = []
                    for jj in range(NH):
                        ic = ic0 + jj
                        gb = (jj % 2) * TG
                        if jj >= 2:
                            p.wait("tensor", SC, sc_ge[jj - 2])
                            p.wait("tensor", SC, sc_ue[jj - 2])
                        first = True
                        te_g = TE.n
                        for cc in range(NC):
                            te_g = p.op("tensor", lambda e, cc=cc, jj=jj,
                                        gb=gb, first=first, lvl=lvl_g:
                                        e.matmul(
                                            ps_g[:, gb:gb + TG],
                                            lvl[:, NH * 128 * cc + 128 * jj:
                                                NH * 128 * cc + 128 * (jj + 1)],
                                            xq_t[:, TG * cc:TG * (cc + 1)],
                                            start=first, stop=(cc == NC - 1)),
                                        sig=TE if cc == NC - 1 else None) or te_g
                            first = False
                        first = True
                        te_u = TE.n
                        for cc in range(NC):
                            te_u = p.op("tensor", lambda e, cc=cc, jj=jj,
                                        gb=gb, first=first, lvl=lvl_u:
                                        e.matmul(
                                            ps_u[:, gb:gb + TG],
                                            lvl[:, NH * 128 * cc + 128 * jj:
                                                NH * 128 * cc + 128 * (jj + 1)],
                                            xq_t[:, TG * cc:TG * (cc + 1)],
                                            start=first, stop=(cc == NC - 1)),
                                        sig=TE if cc == NC - 1 else None) or te_u
                            first = False
                        p.wait("scalar", TE, te_g)
                        if jj >= 2:
                            p.wait("scalar", VE, ve_hp[jj - 2])
                        sc_ge.append(p.op(
                            "scalar", lambda e, ic=ic, gb=gb: e.activation(
                                out=sg_t[:, gb:gb + TG],
                                in_=ps_g[:, gb:gb + TG],
                                func=AF.Silu, scale=dq_g[:, ic:ic + 1]),
                            sig=SC))
                        p.wait("scalar", TE, te_u)
                        sc_ue.append(p.op(
                            "scalar", lambda e, ic=ic, gb=gb: e.activation(
                                out=up_t[:, gb:gb + TG],
                                in_=ps_u[:, gb:gb + TG],
                                func=AF.Identity, scale=dq_u[:, ic:ic + 1]),
                            sig=SC))
                        hb = (jj % 4) * TG
                        p.wait("vector", SC, sc_ge[jj])
                        p.wait("vector", SC, sc_ue[jj])
                        if jj >= 4:
                            p.wait_m("vector", hp_marks[jj - 4])
                        ve_hp.append(p.op(
                            "vector", lambda e, gb=gb, hb=hb: e.tensor_tensor(
                                out=hp_t[:, hb:hb + TG],
                                in0=sg_t[:, gb:gb + TG],
                                in1=up_t[:, gb:gb + TG], op=ALU.mult),
                            sig=VE))
                        p.op("vector", lambda e, hb=hb: e.reduce_max(
                            out=mxcol2[:], in_=hp_t[:, hb:hb + TG],
                            axis=AX.X, apply_absolute_value=True))
                        p.fence("vector", VE)
                        p.op("vector", lambda e: e.tensor_tensor(
                            out=mxcol[:], in0=mxcol[:], in1=mxcol2[:],
                            op=ALU.max))
                        p.fence("vector", VE)
                        col = NIC * TG * g + TG * ic
                        p.wait("scalar", VE, ve_hp[jj])
                        hp_marks.append(p.dma(
                            "scalar", PHP, jj, lambda e, hb=hb, col=col:
                            e.dma_start(out=hp_s[:, col:col + TG],
                                        in_=hp_t[:, hb:hb + TG])))
                    hp_snaps[gidx] = p.snap(PHP)

                    bar_marks_c[gidx] = p.barrier(bar_engs)
                    gidx += 1

            te_hm = p.op("tensor", lambda e: e.matmul(
                ps_b[0:1, 0:128], mxcol[:], s_ident[:], start=True, stop=True),
                sig=TE)
            p.wait("vector", TE, te_hm)
            p.op("vector", lambda e: e.tensor_copy(
                out=row_a[0:1, 0:128], in_=ps_b[0:1, 0:128]))
            p.fence("vector", VE)
            p.op("vector", lambda e: e.reduce_max(
                out=smalls[0:1, 3:4], in_=row_a[0:1, 0:128], axis=AX.X))
            p.fence("vector", VE)
            p.wait("gpsimd", VE, VE.n)
            mcc2 = p.dma("gpsimd", SCC, 2, lambda e: e.dma_start(
                out=cc_in2, in_=smalls[0:1, 3:4]))
            p.wait_m("gpsimd", mcc2)
            p.op("gpsimd", lambda e: e.collective_compute(
                "AllReduce", ALU.max, replica_groups=[list(range(NCORES))],
                ins=[cc_in2.opt()], outs=[cc_out2.opt()]), sig=CC)

            p.drain("sync", PLDX)
            p.drain("sync", SWQ)
            p.drain("scalar", PXQ)
            p.drain("scalar", PHP)
            p.drain("scalar", PWR)
            _bs = p.barrier(bar_engs)
            p.wait_snap("sync", _bs)

        if "C" not in passes:
            _finalize(p, bar_engs)
            return nc
        with ExitStack() as dctx:
            def sbd(name, shape, dtype=F32):
                return dctx.enter_context(nc.sbuf_tensor(name, shape, dtype))

            lvl_d = sbd("lvl_d", [128, NIC * D], BF16)
            rbc_d = sbd("rbc_d", [128, D])
            rr_d = sbd("rr_d", [1, NC * 128])
            wqt_d = sbd("wqt_d", [128, D])
            wqi_d = sbd("wqi_d", [128, D], I32)
            wlv_d = sbd("wlv_d", [128, D])
            sgrp = sbd("sgrp", [128, NC * TG])
            xgrp = sbd("xgrp", [128, NC * TG])
            hg_t = sbd("hg_t", [128, 2 * TG])
            hi_t = sbd("hi_t", [128, 2 * TG], I32)
            hl_t = sbd("hl_t", [128, NIC * TG], BF16)
            g_t = sbd("g_t", [128, 2 * TG])
            o_t = sbd("o_t", [128, 4 * TG])

            ps_d = dctx.enter_context(nc.psum_tensor("ps_d", [128, 2 * TG], F32))
            ps_bd = dctx.enter_context(nc.psum_tensor("ps_bd", [128, 128], F32))
            ps_trd = dctx.enter_context(nc.psum_tensor("ps_trd", [1, 128], F32))

            p.wait("gpsimd", CC, 2)
            mgf2 = p.dma("gpsimd", SCC, 3, lambda e: e.dma_start(
                out=gmax2[:], in_=cc_out2))
            p.wait_m("vector", mgf2)
            p.op("vector", lambda e: e.tensor_tensor(
                out=smalls[0:1, 4:5], in0=s_sig[0:1, 1:2], in1=gmax2[:],
                op=ALU.mult))
            p.fence("vector", VE)
            p.op("vector", lambda e: e.tensor_scalar(
                out=smalls[0:1, 5:6], in0=smalls[0:1, 4:5],
                scalar1=1.0 / 127.0,
                scalar2=None, op0=ALU.mult, op1=ALU.bypass))
            p.fence("vector", VE)
            p.op("vector", lambda e: e.reciprocal(
                out=smalls[0:1, 6:7], in_=smalls[0:1, 5:6]))
            p.fence("vector", VE)
            p.fence("vector", VE)
            ve_sh = VE.n
            p.wait("tensor", VE, ve_sh)
            p.op("tensor", lambda e: e.matmul(
                ps_bd[:, 0:1], ones_row[:], smalls[0:1, 6:7],
                start=True, stop=True))
            te_bc2 = p.op("tensor", lambda e: e.matmul(
                ps_bd[:, 1:2], ones_row[:], smalls[0:1, 5:6],
                start=True, stop=True), sig=TE)
            p.wait("vector", TE, te_bc2)
            p.op("vector", lambda e: e.tensor_copy(out=rsh_bc[:],
                                                   in_=ps_bd[:, 0:1]))
            p.op("vector", lambda e: e.tensor_copy(out=sh_bc[:],
                                                   in_=ps_bd[:, 1:2]))
            p.fence("vector", VE)
            ve_dqd = p.op("vector", lambda e: e.tensor_scalar(
                out=dq_d[:], in0=swd[:], scalar1=sh_bc[:, 0:1],
                scalar2=1.0 / 127.0, op0=ALU.mult, op1=ALU.mult), sig=VE)
            p.fence("vector", VE)

            p.op("vector", lambda e: e.tensor_scalar(
                out=wqt_d[:, 0:NC], in0=swd[:], scalar1=1.0 / 127.0,
                scalar2=None, op0=ALU.mult, op1=ALU.bypass))
            p.fence("vector", VE)
            p.op("vector", lambda e: e.reciprocal(
                out=wqt_d[:, NC:2 * NC], in_=wqt_d[:, 0:NC]))
            p.fence("vector", VE)
            p.fence("vector", VE)
            ve_rd = VE.n
            p.wait("tensor", VE, ve_rd)
            for j in range(NC):
                te_td = p.op("tensor", lambda e, j=j: e.matmul(
                    ps_trd[:], wqt_d[:, NC + j:NC + j + 1], s_ident[:],
                    start=True, stop=True), sig=TE)
                p.wait("vector", TE, te_td)
                p.op("vector", lambda e, j=j: e.tensor_copy(
                    out=rr_d[0:1, 128 * j:128 * (j + 1)], in_=ps_trd[:]),
                    sig=VE)
                if j < NC - 1:
                    p.wait("tensor", VE, VE.n)
            p.fence("vector", VE)
            p.fence("vector", VE)
            ve_rrd = VE.n
            sc_e = []
            for j in range(NC):
                p.wait("tensor", VE, ve_rrd)
                if j >= 1:
                    p.wait("tensor", SC, sc_e[j - 1])
                te_e = p.op("tensor", lambda e, j=j: e.matmul(
                    ps_bd[:], ones_row[:], rr_d[0:1, 128 * j:128 * (j + 1)],
                    start=True, stop=True), sig=TE)
                p.wait("scalar", TE, te_e)
                sc_e.append(p.op("scalar", lambda e, j=j: e.activation(
                    out=rbc_d[:, 128 * j:128 * (j + 1)], in_=ps_bd[:],
                    func=AF.Identity), sig=SC))
            sc_expd = SC.n
            for ck in range(NIC):
                m = p.dma("sync", SWQ, 0, lambda e, ck=ck: e.dma_start(
                    out=wlv_d[:], in_=wdT_d[128 * ck:128 * (ck + 1), :]))
                p.wait_m("vector", m)
                p.wait("vector", SC, sc_expd)
                p.op("vector", lambda e: e.tensor_tensor(
                    out=wqt_d[:], in0=wlv_d[:], in1=rbc_d[:], op=ALU.mult))
                p.op("vector", lambda e: e.tensor_copy(out=wqi_d[:],
                                                       in_=wqt_d[:]))
                ve_q = p.op("vector", lambda e, ck=ck: e.tensor_scalar(
                    out=lvl_d[:, D * ck:D * (ck + 1)], in0=wqi_d[:],
                    scalar1=-128, scalar2=127, op0=ALU.max, op1=ALU.min),
                    sig=VE)
                p.wait("sync", VE, ve_q)
            ve_dlvl = VE.n

            bar_marks_d = [0] * (NG + 2)
            out_snaps = [None] * (NG + 2)
            for g in range(NG):
                if g >= 1:
                    p.wait_snap("sync", bar_marks_d[g - 1])
                    p.wait_snap("gpsimd", out_snaps[g - 1])
                mlds = p.dma("sync", PLDX, g, lambda e, g=g: e.dma_start(
                    out=sgrp[:], in_=s_s[:, NC * TG * g:NC * TG * (g + 1)]))
                mldx = mlds
                for c in range(NC):
                    mldx = p.dma("sync", PLDX, g, lambda e, g=g, c=c:
                                 e.dma_start(
                                     out=xgrp[:, TG * c:TG * (c + 1)],
                                     in_=xT[128 * c:128 * (c + 1),
                                            TG * g:TG * (g + 1)]))
                ve_hl = []
                for ic in range(NIC):
                    cs = ic % 2
                    col = NIC * TG * g + TG * ic
                    if ic >= 2:
                        p.wait("sync", VE, ve_hl[ic - 2])
                    mh = p.dma("sync", PHPL, ic, lambda e, cs=cs, col=col:
                               e.dma_start(
                                   out=hg_t[:, TG * cs:TG * (cs + 1)],
                                   in_=hp_s[:, col:col + TG]))
                    p.wait_m("vector", mh)
                    if g == 0 and ic == 0:
                        p.wait("vector", VE, ve_dlvl)
                    p.op("vector", lambda e, cs=cs: e.tensor_scalar(
                        out=hi_t[:, TG * cs:TG * (cs + 1)],
                        in0=hg_t[:, TG * cs:TG * (cs + 1)],
                        scalar1=rsh_bc[:, 0:1], scalar2=None,
                        op0=ALU.mult, op1=ALU.bypass))
                    ve_hl.append(p.op(
                        "vector", lambda e, ic=ic, cs=cs: e.tensor_scalar(
                            out=hl_t[:, TG * ic:TG * (ic + 1)],
                            in0=hi_t[:, TG * cs:TG * (cs + 1)],
                            scalar1=-128, scalar2=127,
                            op0=ALU.max, op1=ALU.min), sig=VE))
                ve_hl_all = ve_hl[-1]

                p.wait("tensor", VE, ve_hl_all)
                if g == 0:
                    p.wait("tensor", VE, ve_dlvl)
                sc_g = []; ve_o = []; gp_o = []; out_marks = []
                for mch in range(NC):
                    db = (mch % 2) * TG
                    if mch >= 2:
                        p.wait("tensor", SC, sc_g[mch - 2])
                    first = True
                    te_d = TE.n
                    for ic in range(NIC):
                        te_d = p.op("tensor", lambda e, ic=ic, mch=mch, db=db,
                                    first=first: e.matmul(
                                        ps_d[:, db:db + TG],
                                        lvl_d[:, D * ic + 128 * mch:D * ic + 128 * (mch + 1)],
                                        hl_t[:, TG * ic:TG * (ic + 1)],
                                        start=first, stop=(ic == NIC - 1)),
                                    sig=TE if ic == NIC - 1 else None) or te_d
                        first = False
                    p.wait("scalar", TE, te_d)
                    if g == 0 and mch == 0:
                        p.wait("scalar", VE, ve_dqd)
                    if mch >= 2:
                        p.wait("scalar", VE, ve_o[mch - 2])
                    sc_g.append(p.op("scalar", lambda e, mch=mch, db=db:
                                     e.activation(
                                         out=g_t[:, db:db + TG],
                                         in_=ps_d[:, db:db + TG],
                                         func=AF.Silu,
                                         scale=dq_d[:, mch:mch + 1]),
                                     sig=SC))
                    ob = (mch % 2) * 2 * TG
                    p.wait("vector", SC, sc_g[mch])
                    p.wait_m("vector", mlds)
                    if mch >= 2:
                        p.wait("vector", GP, gp_o[mch - 2])
                    ve_o.append(p.op(
                        "vector", lambda e, mch=mch, db=db, ob=ob:
                        e.tensor_tensor(
                            out=o_t[:, ob:ob + TG],
                            in0=sgrp[:, TG * mch:TG * (mch + 1)],
                            in1=g_t[:, db:db + TG], op=ALU.mult), sig=VE))
                    p.wait("gpsimd", VE, ve_o[mch])
                    p.wait_m("gpsimd", mldx)
                    if mch >= 2:
                        p.wait_m("gpsimd", out_marks[mch - 2])
                    gp_o.append(p.op(
                        "gpsimd", lambda e, mch=mch, ob=ob: e.tensor_tensor(
                            out=o_t[:, ob + TG:ob + 2 * TG],
                            in0=o_t[:, ob:ob + TG],
                            in1=xgrp[:, TG * mch:TG * (mch + 1)],
                            op=ALU.add), sig=GP))
                    col = NC * TG * g + TG * mch
                    out_marks.append(p.dma(
                        "gpsimd", POUT, mch, lambda e, ob=ob, col=col:
                        e.dma_start(out=outT[:, col:col + TG],
                                    in_=o_t[:, ob + TG:ob + 2 * TG])))
                out_snaps[g] = p.snap(POUT)

                bar_marks_d[g] = p.barrier(bar_engs)

            p.drain("sync", PLDX)
            p.drain("sync", PHPL)
            p.drain("sync", SWQ)
            p.drain("gpsimd", SCC)
            p.drain("gpsimd", POUT)

        with nc.Block() as block:
            block.sync(lambda e: [f(e) for f in p.items["sync"]])
            block.scalar(lambda e: [f(e) for f in p.items["scalar"]])
            block.vector(lambda e: [f(e) for f in p.items["vector"]])
            block.tensor(lambda e: [f(e) for f in p.items["tensor"]])
            block.gpsimd(lambda e: [f(e) for f in p.items["gpsimd"]])

    return nc


def _prep_inputs(inputs):
    x = np.ascontiguousarray(np.asarray(inputs["L0_attn_x"], dtype=np.float32))
    wq = np.asarray(inputs["L0_attn_Wq"], dtype=np.float32)
    wk = np.asarray(inputs["L0_attn_Wk"], dtype=np.float32)
    wv = np.asarray(inputs["L0_attn_Wv"], dtype=np.float32)
    bq = np.asarray(inputs["L0_attn_bq"], dtype=np.float32)
    bk = np.asarray(inputs["L0_attn_bk"], dtype=np.float32)
    bv = np.asarray(inputs["L0_attn_bv"], dtype=np.float32)
    gam = np.asarray(inputs["L0_attn_gamma"], dtype=np.float32)
    bet = np.asarray(inputs["L0_attn_beta"], dtype=np.float32)
    clip_x = np.asarray(inputs["L0_ffn_clip_x"], dtype=np.float32)
    clip_d = np.asarray(inputs["L0_ffn_clip_down"], dtype=np.float32)
    wg = np.asarray(inputs["L0_ffn_w_gate"], dtype=np.float32)
    wu = np.asarray(inputs["L0_ffn_w_up"], dtype=np.float32)
    wd = np.asarray(inputs["L0_ffn_w_down"], dtype=np.float32)

    ec = np.zeros((16, 1024), dtype=np.float32)
    for c in range(NC):
        ec[2 * c, 128 * c:128 * c + 64] = 1.0
        ec[2 * c + 1, 128 * c + 64:128 * (c + 1)] = 1.0

    shared = {
        "wqT": np.ascontiguousarray(wq.T),
        "wkT": np.ascontiguousarray(wk.T),
        "wvT": np.ascontiguousarray(wv.T),
        "bq_pack": np.ascontiguousarray(bq.reshape(NC, 128).T),
        "bk_row": np.ascontiguousarray(bk.reshape(1, D)),
        "bv_row": np.ascontiguousarray(bv.reshape(1, D)),
        "gamma_pack": np.ascontiguousarray(gam.reshape(NC, 128).T),
        "beta_pack": np.ascontiguousarray(bet.reshape(NC, 128).T),
        "clip2": np.ascontiguousarray(
            np.stack([clip_x[0], clip_d[0]]).reshape(1, 2)),
        "wgT": np.ascontiguousarray(wg.T),
        "wuT": np.ascontiguousarray(wu.T),
        "wg_raw": np.ascontiguousarray(wg),
        "wu_raw": np.ascontiguousarray(wu),
        "wd_raw": np.ascontiguousarray(wd),
        "wdT": np.ascontiguousarray(wd.T),
        "ident": np.eye(128, dtype=np.float32),
        "ec": ec,
        "emask": np.kron(np.eye(2, dtype=np.float32),
                         np.ones((64, 64), np.float32)),
    }
    in_maps = []
    for b in range(NCORES):
        m = dict(shared)
        m["xT"] = np.ascontiguousarray(x[b].T)
        in_maps.append(m)
    return in_maps


def kernel(**inputs):
    in_maps = _prep_inputs(inputs)
    nc = build()
    res = run_bass_kernel_spmd(nc, in_maps, core_ids=list(range(NCORES)))
    out = np.empty((NCORES, NTOK, D), dtype=np.float32)
    for b in range(NCORES):
        r = np.asarray(res.results[b]["outT"])
        r4 = r.reshape(128, NG, NC, TG)
        feat_major = np.transpose(r4, (2, 0, 1, 3)).reshape(D, NTOK)
        out[b] = feat_major.T
    return out


# version 10
# speedup vs baseline: 1.0635x; 1.0635x over previous
import numpy as np
from contextlib import ExitStack

import concourse.bass as bass
import concourse.mybir as mybir
from concourse.bass_utils import run_bass_kernel_spmd

F32 = mybir.dt.float32
BF16 = mybir.dt.bfloat16
I32 = mybir.dt.int32
AF = mybir.ActivationFunctionType
ALU = mybir.AluOpType
AX = mybir.AxisListType

D = 1024
NTOK = 4096
NG = 8
TG = 512
NC = 8
CI = 2816
NIC = 22
NH = 11
NTILE = 4
EPS_LN = 1e-5
EPS_DEN = 1e-6
NCORES = 8

ENGS = ("sync", "scalar", "vector", "tensor", "gpsimd")


class Sem:
    def __init__(self, h):
        self.h = h
        self.n = 0


class Prog:
    def __init__(self, nc, ctx):
        self.nc = nc
        self.ctx = ctx
        self.items = {e: [] for e in ENGS}

    def sem(self, name):
        return Sem(self.ctx.enter_context(self.nc.semaphore(name)))

    def pool(self, name, k):
        return [self.sem(f"{name}{i}") for i in range(k)]

    def op(self, eng, fn, sig=None):
        if sig is None:
            self.items[eng].append(fn)
            return None
        sig.n += 1
        self.items[eng].append(lambda e, f=fn, s=sig: f(e).then_inc(s.h, 1))
        return sig.n

    def dma(self, eng, pool, idx, fn):
        s = pool[idx % len(pool)]
        s.n += 16
        self.items[eng].append(lambda e, f=fn, ss=s: f(e).then_inc(ss.h, 16))
        return (s, s.n)

    def wait(self, eng, sem, n=None):
        v = sem.n if n is None else n
        if v > 0:
            self.items[eng].append(lambda e, s=sem, vv=v: e.wait_ge(s.h, vv))

    def wait_m(self, eng, mark):
        s, v = mark
        if v > 0:
            self.items[eng].append(lambda e, ss=s, vv=v: e.wait_ge(ss.h, vv))

    def snap(self, pool):
        return [(s, s.n) for s in pool]

    def wait_snap(self, eng, snap):
        for s, v in snap:
            if v > 0:
                self.items[eng].append(
                    lambda e, ss=s, vv=v: e.wait_ge(ss.h, vv))

    def drain(self, eng, pool):
        self.wait_snap(eng, self.snap(pool))

    def fence(self, eng, sem):
        self.op(eng, lambda ee: ee.nop(), sig=sem)
        self.wait(eng, sem, sem.n)

    def barrier(self, engsems):
        snap = []
        for e, s in engsems:
            if e != "tensor":
                self.op(e, lambda ee: ee.nop(), sig=s)
            snap.append((e, s, s.n))
        for e, _, _ in snap:
            for o, s, v in snap:
                if o != e:
                    self.wait(e, s, v)
        return [(s, v) for _, s, v in snap]




def _finalize(p, bar_engs):
    nc = p.nc
    with nc.Block() as block:
        block.sync(lambda e: [f(e) for f in p.items["sync"]])
        block.scalar(lambda e: [f(e) for f in p.items["scalar"]])
        block.vector(lambda e: [f(e) for f in p.items["vector"]])
        block.tensor(lambda e: [f(e) for f in p.items["tensor"]])
        block.gpsimd(lambda e: [f(e) for f in p.items["gpsimd"]])

def build(passes=("A", "B1", "B2", "C")):
    nc = bass.Bass("TRN2", target_bir_lowering=False,
                   detect_race_conditions=False)

    def din(name, shape, dtype=F32):
        return nc.dram_tensor(name, shape, dtype, kind="ExternalInput").ap()

    xT = din("xT", [D, NTOK])
    wqT_d = din("wqT", [D, D]); wkT_d = din("wkT", [D, D]); wvT_d = din("wvT", [D, D])
    bq_pack = din("bq_pack", [128, NC])
    bk_row = din("bk_row", [1, D]); bv_row = din("bv_row", [1, D])
    gamma_pack = din("gamma_pack", [128, NC]); beta_pack = din("beta_pack", [128, NC])
    clip2 = din("clip2", [1, 2])
    wgT_d = din("wgT", [D, CI]); wuT_d = din("wuT", [D, CI])
    wg_raw = din("wg_raw", [CI, D]); wu_raw = din("wu_raw", [CI, D])
    wd_raw = din("wd_raw", [D, CI])
    wdT_d = din("wdT", [CI, D])
    ident_d = din("ident", [128, 128])
    ec_d = din("ec", [16, 1024])
    emask_d = din("emask", [128, 128])

    outT = nc.dram_tensor("outT", [128, NC * NTOK], F32, kind="ExternalOutput").ap()

    _k = "ExternalOutput" if "DBG" in passes else "Internal"
    xn_s = nc.dram_tensor("xn_s", [128, NC * NTOK], F32, kind=_k).ap()
    xnh_s = nc.dram_tensor("xnh_s", [128, NC * NTOK], BF16, kind=_k).ap()
    xnl_s = nc.dram_tensor("xnl_s", [128, NC * NTOK], BF16, kind=_k).ap()
    s_s = nc.dram_tensor("s_s", [128, NC * NTOK], F32, kind=_k).ap()
    xq_s = nc.dram_tensor("xq_s", [128, NC * NTOK], BF16, kind=_k).ap()
    hp_s = nc.dram_tensor("hp_s", [128, NIC * NTOK], F32, kind=_k).ap()
    dbg_s = nc.dram_tensor("dbg_s", [128, 8192], F32, kind=_k).ap()
    dbg_s2 = nc.dram_tensor("dbg_s2", [128, 8192], F32, kind=_k).ap()
    cc_in1 = nc.dram_tensor("cc_in1", [1, 1], F32, kind="Internal").ap()
    cc_out1 = nc.dram_tensor("cc_out1", [1, 1], F32, kind="Internal").ap()
    cc_in2 = nc.dram_tensor("cc_in2", [1, 1], F32, kind="Internal").ap()
    cc_out2 = nc.dram_tensor("cc_out2", [1, 1], F32, kind="Internal").ap()

    with ExitStack() as ctx:
        def sb(name, shape, dtype=F32):
            return ctx.enter_context(nc.sbuf_tensor(name, shape, dtype))

        p = Prog(nc, ctx)

        s_ident = sb("s_ident", [128, 128])
        s_ec = sb("s_ec", [16, 1024])
        ones_col = sb("ones_col", [128, 1])
        ones_col_b = sb("ones_col_b", [128, 1], BF16)
        ones_row = sb("ones_row", [1, 128])
        ones_row_b = sb("ones_row_b", [1, 128], BF16)
        s_bq = sb("s_bq", [128, NC])
        s_gamma = sb("s_gamma", [128, NC])
        s_beta = sb("s_beta", [128, NC])
        s_clip = sb("s_clip", [1, 2])
        s_sig = sb("s_sig", [1, 2])
        s_eps = sb("s_eps", [1, 1])
        smalls = sb("smalls", [1, 16])
        gmax1 = sb("gmax1", [1, 1]); gmax2 = sb("gmax2", [1, 1])
        mxcol = sb("mxcol", [128, 1]); mxcol2 = sb("mxcol2", [128, 1])
        rsx_bc = sb("rsx_bc", [128, 1]); rsh_bc = sb("rsh_bc", [128, 1])
        sx_bc = sb("sx_bc", [128, 1]); sh_bc = sb("sh_bc", [128, 1])
        kvblk = sb("kvblk", [128, NC * 128])
        ksum_fm = sb("ksum_fm", [128, NC])
        ksum_blk = sb("ksum_blk", [128, NC * 16])
        ksum_ex = sb("ksum_ex", [128, NC * 128])
        s_emask = sb("s_emask", [128, 128])
        k_acc = sb("k_acc", [128, D])
        swg = sb("swg", [128, NIC]); swu = sb("swu", [128, NIC])
        swd = sb("swd", [128, NC])
        dq_g = sb("dq_g", [128, NIC]); dq_u = sb("dq_u", [128, NIC])
        dq_d = sb("dq_d", [128, NC])
        row_a = sb("row_a", [1, TG]); row_b = sb("row_b", [1, TG])
        row_c = sb("row_c", [1, TG]); row_d = sb("row_d", [1, TG])

        if "SEMSHIFT" in passes:
            _dummy = [p.sem(f"dummy{i}") for i in range(10)]
        VE = p.sem("ve"); SC = p.sem("sc"); TE = p.sem("te"); GP = p.sem("gp")
        CC = p.sem("cc")
        PCONST = p.pool("pconst", 1)
        PWLD = p.pool("pwld", 2)
        PLDX = p.pool("pldx", 4)
        PWR = p.pool("pwr", 2)
        PSTX = p.pool("pstx", 4)
        PXQ = p.pool("pxq", 2)
        PHP = p.pool("php", 4)
        PHPL = p.pool("phpl", 4)
        SWQ = p.pool("swq", 2)
        POUT = p.pool("pout", 4)
        SCC = p.pool("scc", 4)

        bar_engs = (("scalar", SC), ("vector", VE), ("tensor", TE),
                    ("gpsimd", GP))

        for t_, s_ in ((ident_d, s_ident), (ec_d, s_ec), (emask_d, s_emask),
                       (bq_pack, s_bq),
                       (gamma_pack, s_gamma), (beta_pack, s_beta),
                       (clip2, s_clip)):
            mconst = p.dma("sync", PCONST, 0, lambda e, t_=t_, s_=s_:
                           e.dma_start(out=s_[:], in_=t_))

        p.op("vector", lambda e: e.memset(ones_col[:], 1.0))
        p.op("vector", lambda e: e.memset(ones_col_b[:], 1.0))
        p.op("vector", lambda e: e.memset(ones_row[:], 1.0))
        p.op("vector", lambda e: e.memset(ones_row_b[:], 1.0))
        p.op("vector", lambda e: e.memset(s_eps[:], EPS_LN))
        p.op("vector", lambda e: e.memset(mxcol[:], 0.0))
        p.op("vector", lambda e: e.memset(ksum_blk[:], 0.0))
        p.op("vector", lambda e: e.memset(kvblk[:], 0.0))
        p.op("gpsimd", lambda e: e.memset(k_acc[:], 0.0), sig=GP)

        p.wait_m("scalar", mconst)
        p.op("scalar", lambda e: e.activation(out=s_sig[:], in_=s_clip[:],
                                              func=AF.Sigmoid), sig=SC)
        p.wait_m("vector", mconst)

        _a_stage = "A"
        for _s in ("A0", "A1", "A2"):
            if _s in passes:
                _a_stage = _s
        _do_groups = _a_stage != "A0"
        _do_kvproj = _a_stage in ("A2", "A")
        _do_kv = _a_stage == "A"
        _ngroups = NG
        for _s in passes:
            if _s.startswith("G"):
                _ngroups = int(_s[1:])
        _no_store = "NOSTORE" in passes
        _no_stats = "NOSTATS" in passes
        _no_rowmax = "NORM" in passes
        with ExitStack() as actx:
            def sba(name, shape, dtype=F32):
                return actx.enter_context(nc.sbuf_tensor(name, shape, dtype))

            s_bkh = sba("s_bkh", [1, D], BF16); s_bkl = sba("s_bkl", [1, D], BF16)
            s_bvh = sba("s_bvh", [1, D], BF16); s_bvl = sba("s_bvl", [1, D], BF16)
            wk_h = sba("wk_h", [128, NC * D], BF16); wk_l = sba("wk_l", [128, NC * D], BF16)
            wv_h = sba("wv_h", [128, NC * D], BF16); wv_l = sba("wv_l", [128, NC * D], BF16)
            wld_t = sba("wld_t", [128, 2 * D])
            wraw_t = sba("wraw_t", [128, 2 * D])
            xg = sba("xg", [128, 2 * NC * TG])
            xh_t = sba("xh_t", [128, 2 * TG], BF16)
            xl_t = sba("xl_t", [128, 2 * TG], BF16)
            xsq_t = sba("xsq_t", [128, 2 * TG])
            xn_t = sba("xn_t", [128, 2 * TG])
            xnh_t = sba("xnh_t", [128, NC * TG], BF16)
            xnl_t = sba("xnl_t", [128, NC * TG], BF16)
            k_tile = sba("k_tile", [128, D]); v_tile = sba("v_tile", [128, D])

            ps_kv = actx.enter_context(nc.psum_tensor("ps_kv", [128, NC * 128], F32))
            ps_k = actx.enter_context(nc.psum_tensor("ps_k", [128, D], F32))
            ps_v = actx.enter_context(nc.psum_tensor("ps_v", [128, D], F32))
            ps_st = actx.enter_context(nc.psum_tensor("ps_st", [128, 2 * TG], F32))

            p.dma("sync", PWLD, 0, lambda e: e.dma_start(
                out=wld_t[0:1, 0:D], in_=bk_row))
            mbias = p.dma("sync", PWLD, 0, lambda e: e.dma_start(
                out=wld_t[0:1, D:2 * D], in_=bv_row))
            p.wait_m("vector", mbias)
            p.op("vector", lambda e: e.tensor_copy(out=s_bkh[:],
                                                   in_=wld_t[0:1, 0:D]))
            p.op("vector", lambda e: e.tensor_tensor(
                out=s_bkl[:], in0=wld_t[0:1, 0:D], in1=s_bkh[:],
                op=ALU.subtract))
            p.op("vector", lambda e: e.tensor_copy(out=s_bvh[:],
                                                   in_=wld_t[0:1, D:2 * D]))
            ve_bias = p.op("vector", lambda e: e.tensor_tensor(
                out=s_bvl[:], in0=wld_t[0:1, D:2 * D], in1=s_bvh[:],
                op=ALU.subtract), sig=VE)
            p.wait("sync", VE, ve_bias)

            wsplit_marks = []
            idx = 0
            for wi, (src, dh, dl) in enumerate(
                    ((wkT_d, wk_h, wk_l), (wvT_d, wv_h, wv_l))):
                for c in range(NC):
                    slot = idx % 2
                    if idx >= 2:
                        p.wait("sync", VE, wsplit_marks[idx - 2])
                    m = p.dma("sync", PWLD, idx, lambda e, src=src, c=c,
                              slot=slot: e.dma_start(
                                  out=wld_t[:, slot * D:(slot + 1) * D],
                                  in_=src[128 * c:128 * (c + 1), :]))
                    p.wait_m("vector", m)
                    p.op("vector", lambda e, dh=dh, c=c, slot=slot:
                         e.tensor_copy(
                             out=dh[:, D * c:D * (c + 1)],
                             in_=wld_t[:, slot * D:(slot + 1) * D]))
                    wsplit_marks.append(p.op(
                        "vector", lambda e, dh=dh, dl=dl, c=c, slot=slot:
                        e.tensor_tensor(
                            out=dl[:, D * c:D * (c + 1)],
                            in0=wld_t[:, slot * D:(slot + 1) * D],
                            in1=dh[:, D * c:D * (c + 1)], op=ALU.subtract),
                        sig=VE))
                    idx += 1

            rowmax_jobs = []
            for wi, (raw, dst) in enumerate(((wg_raw, swg), (wu_raw, swu))):
                for j in range(NIC):
                    rowmax_jobs.append((raw, dst, j))
            ve_rm_marks = []

            def emit_rowmax(job_idx):
                raw, dst, j = rowmax_jobs[job_idx]
                slot = job_idx % 2
                if job_idx >= 2:
                    p.wait("scalar", VE, ve_rm_marks[job_idx - 2])
                m = p.dma("scalar", PWR, job_idx, lambda e, raw=raw, j=j,
                          slot=slot: e.dma_start(
                              out=wraw_t[:, slot * D:(slot + 1) * D],
                              in_=raw[128 * j:128 * (j + 1), :]))
                p.wait_m("vector", m)
                ve_rm_marks.append(p.op(
                    "vector", lambda e, dst=dst, j=j, slot=slot:
                    e.reduce_max(out=dst[:, j:j + 1],
                                 in_=wraw_t[:, slot * D:(slot + 1) * D],
                                 axis=AX.X, apply_absolute_value=True),
                    sig=VE))

            ldx_marks = []
            bar_marks = [0] * (NG + 2)
            stx_snaps = [None] * (NG + 2)
            for g in range(_ngroups if _do_groups else 0):
                slot = g % 2
                gbase = slot * NC * TG
                if g >= 2:
                    p.wait_snap("sync", bar_marks[g - 2])
                for c in range(NC):
                    m = p.dma("sync", PLDX, g, lambda e, g=g, c=c, gbase=gbase:
                              e.dma_start(
                                  out=xg[:, gbase + TG * c:gbase + TG * (c + 1)],
                                  in_=xT[128 * c:128 * (c + 1),
                                         TG * g:TG * (g + 1)]))
                ldx_marks.append(m)
                if g >= 1:
                    p.wait_snap("scalar", stx_snaps[g - 1])
                    p.wait_snap("vector", stx_snaps[g - 1])

                lo = (len(rowmax_jobs) * g) // NG
                hi = (len(rowmax_jobs) * (g + 1)) // NG
                for ji in range(lo, hi):
                    if not _no_rowmax:
                        emit_rowmax(ji)

                p.wait_m("vector", ldx_marks[g])
                p.wait_m("scalar", ldx_marks[g])
                if _no_stats:
                    if "BARNOP" in passes:
                        for _e, _s in bar_engs:
                            p.op(_e, lambda ee: ee.nop())
                    elif "BARINC" in passes:
                        for _e, _s in bar_engs:
                            p.op(_e, lambda ee: ee.nop(), sig=_s)
                    elif any(_x.startswith("BW") for _x in passes):
                        sel = [_x for _x in passes if _x.startswith("BW")][0][2:]
                        snap = []
                        for _e, _s in bar_engs:
                            p.op(_e, lambda ee: ee.nop(), sig=_s)
                            snap.append((_e, _s, _s.n))
                        pairs = {"sv": ("scalar", "vector"),
                                 "st": ("scalar", "tensor"),
                                 "sg": ("scalar", "gpsimd"),
                                 "vs": ("vector", "scalar"),
                                 "vt": ("vector", "tensor"),
                                 "vg": ("vector", "gpsimd"),
                                 "ts": ("tensor", "scalar"),
                                 "tv": ("tensor", "vector"),
                                 "tg": ("tensor", "gpsimd"),
                                 "gs": ("gpsimd", "scalar"),
                                 "gv": ("gpsimd", "vector"),
                                 "gt": ("gpsimd", "tensor")}
                        for code in sel.split("-"):
                            we, oe = pairs[code]
                            for _e, _s, _v in snap:
                                if _e == oe:
                                    p.wait(we, _s, _v)
                    elif "NOBAR" not in passes:
                        bar_marks[g] = p.barrier(bar_engs)
                    stx_snaps[g] = p.snap(PSTX)
                    continue
                ve_sp = []
                sc_sq = []
                te_sum = []
                te_sq = []
                for c in range(NC):
                    cs = c % 2
                    if c >= 2:
                        p.wait("vector", TE, te_sum[c - 2])
                        p.wait("scalar", TE, te_sq[c - 2])
                    p.op("vector", lambda e, c=c, cs=cs, gbase=gbase:
                         e.tensor_copy(
                             out=xh_t[:, TG * cs:TG * (cs + 1)],
                             in_=xg[:, gbase + TG * c:gbase + TG * (c + 1)]))
                    ve_sp.append(p.op(
                        "vector", lambda e, c=c, cs=cs, gbase=gbase:
                        e.tensor_tensor(
                            out=xl_t[:, TG * cs:TG * (cs + 1)],
                            in0=xg[:, gbase + TG * c:gbase + TG * (c + 1)],
                            in1=xh_t[:, TG * cs:TG * (cs + 1)],
                            op=ALU.subtract), sig=VE))
                    sc_sq.append(p.op(
                        "scalar", lambda e, c=c, cs=cs, gbase=gbase:
                        e.activation(
                            out=xsq_t[:, TG * cs:TG * (cs + 1)],
                            in_=xg[:, gbase + TG * c:gbase + TG * (c + 1)],
                            func=AF.Square), sig=SC))
                    p.wait("tensor", VE, ve_sp[c])
                    p.op("tensor", lambda e, c=c, cs=cs: e.matmul(
                        ps_st[0:1, 0:TG], ones_col_b[:],
                        xh_t[:, TG * cs:TG * (cs + 1)],
                        start=(c == 0), stop=False))
                    te_sum.append(p.op("tensor", lambda e, c=c, cs=cs: e.matmul(
                        ps_st[0:1, 0:TG], ones_col_b[:],
                        xl_t[:, TG * cs:TG * (cs + 1)],
                        start=False, stop=(c == NC - 1)), sig=TE))
                    p.wait("tensor", SC, sc_sq[c])
                    te_sq.append(p.op("tensor", lambda e, c=c, cs=cs: e.matmul(
                        ps_st[0:1, TG:2 * TG], ones_col[:],
                        xsq_t[:, TG * cs:TG * (cs + 1)],
                        start=(c == 0), stop=(c == NC - 1)), sig=TE))
                te_stats = TE.n

                p.wait("vector", TE, te_stats)
                p.op("vector", lambda e: e.tensor_scalar(
                    out=row_a[:], in0=ps_st[0:1, 0:TG], scalar1=-1.0 / D,
                    scalar2=None, op0=ALU.mult, op1=ALU.bypass))
                p.op("vector", lambda e: e.tensor_scalar(
                    out=row_b[:], in0=ps_st[0:1, TG:2 * TG], scalar1=1.0 / D,
                    scalar2=None, op0=ALU.mult, op1=ALU.bypass))
                p.op("vector", lambda e: e.tensor_tensor(
                    out=row_c[:], in0=row_a[:], in1=row_a[:], op=ALU.mult))
                ve_sm1 = p.op("vector", lambda e: e.tensor_tensor(
                    out=row_b[:], in0=row_b[:], in1=row_c[:], op=ALU.subtract),
                    sig=VE)
                p.wait("scalar", VE, ve_sm1)
                sc_sqrt = p.op("scalar", lambda e: e.activation(
                    out=row_c[:], in_=row_b[:], func=AF.Sqrt,
                    bias=s_eps[0:1, 0:1]), sig=SC)
                p.wait("vector", SC, sc_sqrt)
                p.op("vector", lambda e: e.reciprocal(
                    out=row_d[:], in_=row_c[:]))
                p.fence("vector", VE)
                p.fence("vector", VE)
                ve_sm2 = VE.n

                p.wait("tensor", VE, ve_sm2)
                p.op("tensor", lambda e: e.matmul(
                    ps_st[0:128, 0:TG], ones_row[:], row_a[:],
                    start=True, stop=True))
                te_exp = p.op("tensor", lambda e: e.matmul(
                    ps_st[0:128, TG:2 * TG], ones_row[:], row_d[:],
                    start=True, stop=True), sig=TE)

                p.wait("vector", TE, te_exp)
                ve_xn = []
                stx_marks = []
                for c in range(NC):
                    cs = c % 2
                    if c >= 2:
                        p.wait_m("vector", stx_marks[c - 2])
                    p.op("vector", lambda e, c=c, cs=cs, gbase=gbase:
                         e.tensor_tensor(
                             out=xsq_t[:, TG * cs:TG * (cs + 1)],
                             in0=xg[:, gbase + TG * c:gbase + TG * (c + 1)],
                             in1=ps_st[0:128, 0:TG], op=ALU.add))
                    p.op("vector", lambda e, cs=cs: e.tensor_tensor(
                        out=xn_t[:, TG * cs:TG * (cs + 1)],
                        in0=xsq_t[:, TG * cs:TG * (cs + 1)],
                        in1=ps_st[0:128, TG:2 * TG], op=ALU.mult))
                    p.op("vector", lambda e, c=c, cs=cs: e.tensor_copy(
                        out=xnh_t[:, TG * c:TG * (c + 1)],
                        in_=xn_t[:, TG * cs:TG * (cs + 1)]))
                    ve_xn.append(p.op(
                        "vector", lambda e, c=c, cs=cs: e.tensor_tensor(
                            out=xnl_t[:, TG * c:TG * (c + 1)],
                            in0=xn_t[:, TG * cs:TG * (cs + 1)],
                            in1=xnh_t[:, TG * c:TG * (c + 1)],
                            op=ALU.subtract), sig=VE))
                    p.op("vector", lambda e, cs=cs: e.reduce_max(
                        out=mxcol2[:], in_=xn_t[:, TG * cs:TG * (cs + 1)],
                        axis=AX.X, apply_absolute_value=True))
                    p.fence("vector", VE)
                    p.op("vector", lambda e: e.tensor_tensor(
                        out=mxcol[:], in0=mxcol[:], in1=mxcol2[:],
                        op=ALU.max))
                    p.fence("vector", VE)
                    col = NC * TG * g + TG * c
                    p.wait("scalar", VE, ve_xn[c])
                    if not _no_store:
                        p.dma("scalar", PSTX, c, lambda e, cs=cs, col=col:
                              e.dma_start(out=xn_s[:, col:col + TG],
                                          in_=xn_t[:, TG * cs:TG * (cs + 1)]))
                        p.dma("scalar", PSTX, c, lambda e, c=c, col=col:
                              e.dma_start(out=xnh_s[:, col:col + TG],
                                          in_=xnh_t[:, TG * c:TG * (c + 1)]))
                        stx_marks.append(p.dma(
                            "scalar", PSTX, c, lambda e, c=c, col=col:
                            e.dma_start(out=xnl_s[:, col:col + TG],
                                        in_=xnl_t[:, TG * c:TG * (c + 1)])))
                    else:
                        stx_marks.append((PSTX[0], 0))
                stx_snaps[g] = p.snap(PSTX)

                p.wait("tensor", VE, ve_xn[NC - 1])
                te_k = []; te_v = []; te_kv = []
                ve_kev = []; sc_vev = []; gp_acc = []
                for t in range(NTILE if _do_kvproj else 0):
                    toff = 128 * t
                    if t >= 1:
                        p.wait("tensor", VE, ve_kev[t - 1])
                    for nh in range(2):
                        first = True
                        for cc in range(NC):
                            for (la, rb) in ((xnh_t, wk_h), (xnh_t, wk_l),
                                             (xnl_t, wk_h)):
                                p.op("tensor", lambda e, cc=cc, la=la, rb=rb,
                                     nh=nh, toff=toff, first=first: e.matmul(
                                         ps_k[:, 512 * nh:512 * (nh + 1)],
                                         la[:, TG * cc + toff:TG * cc + toff + 128],
                                         rb[:, D * cc + 512 * nh:D * cc + 512 * (nh + 1)],
                                         start=first, stop=False))
                                first = False
                        p.op("tensor", lambda e, nh=nh: e.matmul(
                            ps_k[:, 512 * nh:512 * (nh + 1)], ones_row_b[:],
                            s_bkh[0:1, 512 * nh:512 * (nh + 1)],
                            start=False, stop=False))
                        if nh == 1:
                            te_k.append(p.op("tensor", lambda e, nh=nh:
                                             e.matmul(
                                                 ps_k[:, 512 * nh:512 * (nh + 1)],
                                                 ones_row_b[:],
                                                 s_bkl[0:1, 512 * nh:512 * (nh + 1)],
                                                 start=False, stop=True),
                                             sig=TE))
                        else:
                            p.op("tensor", lambda e, nh=nh: e.matmul(
                                ps_k[:, 512 * nh:512 * (nh + 1)], ones_row_b[:],
                                s_bkl[0:1, 512 * nh:512 * (nh + 1)],
                                start=False, stop=True))
                    if t >= 1:
                        p.wait("tensor", SC, sc_vev[t - 1])
                    for nh in range(2):
                        first = True
                        for cc in range(NC):
                            for (la, rb) in ((xnh_t, wv_h), (xnh_t, wv_l),
                                             (xnl_t, wv_h)):
                                p.op("tensor", lambda e, cc=cc, la=la, rb=rb,
                                     nh=nh, toff=toff, first=first: e.matmul(
                                         ps_v[:, 512 * nh:512 * (nh + 1)],
                                         la[:, TG * cc + toff:TG * cc + toff + 128],
                                         rb[:, D * cc + 512 * nh:D * cc + 512 * (nh + 1)],
                                         start=first, stop=False))
                                first = False
                        p.op("tensor", lambda e, nh=nh: e.matmul(
                            ps_v[:, 512 * nh:512 * (nh + 1)], ones_row_b[:],
                            s_bvh[0:1, 512 * nh:512 * (nh + 1)],
                            start=False, stop=False))
                        if nh == 1:
                            te_v.append(p.op("tensor", lambda e, nh=nh:
                                             e.matmul(
                                                 ps_v[:, 512 * nh:512 * (nh + 1)],
                                                 ones_row_b[:],
                                                 s_bvl[0:1, 512 * nh:512 * (nh + 1)],
                                                 start=False, stop=True),
                                             sig=TE))
                        else:
                            p.op("tensor", lambda e, nh=nh: e.matmul(
                                ps_v[:, 512 * nh:512 * (nh + 1)], ones_row_b[:],
                                s_bvl[0:1, 512 * nh:512 * (nh + 1)],
                                start=False, stop=True))

                    p.wait("vector", TE, te_k[t])
                    if te_kv:
                        p.wait("vector", TE, te_kv[-1])
                    if gp_acc:
                        p.wait("vector", GP, gp_acc[-1])
                    ve_kev.append(p.op("vector", lambda e: e.tensor_relu(
                        out=k_tile[:], in_=ps_k[:]), sig=VE))
                    p.wait("scalar", TE, te_v[t])
                    if te_kv:
                        p.wait("scalar", TE, te_kv[-1])
                    sc_vev.append(p.op("scalar", lambda e: e.activation(
                        out=v_tile[:], in_=ps_v[:], func=AF.Identity), sig=SC))

                    if _do_kv:
                        p.wait("tensor", VE, ve_kev[t])
                        p.wait("tensor", SC, sc_vev[t])
                        for cc in range(NC):
                            _sig = TE if cc == NC - 1 else None
                            _mk = p.op("tensor", lambda e, cc=cc, g=g, t=t:
                                       e.matmul(
                                           ps_kv[:, 128 * cc:128 * (cc + 1)],
                                           k_tile[:, 128 * cc:128 * (cc + 1)],
                                           v_tile[:, 128 * cc:128 * (cc + 1)],
                                           start=(g == 0 and t == 0
                                                  and cc % 4 == 0),
                                           stop=(g == NG - 1
                                                 and t == NTILE - 1
                                                 and cc % 4 == 3)), sig=_sig)
                        te_kv.append(_mk)
                        p.wait("gpsimd", VE, ve_kev[t])
                        gp_acc.append(p.op("gpsimd", lambda e: e.tensor_tensor(
                            out=k_acc[:], in0=k_acc[:], in1=k_tile[:],
                            op=ALU.add), sig=GP))

                bar_marks[g] = p.barrier(bar_engs)

            p.wait("tensor", GP, GP.n)
            te_ks = TE.n
            for c in range(NC if _do_kv else 0):
                te_ks = p.op("tensor", lambda e, c=c: e.matmul(
                    ps_st[0:128, c:c + 1], k_acc[:, 128 * c:128 * (c + 1)],
                    ones_col[:], start=(c == 0), stop=(c == NC - 1)),
                    sig=TE if c == NC - 1 else None) or te_ks
            p.wait("vector", TE, te_ks)
            if _do_kv:
                p.op("vector", lambda e: e.tensor_copy(out=ksum_fm[:],
                                                       in_=ps_st[0:128, 0:NC]))
                for c in range(NC):
                    p.op("vector", lambda e, c=c: e.tensor_copy(
                        out=ksum_blk[0:64, 16 * c + 2 * c:16 * c + 2 * c + 1],
                        in_=ps_st[0:64, c:c + 1]))
                    p.op("vector", lambda e, c=c: e.tensor_copy(
                        out=ksum_blk[64:128, 16 * c + 2 * c + 1:16 * c + 2 * c + 2],
                        in_=ps_st[64:128, c:c + 1]))
                for c in range(NC):
                    p.op("vector", lambda e, c=c: e.tensor_copy(
                        out=kvblk[0:64, 128 * c:128 * c + 64],
                        in_=ps_kv[0:64, 128 * c:128 * c + 64]))
                    p.op("vector", lambda e, c=c: e.tensor_copy(
                        out=kvblk[64:128, 128 * c + 64:128 * (c + 1)],
                        in_=ps_kv[64:128, 128 * c + 64:128 * (c + 1)]))
            if _do_kv:
                p.fence("vector", VE)
                for c in range(NC):
                    p.op("vector", lambda e, c=c: e.tensor_scalar(
                        out=ksum_ex[:, 128 * c:128 * (c + 1)], in0=s_emask[:],
                        scalar1=ksum_fm[:, c:c + 1], scalar2=None,
                        op0=ALU.mult, op1=ALU.bypass))
            ve_fin = p.op("vector", lambda e: e.nop(), sig=VE)

            if not _do_groups:
                p.op("vector", lambda e: e.memset(smalls[0:1, 0:1], 1.0))
            te_gm = p.op("tensor", lambda e: e.matmul(
                ps_st[0:1, TG:TG + 128], mxcol[:], s_ident[:],
                start=True, stop=True), sig=TE)
            p.wait("vector", TE, te_gm)
            p.op("vector", lambda e: e.tensor_copy(
                out=row_a[0:1, 0:128], in_=ps_st[0:1, TG:TG + 128]))
            p.fence("vector", VE)
            if "DBG" in passes:
                p.dma("scalar", PSTX, 3, lambda e: e.dma_start(
                    out=dbg_s[0:1, 4200:4328], in_=row_a[0:1, 0:128]))
                p.dma("scalar", PSTX, 3, lambda e: e.dma_start(
                    out=dbg_s[:, 4330:4331], in_=mxcol[:],
                    allow_slow_non_contiguous=True))
            p.op("vector", lambda e: e.reduce_max(
                out=smalls[0:1, 0:1], in_=row_a[0:1, 0:128], axis=AX.X))
            ve_gm = p.fence("vector", VE) or VE.n
            if "DBG" in passes:
                p.wait("scalar", VE, VE.n)
                p.dma("scalar", PSTX, 3, lambda e: e.dma_start(
                    out=dbg_s[0:1, 4340:4341], in_=smalls[0:1, 0:1]))
            p.wait("gpsimd", VE, VE.n)
            mcc = p.dma("gpsimd", SCC, 0, lambda e: e.dma_start(
                out=cc_in1, in_=smalls[0:1, 0:1]))
            p.wait_m("gpsimd", mcc)
            p.op("gpsimd", lambda e: e.collective_compute(
                "AllReduce", ALU.max, replica_groups=[list(range(NCORES))],
                ins=[cc_in1.opt()], outs=[cc_out1.opt()]), sig=CC)

            if "DBG" in passes:
                p.wait("scalar", VE, VE.n)
                p.dma("scalar", PSTX, 0, lambda e: e.dma_start(
                    out=dbg_s[:, 0:NC * 128], in_=kvblk[:]))
                p.dma("scalar", PSTX, 0, lambda e: e.dma_start(
                    out=dbg_s[:, 1024:1024 + NC], in_=ksum_fm[:]))
                p.dma("scalar", PSTX, 0, lambda e: e.dma_start(
                    out=dbg_s[:, 1040:1041], in_=mxcol[:],
                    allow_slow_non_contiguous=True))
                p.dma("scalar", PSTX, 0, lambda e: e.dma_start(
                    out=dbg_s[0:1, 1050:1066], in_=smalls[:]))
                p.dma("scalar", PSTX, 0, lambda e: e.dma_start(
                    out=dbg_s[:, 1100:1100 + NC * 16], in_=ksum_blk[:]))
            p.drain("sync", PWLD)
            p.drain("sync", PLDX)
            p.drain("scalar", PWR)
            p.drain("scalar", PSTX)
            p.wait("vector", VE, ve_fin)
            _bs = p.barrier(bar_engs)
            p.wait_snap("sync", _bs)

        if "B1" not in passes:
            _finalize(p, bar_engs)
            return nc
        with ExitStack() as bctx:
            def sbb(name, shape, dtype=F32):
                return bctx.enter_context(nc.sbuf_tensor(name, shape, dtype))

            wq_h = sbb("wq_h", [128, NC * D], BF16)
            wq_l = sbb("wq_l", [128, NC * D], BF16)
            wld_b = sbb("wld_b", [128, 2 * D])
            xng = sbb("xng", [128, 2 * NC * TG])
            xnh_b = sbb("xnh_b", [128, NC * TG], BF16)
            xnl_b = sbb("xnl_b", [128, NC * TG], BF16)
            qT_t = sbb("qT_t", [128, NC * TG])
            y_t = sbb("y_t", [128, NC * TG])

            rbc_t = sbb("rbc_t", [128, 2 * TG])
            t2_t = sbb("t2_t", [128, 2 * TG])
            s_t = sbb("s_t", [128, 2 * TG])
            ysq_t = sbb("ysq_t", [128, 2 * TG])
            wraw_d = sbb("wraw_d", [128, 2 * CI])
            dbg2_s = (sbb("dbg2_s", [128, 2 * TG]) if "DBG" in passes else None)

            ps_q = bctx.enter_context(nc.psum_tensor("ps_q", [128, 2 * TG], F32))
            ps_dbc = bctx.enter_context(nc.psum_tensor("ps_dbc", [128, TG], F32))
            ps_ctx = bctx.enter_context(nc.psum_tensor("ps_ctx", [128, TG], F32))
            ps_stb = bctx.enter_context(nc.psum_tensor("ps_stb", [128, 2 * TG], F32))

            wq_marks = []
            for c in range(NC):
                slot = c % 2
                if c >= 2:
                    p.wait("sync", VE, wq_marks[c - 2])
                m = p.dma("sync", PWLD, c, lambda e, c=c, slot=slot:
                          e.dma_start(out=wld_b[:, slot * D:(slot + 1) * D],
                                      in_=wqT_d[128 * c:128 * (c + 1), :]))
                p.wait_m("vector", m)
                p.op("vector", lambda e, c=c, slot=slot: e.tensor_copy(
                    out=wq_h[:, D * c:D * (c + 1)],
                    in_=wld_b[:, slot * D:(slot + 1) * D]))
                wq_marks.append(p.op(
                    "vector", lambda e, c=c, slot=slot: e.tensor_tensor(
                        out=wq_l[:, D * c:D * (c + 1)],
                        in0=wld_b[:, slot * D:(slot + 1) * D],
                        in1=wq_h[:, D * c:D * (c + 1)], op=ALU.subtract),
                    sig=VE))
            ve_wq = VE.n

            wd_marks = []

            def emit_wd_rowmax(j):
                slot = j % 2
                if j >= 2:
                    p.wait("scalar", VE, wd_marks[j - 2])
                m = p.dma("scalar", PWR, j, lambda e, j=j, slot=slot:
                          e.dma_start(
                              out=wraw_d[:, slot * CI:(slot + 1) * CI],
                              in_=wd_raw[128 * j:128 * (j + 1), :]))
                p.wait_m("vector", m)
                wd_marks.append(p.op(
                    "vector", lambda e, j=j, slot=slot: e.reduce_max(
                        out=swd[:, j:j + 1],
                        in_=wraw_d[:, slot * CI:(slot + 1) * CI],
                        axis=AX.X, apply_absolute_value=True), sig=VE))

            bar_marks_b = [0] * (NG + 2)
            stx_snaps_b = [None] * (NG + 2)
            ldb_marks = []
            for g in range(NG):
                slot = g % 2
                gbase = slot * NC * TG
                if g >= 2:
                    p.wait_snap("sync", bar_marks_b[g - 2])
                p.dma("sync", PLDX, g, lambda e, g=g, gbase=gbase: e.dma_start(
                    out=xng[:, gbase:gbase + NC * TG],
                    in_=xn_s[:, NC * TG * g:NC * TG * (g + 1)]))
                if g >= 1:
                    p.wait_snap("sync", bar_marks_b[g - 1])
                p.dma("sync", PLDX, g, lambda e, g=g: e.dma_start(
                    out=xnh_b[:], in_=xnh_s[:, NC * TG * g:NC * TG * (g + 1)]))
                m = p.dma("sync", PLDX, g, lambda e, g=g: e.dma_start(
                    out=xnl_b[:], in_=xnl_s[:, NC * TG * g:NC * TG * (g + 1)]))
                ldb_marks.append(m)
                if g >= 1:
                    p.wait_snap("scalar", stx_snaps_b[g - 1])

                if g < NC:
                    emit_wd_rowmax(g)

                p.wait_m("tensor", ldb_marks[g])
                if g == 0:
                    p.wait("tensor", VE, ve_wq)
                sc_qe = []
                te_qm = TE.n
                for mch in range(NC):
                    qb = (mch % 2) * TG
                    if mch >= 2:
                        p.wait("tensor", SC, sc_qe[mch - 2])
                    first = True
                    for cc in range(NC):
                        for pi, (la, rb) in enumerate(
                                ((xnh_b, wq_h), (xnh_b, wq_l),
                                 (xnl_b, wq_h))):
                            last = (cc == NC - 1 and pi == 2)
                            te_qm = p.op("tensor", lambda e, cc=cc, la=la,
                                          rb=rb, mch=mch, qb=qb, first=first,
                                          last=last: e.matmul(
                                              ps_q[:, qb:qb + TG],
                                              rb[:, D * cc + 128 * mch:D * cc + 128 * (mch + 1)],
                                              la[:, TG * cc:TG * (cc + 1)],
                                              start=first, stop=last),
                                          sig=TE if last else None) or te_qm
                            first = False
                    p.wait("scalar", TE, te_qm)
                    sc_qe.append(p.op("scalar", lambda e, mch=mch, qb=qb:
                                      e.activation(
                                          out=qT_t[:, TG * mch:TG * (mch + 1)],
                                          in_=ps_q[:, qb:qb + TG],
                                          func=AF.Relu,
                                          bias=s_bq[:, mch:mch + 1]), sig=SC))

                p.wait_m("vector", ldb_marks[g])
                ve_y = []
                sc_ysq = []
                te_ysq = []
                for cc in range(NC):
                    cs = cc % 2
                    p.wait("tensor", SC, sc_qe[cc])
                    if cc >= 1:
                        p.wait("tensor", VE, ve_y[cc - 1])
                    te_dbc = p.op("tensor", lambda e, cc=cc: e.matmul(
                        ps_dbc[:], ksum_ex[:, 128 * cc:128 * (cc + 1)],
                        qT_t[:, TG * cc:TG * (cc + 1)], start=True, stop=True),
                        sig=TE)
                    te_ctx = p.op("tensor", lambda e, cc=cc: e.matmul(
                        ps_ctx[:], kvblk[:, 128 * cc:128 * (cc + 1)],
                        qT_t[:, TG * cc:TG * (cc + 1)], start=True, stop=True),
                        sig=TE)
                    p.wait("vector", TE, te_dbc)
                    p.op("vector", lambda e, cs=cs: e.reciprocal(
                        out=rbc_t[:, TG * cs:TG * (cs + 1)], in_=ps_dbc[:]))
                    p.wait("vector", TE, te_ctx)
                    if cc >= 2:
                        p.wait("vector", TE, te_ysq[cc - 2])
                    if "DBG" in passes and g in (0, 1) and cc in (0, 3):
                        _off = 4928 + (g * 2 + (0 if cc == 0 else 1)) * TG
                        p.op("vector", lambda e: e.tensor_copy(
                            out=dbg2_s[:, 0:TG], in_=ps_ctx[:]))
                        p.op("vector", lambda e, cs=cs: e.tensor_copy(
                            out=dbg2_s[:, TG:2 * TG],
                            in_=rbc_t[:, TG * cs:TG * (cs + 1)]))
                        p.fence("vector", VE)
                        p.wait("scalar", VE, VE.n)
                        _m1 = p.dma("scalar", PSTX, 3, lambda e, _off=_off:
                                    e.dma_start(out=dbg_s[:, _off:_off + TG],
                                                in_=dbg2_s[:, 0:TG]))
                        _m2 = p.dma("scalar", PSTX, 3, lambda e, _off=_off:
                                    e.dma_start(out=dbg_s2[:, _off:_off + TG],
                                                in_=dbg2_s[:, TG:2 * TG]))
                        p.wait_m("vector", _m1)
                        p.wait_m("vector", _m2)
                    p.op("vector", lambda e, cs=cs: e.scalar_tensor_tensor(
                        out=ysq_t[:, TG * cs:TG * (cs + 1)],
                        in0=rbc_t[:, TG * cs:TG * (cs + 1)],
                        scalar=1.0 / EPS_DEN, in1=ps_ctx[:],
                        op0=ALU.min, op1=ALU.mult))
                    ve_y.append(p.op(
                        "vector", lambda e, cc=cc, cs=cs, gbase=gbase:
                        e.tensor_tensor(
                            out=y_t[:, TG * cc:TG * (cc + 1)],
                            in0=ysq_t[:, TG * cs:TG * (cs + 1)],
                            in1=xng[:, gbase + TG * cc:gbase + TG * (cc + 1)],
                            op=ALU.add), sig=VE))
                    p.wait("tensor", VE, ve_y[cc])
                    p.op("tensor", lambda e, cc=cc: e.matmul(
                        ps_stb[0:1, 0:TG], ones_col[:],
                        y_t[:, TG * cc:TG * (cc + 1)],
                        start=(cc == 0), stop=(cc == NC - 1)))
                    p.wait("scalar", VE, ve_y[cc])
                    sc_ysq.append(p.op(
                        "scalar", lambda e, cc=cc, cs=cs: e.activation(
                            out=ysq_t[:, TG * cs:TG * (cs + 1)],
                            in_=y_t[:, TG * cc:TG * (cc + 1)],
                            func=AF.Square), sig=SC))
                    p.wait("tensor", SC, sc_ysq[cc])
                    te_ysq.append(p.op(
                        "tensor", lambda e, cc=cc, cs=cs: e.matmul(
                            ps_stb[0:1, TG:2 * TG], ones_col[:],
                            ysq_t[:, TG * cs:TG * (cs + 1)],
                            start=(cc == 0), stop=(cc == NC - 1)), sig=TE))
                te_yst = TE.n
                if "DBG" in passes and g == 1:
                    p.wait("scalar", VE, VE.n)
                    p.dma("scalar", PSTX, 3, lambda e: e.dma_start(
                        out=dbg_s2[:, 4096:4096 + NC * TG // 8],
                        in_=y_t[:, 0:NC * TG // 8]))
                    p.dma("scalar", PSTX, 3, lambda e: e.dma_start(
                        out=dbg_s2[:, 4608:4608 + NC * TG // 8],
                        in_=y_t[:, 7 * TG:7 * TG + NC * TG // 8]))

                p.wait("vector", TE, te_yst)
                p.op("vector", lambda e: e.tensor_scalar(
                    out=row_a[:], in0=ps_stb[0:1, 0:TG], scalar1=-1.0 / D,
                    scalar2=None, op0=ALU.mult, op1=ALU.bypass))
                p.op("vector", lambda e: e.tensor_scalar(
                    out=row_b[:], in0=ps_stb[0:1, TG:2 * TG], scalar1=1.0 / D,
                    scalar2=None, op0=ALU.mult, op1=ALU.bypass))
                p.op("vector", lambda e: e.tensor_tensor(
                    out=row_c[:], in0=row_a[:], in1=row_a[:], op=ALU.mult))
                ve_ysm = p.op("vector", lambda e: e.tensor_tensor(
                    out=row_b[:], in0=row_b[:], in1=row_c[:], op=ALU.subtract),
                    sig=VE)
                p.wait("scalar", VE, ve_ysm)
                sc_ysqrt = p.op("scalar", lambda e: e.activation(
                    out=row_c[:], in_=row_b[:], func=AF.Sqrt,
                    bias=s_eps[0:1, 0:1]), sig=SC)
                p.wait("vector", SC, sc_ysqrt)
                p.op("vector", lambda e: e.reciprocal(
                    out=row_d[:], in_=row_c[:]))
                p.fence("vector", VE)
                p.fence("vector", VE)
                ve_ysm2 = VE.n
                p.wait("tensor", VE, ve_ysm2)
                p.op("tensor", lambda e: e.matmul(
                    ps_stb[0:128, 0:TG], ones_row[:], row_a[:],
                    start=True, stop=True))
                te_yexp = p.op("tensor", lambda e: e.matmul(
                    ps_stb[0:128, TG:2 * TG], ones_row[:], row_d[:],
                    start=True, stop=True), sig=TE)

                p.wait("vector", TE, te_yexp)
                sc_s = []
                sts_marks = []
                for cc in range(NC):
                    cs = cc % 2
                    if cc >= 2:
                        p.wait("vector", SC, sc_s[cc - 2])
                    p.op("vector", lambda e, cc=cc, cs=cs: e.tensor_tensor(
                        out=ysq_t[:, TG * cs:TG * (cs + 1)],
                        in0=y_t[:, TG * cc:TG * (cc + 1)],
                        in1=ps_stb[0:128, 0:TG], op=ALU.add))
                    ve_t2 = p.op("vector", lambda e, cs=cs: e.tensor_tensor(
                        out=t2_t[:, TG * cs:TG * (cs + 1)],
                        in0=ysq_t[:, TG * cs:TG * (cs + 1)],
                        in1=ps_stb[0:128, TG:2 * TG], op=ALU.mult), sig=VE)
                    p.wait("scalar", VE, ve_t2)
                    if cc >= 2:
                        p.wait_m("scalar", sts_marks[cc - 2])
                    sc_s.append(p.op("scalar", lambda e, cc=cc, cs=cs:
                                     e.activation(
                                         out=s_t[:, TG * cs:TG * (cs + 1)],
                                         in_=t2_t[:, TG * cs:TG * (cs + 1)],
                                         func=AF.Identity,
                                         scale=s_gamma[:, cc:cc + 1],
                                         bias=s_beta[:, cc:cc + 1]), sig=SC))
                    col = NC * TG * g + TG * cc
                    sts_marks.append(p.dma(
                        "scalar", PSTX, cc, lambda e, cs=cs, col=col:
                        e.dma_start(out=s_s[:, col:col + TG],
                                    in_=s_t[:, TG * cs:TG * (cs + 1)])))
                stx_snaps_b[g] = p.snap(PSTX)

                bar_marks_b[g] = p.barrier(bar_engs)

            if "DBG" in passes:
                p.wait("scalar", VE, VE.n)
                p.wait("scalar", SC, SC.n)
                p.dma("scalar", PSTX, 1, lambda e: e.dma_start(
                    out=dbg_s[:, 1300:1300 + TG], in_=qT_t[:, 0:TG]))
                p.dma("scalar", PSTX, 1, lambda e: e.dma_start(
                    out=dbg_s[:, 3000:3000 + NC * TG // 8],
                    in_=y_t[:, 0:NC * TG // 8]))
            p.drain("sync", PWLD)
            p.drain("sync", PLDX)
            p.drain("scalar", PWR)
            p.drain("scalar", PSTX)
            _bs = p.barrier(bar_engs)
            p.wait_snap("sync", _bs)

        if "B2" not in passes:
            _finalize(p, bar_engs)
            return nc
        with ExitStack() as cctx:
            def sbc(name, shape, dtype=F32):
                return cctx.enter_context(nc.sbuf_tensor(name, shape, dtype))

            lvl_g = sbc("lvl_g", [128, NC * NH * 128], BF16)
            lvl_u = sbc("lvl_u", [128, NC * NH * 128], BF16)
            rbc_w = sbc("rbc_w", [128, NH * 128])
            rr_t = sbc("rr_t", [1, NH * 128])
            wqt_t = sbc("wqt_t", [128, NH * 128])
            wqi_t = sbc("wqi_t", [128, NH * 128], I32)
            wlv_t = sbc("wlv_t", [128, 2 * NH * 128])
            xng_c = sbc("xng_c", [128, 2 * NC * TG])
            xq_t = sbc("xq_t", [128, NC * TG], BF16)
            xqi_t = sbc("xqi_t", [128, 2 * TG], I32)
            sg_t = sbc("sg_t", [128, 2 * TG])
            up_t = sbc("up_t", [128, 2 * TG])
            hp_t = sbc("hp_t", [128, 4 * TG])

            ps_g = cctx.enter_context(nc.psum_tensor("ps_g", [128, 2 * TG], F32))
            ps_u = cctx.enter_context(nc.psum_tensor("ps_u", [128, 2 * TG], F32))
            ps_b = cctx.enter_context(nc.psum_tensor("ps_b", [128, 128], F32))
            ps_tr = cctx.enter_context(nc.psum_tensor("ps_tr", [1, 128], F32))

            p.op("vector", lambda e: e.memset(mxcol[:], 0.0))

            p.wait("gpsimd", CC, 1)
            mgf = p.dma("gpsimd", SCC, 1, lambda e: e.dma_start(
                out=gmax1[:], in_=cc_out1))
            p.wait_m("vector", mgf)
            p.op("vector", lambda e: e.tensor_tensor(
                out=smalls[0:1, 0:1], in0=s_sig[0:1, 0:1], in1=gmax1[:],
                op=ALU.mult))
            p.fence("vector", VE)
            p.op("vector", lambda e: e.tensor_scalar(
                out=smalls[0:1, 1:2], in0=smalls[0:1, 0:1], scalar1=1.0 / 127.0,
                scalar2=None, op0=ALU.mult, op1=ALU.bypass))
            p.fence("vector", VE)
            p.op("vector", lambda e: e.reciprocal(
                out=smalls[0:1, 2:3], in_=smalls[0:1, 1:2]))
            p.fence("vector", VE)
            p.fence("vector", VE)
            ve_sx = VE.n
            p.wait("tensor", VE, ve_sx)
            p.op("tensor", lambda e: e.matmul(
                ps_b[:, 0:1], ones_row[:], smalls[0:1, 2:3],
                start=True, stop=True))
            te_bc = p.op("tensor", lambda e: e.matmul(
                ps_b[:, 1:2], ones_row[:], smalls[0:1, 1:2],
                start=True, stop=True), sig=TE)
            p.wait("vector", TE, te_bc)
            p.op("vector", lambda e: e.tensor_copy(out=rsx_bc[:],
                                                   in_=ps_b[:, 0:1]))
            p.op("vector", lambda e: e.tensor_copy(out=sx_bc[:],
                                                   in_=ps_b[:, 1:2]))
            p.fence("vector", VE)
            p.op("vector", lambda e: e.tensor_scalar(
                out=dq_g[:], in0=swg[:], scalar1=sx_bc[:, 0:1],
                scalar2=1.0 / 127.0, op0=ALU.mult, op1=ALU.mult))
            ve_dq = p.op("vector", lambda e: e.tensor_scalar(
                out=dq_u[:], in0=swu[:], scalar1=sx_bc[:, 0:1],
                scalar2=1.0 / 127.0, op0=ALU.mult, op1=ALU.mult), sig=VE)
            p.fence("vector", VE)
            if "DBG" in passes:
                p.wait("scalar", VE, VE.n)
                p.dma("scalar", PSTX, 2, lambda e: e.dma_start(
                    out=dbg_s[0:1, 4000:4001], in_=gmax1[:]))
                p.dma("scalar", PSTX, 2, lambda e: e.dma_start(
                    out=dbg_s[:, 4010:4011], in_=rsx_bc[:],
                    allow_slow_non_contiguous=True))
                p.dma("scalar", PSTX, 2, lambda e: e.dma_start(
                    out=dbg_s[:, 4020:4021], in_=sx_bc[:],
                    allow_slow_non_contiguous=True))
                p.dma("scalar", PSTX, 2, lambda e: e.dma_start(
                    out=dbg_s[:, 4030:4030 + NIC], in_=dq_g[:]))
                p.dma("scalar", PSTX, 2, lambda e: e.dma_start(
                    out=dbg_s[:, 4060:4060 + NIC], in_=swg[:]))
                p.dma("scalar", PSTX, 2, lambda e: e.dma_start(
                    out=dbg_s[:, 4090:4090 + NIC], in_=swu[:]))
                p.dma("scalar", PSTX, 2, lambda e: e.dma_start(
                    out=dbg_s[0:1, 4120:4122], in_=s_sig[:]))

            bar_marks_c = [0] * (2 * NG + 4)
            xq_snaps = [None] * (NG + 2)
            hp_snaps = [None] * (2 * NG + 4)
            gidx = 0
            for half in range(2):
                ic0 = half * NH
                for wi, (sw_, lvl, wsrc) in enumerate(
                        ((swg, lvl_g, wgT_d), (swu, lvl_u, wuT_d))):
                    p.op("vector", lambda e, sw_=sw_: e.tensor_scalar(
                        out=wqt_t[:, 0:NIC], in0=sw_[:], scalar1=1.0 / 127.0,
                        scalar2=None, op0=ALU.mult, op1=ALU.bypass))
                    p.fence("vector", VE)
                    p.op("vector", lambda e: e.reciprocal(
                        out=wqt_t[:, NIC:2 * NIC], in_=wqt_t[:, 0:NIC]))
                    p.fence("vector", VE)
                    p.fence("vector", VE)
                    ve_r = VE.n
                    p.wait("tensor", VE, ve_r)
                    for jj in range(NH):
                        j = ic0 + jj
                        te_t = p.op("tensor", lambda e, j=j: e.matmul(
                            ps_tr[:], wqt_t[:, NIC + j:NIC + j + 1],
                            s_ident[:], start=True, stop=True), sig=TE)
                        p.wait("vector", TE, te_t)
                        p.op("vector", lambda e, jj=jj: e.tensor_copy(
                            out=rr_t[0:1, 128 * jj:128 * (jj + 1)],
                            in_=ps_tr[:]), sig=VE)
                        if jj < NH - 1:
                            p.wait("tensor", VE, VE.n)
                    p.fence("vector", VE)
                    p.fence("vector", VE)
                    ve_rr = VE.n
                    p.wait("scalar", VE, VE.n)
                    sc_e = []
                    for jj in range(NH):
                        p.wait("tensor", VE, ve_rr)
                        if jj >= 1:
                            p.wait("tensor", SC, sc_e[jj - 1])
                        te_e = p.op("tensor", lambda e, jj=jj: e.matmul(
                            ps_b[:], ones_row[:],
                            rr_t[0:1, 128 * jj:128 * (jj + 1)],
                            start=True, stop=True), sig=TE)
                        p.wait("scalar", TE, te_e)
                        sc_e.append(p.op(
                            "scalar", lambda e, jj=jj: e.activation(
                                out=rbc_w[:, 128 * jj:128 * (jj + 1)],
                                in_=ps_b[:], func=AF.Identity), sig=SC))
                    sc_exp = SC.n
                    ve_qs_w = []
                    for ck in range(NC):
                        sl = (ck % 2) * NH * 128
                        if ck >= 2:
                            p.wait("sync", VE, ve_qs_w[ck - 2])
                        else:
                            p.wait("sync", VE, VE.n)
                        m = p.dma("sync", SWQ, ck, lambda e, ck=ck, wsrc=wsrc,
                                  ic0=ic0, sl=sl: e.dma_start(
                                      out=wlv_t[:, sl:sl + NH * 128],
                                      in_=wsrc[128 * ck:128 * (ck + 1),
                                               128 * ic0:128 * (ic0 + NH)]))
                        p.wait_m("vector", m)
                        p.wait("vector", SC, sc_exp)
                        p.op("vector", lambda e, sl=sl: e.tensor_tensor(
                            out=wqt_t[:], in0=wlv_t[:, sl:sl + NH * 128],
                            in1=rbc_w[:], op=ALU.mult))
                        p.op("vector", lambda e: e.tensor_copy(
                            out=wqi_t[:], in_=wqt_t[:]))
                        ve_qs_w.append(p.op(
                            "vector", lambda e, lvl=lvl, ck=ck:
                            e.tensor_scalar(
                                out=lvl[:, NH * 128 * ck:NH * 128 * (ck + 1)],
                                in0=wqi_t[:], scalar1=-128, scalar2=127,
                                op0=ALU.max, op1=ALU.min), sig=VE))
                ve_wq_lvls = VE.n

                for g in range(NG):
                    slot = g % 2
                    gbase = slot * NC * TG
                    if gidx >= 2:
                        p.wait_snap("sync", bar_marks_c[gidx - 2])
                    if half == 0:
                        mld = p.dma("sync", PLDX, gidx, lambda e, g=g,
                                    gbase=gbase: e.dma_start(
                                        out=xng_c[:, gbase:gbase + NC * TG],
                                        in_=xn_s[:, NC * TG * g:NC * TG * (g + 1)]))
                    else:
                        if gidx >= 1:
                            p.wait_snap("sync", bar_marks_c[gidx - 1])
                        if xq_snaps[g] is not None:
                            p.wait_snap("sync", xq_snaps[g])
                        mld = p.dma("sync", PLDX, gidx, lambda e, g=g:
                                    e.dma_start(
                                        out=xq_t[:],
                                        in_=xq_s[:, NC * TG * g:NC * TG * (g + 1)]))
                    if gidx >= 1:
                        p.wait_snap("vector", hp_snaps[gidx - 1])

                    if half == 0:
                        p.wait_m("vector", mld)
                        if gidx == 0:
                            p.wait("vector", VE, ve_dq)
                        if g >= 1 and xq_snaps[g - 1] is not None:
                            p.wait_snap("vector", xq_snaps[g - 1])
                        ve_xq = []
                        for cc in range(NC):
                            cs = cc % 2
                            p.op("vector", lambda e, cc=cc, cs=cs,
                                 gbase=gbase: e.tensor_scalar(
                                     out=xqi_t[:, TG * cs:TG * (cs + 1)],
                                     in0=xng_c[:, gbase + TG * cc:gbase + TG * (cc + 1)],
                                     scalar1=rsx_bc[:, 0:1], scalar2=None,
                                     op0=ALU.mult, op1=ALU.bypass))
                            ve_xq.append(p.op(
                                "vector", lambda e, cc=cc, cs=cs:
                                e.tensor_scalar(
                                    out=xq_t[:, TG * cc:TG * (cc + 1)],
                                    in0=xqi_t[:, TG * cs:TG * (cs + 1)],
                                    scalar1=-128, scalar2=127,
                                    op0=ALU.max, op1=ALU.min), sig=VE))
                            col = NC * TG * g + TG * cc
                            p.wait("scalar", VE, ve_xq[cc])
                            p.dma("scalar", PXQ, g, lambda e, cc=cc, col=col:
                                  e.dma_start(
                                      out=xq_s[:, col:col + TG],
                                      in_=xq_t[:, TG * cc:TG * (cc + 1)]))
                        xq_snaps[g] = p.snap(PXQ)
                        ve_xq_all = ve_xq[-1]
                        p.wait("tensor", VE, ve_xq_all)
                    else:
                        p.wait_m("tensor", mld)
                    if gidx == 0 or (half == 1 and g == 0):
                        p.wait("tensor", VE, ve_wq_lvls)

                    sc_ge = []; sc_ue = []; ve_hp = []
                    hp_marks = []
                    for jj in range(NH):
                        ic = ic0 + jj
                        gb = (jj % 2) * TG
                        if jj >= 2:
                            p.wait("tensor", SC, sc_ge[jj - 2])
                            p.wait("tensor", SC, sc_ue[jj - 2])
                        first = True
                        te_g = TE.n
                        for cc in range(NC):
                            te_g = p.op("tensor", lambda e, cc=cc, jj=jj,
                                        gb=gb, first=first, lvl=lvl_g:
                                        e.matmul(
                                            ps_g[:, gb:gb + TG],
                                            lvl[:, NH * 128 * cc + 128 * jj:
                                                NH * 128 * cc + 128 * (jj + 1)],
                                            xq_t[:, TG * cc:TG * (cc + 1)],
                                            start=first, stop=(cc == NC - 1)),
                                        sig=TE if cc == NC - 1 else None) or te_g
                            first = False
                        first = True
                        te_u = TE.n
                        for cc in range(NC):
                            te_u = p.op("tensor", lambda e, cc=cc, jj=jj,
                                        gb=gb, first=first, lvl=lvl_u:
                                        e.matmul(
                                            ps_u[:, gb:gb + TG],
                                            lvl[:, NH * 128 * cc + 128 * jj:
                                                NH * 128 * cc + 128 * (jj + 1)],
                                            xq_t[:, TG * cc:TG * (cc + 1)],
                                            start=first, stop=(cc == NC - 1)),
                                        sig=TE if cc == NC - 1 else None) or te_u
                            first = False
                        p.wait("scalar", TE, te_g)
                        if jj >= 2:
                            p.wait("scalar", VE, ve_hp[jj - 2])
                        sc_ge.append(p.op(
                            "scalar", lambda e, ic=ic, gb=gb: e.activation(
                                out=sg_t[:, gb:gb + TG],
                                in_=ps_g[:, gb:gb + TG],
                                func=AF.Silu, scale=dq_g[:, ic:ic + 1]),
                            sig=SC))
                        p.wait("scalar", TE, te_u)
                        sc_ue.append(p.op(
                            "scalar", lambda e, ic=ic, gb=gb: e.activation(
                                out=up_t[:, gb:gb + TG],
                                in_=ps_u[:, gb:gb + TG],
                                func=AF.Identity, scale=dq_u[:, ic:ic + 1]),
                            sig=SC))
                        hb = (jj % 4) * TG
                        p.wait("vector", SC, sc_ge[jj])
                        p.wait("vector", SC, sc_ue[jj])
                        if jj >= 4:
                            p.wait_m("vector", hp_marks[jj - 4])
                        ve_hp.append(p.op(
                            "vector", lambda e, gb=gb, hb=hb: e.tensor_tensor(
                                out=hp_t[:, hb:hb + TG],
                                in0=sg_t[:, gb:gb + TG],
                                in1=up_t[:, gb:gb + TG], op=ALU.mult),
                            sig=VE))
                        p.op("vector", lambda e, hb=hb: e.reduce_max(
                            out=mxcol2[:], in_=hp_t[:, hb:hb + TG],
                            axis=AX.X, apply_absolute_value=True))
                        p.fence("vector", VE)
                        p.op("vector", lambda e: e.tensor_tensor(
                            out=mxcol[:], in0=mxcol[:], in1=mxcol2[:],
                            op=ALU.max))
                        p.fence("vector", VE)
                        col = NIC * TG * g + TG * ic
                        p.wait("scalar", VE, ve_hp[jj])
                        hp_marks.append(p.dma(
                            "scalar", PHP, jj, lambda e, hb=hb, col=col:
                            e.dma_start(out=hp_s[:, col:col + TG],
                                        in_=hp_t[:, hb:hb + TG])))
                    hp_snaps[gidx] = p.snap(PHP)

                    bar_marks_c[gidx] = p.barrier(bar_engs)
                    gidx += 1

            te_hm = p.op("tensor", lambda e: e.matmul(
                ps_b[0:1, 0:128], mxcol[:], s_ident[:], start=True, stop=True),
                sig=TE)
            p.wait("vector", TE, te_hm)
            p.op("vector", lambda e: e.tensor_copy(
                out=row_a[0:1, 0:128], in_=ps_b[0:1, 0:128]))
            p.fence("vector", VE)
            p.op("vector", lambda e: e.reduce_max(
                out=smalls[0:1, 3:4], in_=row_a[0:1, 0:128], axis=AX.X))
            p.fence("vector", VE)
            p.wait("gpsimd", VE, VE.n)
            mcc2 = p.dma("gpsimd", SCC, 2, lambda e: e.dma_start(
                out=cc_in2, in_=smalls[0:1, 3:4]))
            p.wait_m("gpsimd", mcc2)
            p.op("gpsimd", lambda e: e.collective_compute(
                "AllReduce", ALU.max, replica_groups=[list(range(NCORES))],
                ins=[cc_in2.opt()], outs=[cc_out2.opt()]), sig=CC)

            p.drain("sync", PLDX)
            p.drain("sync", SWQ)
            p.drain("scalar", PXQ)
            p.drain("scalar", PHP)
            p.drain("scalar", PWR)
            _bs = p.barrier(bar_engs)
            p.wait_snap("sync", _bs)

        if "C" not in passes:
            _finalize(p, bar_engs)
            return nc
        with ExitStack() as dctx:
            def sbd(name, shape, dtype=F32):
                return dctx.enter_context(nc.sbuf_tensor(name, shape, dtype))

            lvl_d = sbd("lvl_d", [128, NIC * D], BF16)
            rbc_d = sbd("rbc_d", [128, D])
            rr_d = sbd("rr_d", [1, NC * 128])
            wqt_d = sbd("wqt_d", [128, D])
            wqi_d = sbd("wqi_d", [128, D], I32)
            wlv_d = sbd("wlv_d", [128, 2 * D])
            sgrp = sbd("sgrp", [128, NC * TG])
            xgrp = sbd("xgrp", [128, NC * TG])
            hg_t = sbd("hg_t", [128, 2 * TG])
            hi_t = sbd("hi_t", [128, 2 * TG], I32)
            hl_t = sbd("hl_t", [128, NIC * TG], BF16)
            g_t = sbd("g_t", [128, 2 * TG])
            o_t = sbd("o_t", [128, 4 * TG])

            ps_d = dctx.enter_context(nc.psum_tensor("ps_d", [128, 2 * TG], F32))
            ps_bd = dctx.enter_context(nc.psum_tensor("ps_bd", [128, 128], F32))
            ps_trd = dctx.enter_context(nc.psum_tensor("ps_trd", [1, 128], F32))

            p.op("vector", lambda e: e.tensor_scalar(
                out=wqt_d[:, 0:NC], in0=swd[:], scalar1=1.0 / 127.0,
                scalar2=None, op0=ALU.mult, op1=ALU.bypass))
            p.fence("vector", VE)
            p.op("vector", lambda e: e.reciprocal(
                out=wqt_d[:, NC:2 * NC], in_=wqt_d[:, 0:NC]))
            p.fence("vector", VE)
            p.fence("vector", VE)
            ve_rd = VE.n
            p.wait("tensor", VE, ve_rd)
            for j in range(NC):
                te_td = p.op("tensor", lambda e, j=j: e.matmul(
                    ps_trd[:], wqt_d[:, NC + j:NC + j + 1], s_ident[:],
                    start=True, stop=True), sig=TE)
                p.wait("vector", TE, te_td)
                p.op("vector", lambda e, j=j: e.tensor_copy(
                    out=rr_d[0:1, 128 * j:128 * (j + 1)], in_=ps_trd[:]),
                    sig=VE)
                if j < NC - 1:
                    p.wait("tensor", VE, VE.n)
            p.fence("vector", VE)
            p.fence("vector", VE)
            ve_rrd = VE.n
            sc_e = []
            for j in range(NC):
                p.wait("tensor", VE, ve_rrd)
                if j >= 1:
                    p.wait("tensor", SC, sc_e[j - 1])
                te_e = p.op("tensor", lambda e, j=j: e.matmul(
                    ps_bd[:], ones_row[:], rr_d[0:1, 128 * j:128 * (j + 1)],
                    start=True, stop=True), sig=TE)
                p.wait("scalar", TE, te_e)
                sc_e.append(p.op("scalar", lambda e, j=j: e.activation(
                    out=rbc_d[:, 128 * j:128 * (j + 1)], in_=ps_bd[:],
                    func=AF.Identity), sig=SC))
            sc_expd = SC.n
            ve_qs_d = []
            for ck in range(NIC):
                sl = (ck % 2) * D
                if ck >= 2:
                    p.wait("sync", VE, ve_qs_d[ck - 2])
                elif ck == 0:
                    p.wait("sync", VE, VE.n)
                m = p.dma("sync", SWQ, ck, lambda e, ck=ck, sl=sl:
                          e.dma_start(out=wlv_d[:, sl:sl + D],
                                      in_=wdT_d[128 * ck:128 * (ck + 1), :]))
                p.wait_m("vector", m)
                p.wait("vector", SC, sc_expd)
                p.op("vector", lambda e, sl=sl: e.tensor_tensor(
                    out=wqt_d[:], in0=wlv_d[:, sl:sl + D], in1=rbc_d[:],
                    op=ALU.mult))
                p.op("vector", lambda e: e.tensor_copy(out=wqi_d[:],
                                                       in_=wqt_d[:]))
                ve_qs_d.append(p.op("vector", lambda e, ck=ck: e.tensor_scalar(
                    out=lvl_d[:, D * ck:D * (ck + 1)], in0=wqi_d[:],
                    scalar1=-128, scalar2=127, op0=ALU.max, op1=ALU.min),
                    sig=VE))
            ve_dlvl = VE.n

            p.wait("gpsimd", CC, 2)
            mgf2 = p.dma("gpsimd", SCC, 3, lambda e: e.dma_start(
                out=gmax2[:], in_=cc_out2))
            p.wait_m("vector", mgf2)
            p.op("vector", lambda e: e.tensor_tensor(
                out=smalls[0:1, 4:5], in0=s_sig[0:1, 1:2], in1=gmax2[:],
                op=ALU.mult))
            p.fence("vector", VE)
            p.op("vector", lambda e: e.tensor_scalar(
                out=smalls[0:1, 5:6], in0=smalls[0:1, 4:5],
                scalar1=1.0 / 127.0,
                scalar2=None, op0=ALU.mult, op1=ALU.bypass))
            p.fence("vector", VE)
            p.op("vector", lambda e: e.reciprocal(
                out=smalls[0:1, 6:7], in_=smalls[0:1, 5:6]))
            p.fence("vector", VE)
            p.fence("vector", VE)
            ve_sh = VE.n
            p.wait("tensor", VE, ve_sh)
            p.wait("tensor", SC, sc_expd)
            p.op("tensor", lambda e: e.matmul(
                ps_bd[:, 0:1], ones_row[:], smalls[0:1, 6:7],
                start=True, stop=True))
            te_bc2 = p.op("tensor", lambda e: e.matmul(
                ps_bd[:, 1:2], ones_row[:], smalls[0:1, 5:6],
                start=True, stop=True), sig=TE)
            p.wait("vector", TE, te_bc2)
            p.op("vector", lambda e: e.tensor_copy(out=rsh_bc[:],
                                                   in_=ps_bd[:, 0:1]))
            p.op("vector", lambda e: e.tensor_copy(out=sh_bc[:],
                                                   in_=ps_bd[:, 1:2]))
            p.fence("vector", VE)
            ve_dqd = p.op("vector", lambda e: e.tensor_scalar(
                out=dq_d[:], in0=swd[:], scalar1=sh_bc[:, 0:1],
                scalar2=1.0 / 127.0, op0=ALU.mult, op1=ALU.mult), sig=VE)
            p.fence("vector", VE)


            bar_marks_d = [0] * (NG + 2)
            out_snaps = [None] * (NG + 2)
            for g in range(NG):
                if g >= 1:
                    p.wait_snap("sync", bar_marks_d[g - 1])
                    p.wait_snap("gpsimd", out_snaps[g - 1])
                mlds = p.dma("sync", PLDX, g, lambda e, g=g: e.dma_start(
                    out=sgrp[:], in_=s_s[:, NC * TG * g:NC * TG * (g + 1)]))
                mldx = mlds
                for c in range(NC):
                    mldx = p.dma("sync", PLDX, g, lambda e, g=g, c=c:
                                 e.dma_start(
                                     out=xgrp[:, TG * c:TG * (c + 1)],
                                     in_=xT[128 * c:128 * (c + 1),
                                            TG * g:TG * (g + 1)]))
                ve_hl = []
                for ic in range(NIC):
                    cs = ic % 2
                    col = NIC * TG * g + TG * ic
                    if ic >= 2:
                        p.wait("sync", VE, ve_hl[ic - 2])
                    mh = p.dma("sync", PHPL, ic, lambda e, cs=cs, col=col:
                               e.dma_start(
                                   out=hg_t[:, TG * cs:TG * (cs + 1)],
                                   in_=hp_s[:, col:col + TG]))
                    p.wait_m("vector", mh)
                    if g == 0 and ic == 0:
                        p.wait("vector", VE, ve_dqd)
                    p.op("vector", lambda e, cs=cs: e.tensor_scalar(
                        out=hi_t[:, TG * cs:TG * (cs + 1)],
                        in0=hg_t[:, TG * cs:TG * (cs + 1)],
                        scalar1=rsh_bc[:, 0:1], scalar2=None,
                        op0=ALU.mult, op1=ALU.bypass))
                    ve_hl.append(p.op(
                        "vector", lambda e, ic=ic, cs=cs: e.tensor_scalar(
                            out=hl_t[:, TG * ic:TG * (ic + 1)],
                            in0=hi_t[:, TG * cs:TG * (cs + 1)],
                            scalar1=-128, scalar2=127,
                            op0=ALU.max, op1=ALU.min), sig=VE))
                ve_hl_all = ve_hl[-1]

                p.wait("tensor", VE, ve_hl_all)
                if g == 0:
                    p.wait("tensor", VE, ve_dlvl)
                sc_g = []; ve_o = []; gp_o = []; out_marks = []
                for mch in range(NC):
                    db = (mch % 2) * TG
                    if mch >= 2:
                        p.wait("tensor", SC, sc_g[mch - 2])
                    first = True
                    te_d = TE.n
                    for ic in range(NIC):
                        te_d = p.op("tensor", lambda e, ic=ic, mch=mch, db=db,
                                    first=first: e.matmul(
                                        ps_d[:, db:db + TG],
                                        lvl_d[:, D * ic + 128 * mch:D * ic + 128 * (mch + 1)],
                                        hl_t[:, TG * ic:TG * (ic + 1)],
                                        start=first, stop=(ic == NIC - 1)),
                                    sig=TE if ic == NIC - 1 else None) or te_d
                        first = False
                    p.wait("scalar", TE, te_d)
                    if g == 0 and mch == 0:
                        p.wait("scalar", VE, ve_dqd)
                    if mch >= 2:
                        p.wait("scalar", VE, ve_o[mch - 2])
                    sc_g.append(p.op("scalar", lambda e, mch=mch, db=db:
                                     e.activation(
                                         out=g_t[:, db:db + TG],
                                         in_=ps_d[:, db:db + TG],
                                         func=AF.Silu,
                                         scale=dq_d[:, mch:mch + 1]),
                                     sig=SC))
                    ob = (mch % 2) * 2 * TG
                    p.wait("vector", SC, sc_g[mch])
                    p.wait_m("vector", mlds)
                    if mch >= 2:
                        p.wait("vector", GP, gp_o[mch - 2])
                    ve_o.append(p.op(
                        "vector", lambda e, mch=mch, db=db, ob=ob:
                        e.tensor_tensor(
                            out=o_t[:, ob:ob + TG],
                            in0=sgrp[:, TG * mch:TG * (mch + 1)],
                            in1=g_t[:, db:db + TG], op=ALU.mult), sig=VE))
                    p.wait("gpsimd", VE, ve_o[mch])
                    p.wait_m("gpsimd", mldx)
                    if mch >= 2:
                        p.wait_m("gpsimd", out_marks[mch - 2])
                    gp_o.append(p.op(
                        "gpsimd", lambda e, mch=mch, ob=ob: e.tensor_tensor(
                            out=o_t[:, ob + TG:ob + 2 * TG],
                            in0=o_t[:, ob:ob + TG],
                            in1=xgrp[:, TG * mch:TG * (mch + 1)],
                            op=ALU.add), sig=GP))
                    col = NC * TG * g + TG * mch
                    out_marks.append(p.dma(
                        "gpsimd", POUT, mch, lambda e, ob=ob, col=col:
                        e.dma_start(out=outT[:, col:col + TG],
                                    in_=o_t[:, ob + TG:ob + 2 * TG])))
                out_snaps[g] = p.snap(POUT)

                bar_marks_d[g] = p.barrier(bar_engs)

            p.drain("sync", PLDX)
            p.drain("sync", PHPL)
            p.drain("sync", SWQ)
            p.drain("gpsimd", SCC)
            p.drain("gpsimd", POUT)

        with nc.Block() as block:
            block.sync(lambda e: [f(e) for f in p.items["sync"]])
            block.scalar(lambda e: [f(e) for f in p.items["scalar"]])
            block.vector(lambda e: [f(e) for f in p.items["vector"]])
            block.tensor(lambda e: [f(e) for f in p.items["tensor"]])
            block.gpsimd(lambda e: [f(e) for f in p.items["gpsimd"]])

    return nc


def _prep_inputs(inputs):
    x = np.ascontiguousarray(np.asarray(inputs["L0_attn_x"], dtype=np.float32))
    wq = np.asarray(inputs["L0_attn_Wq"], dtype=np.float32)
    wk = np.asarray(inputs["L0_attn_Wk"], dtype=np.float32)
    wv = np.asarray(inputs["L0_attn_Wv"], dtype=np.float32)
    bq = np.asarray(inputs["L0_attn_bq"], dtype=np.float32)
    bk = np.asarray(inputs["L0_attn_bk"], dtype=np.float32)
    bv = np.asarray(inputs["L0_attn_bv"], dtype=np.float32)
    gam = np.asarray(inputs["L0_attn_gamma"], dtype=np.float32)
    bet = np.asarray(inputs["L0_attn_beta"], dtype=np.float32)
    clip_x = np.asarray(inputs["L0_ffn_clip_x"], dtype=np.float32)
    clip_d = np.asarray(inputs["L0_ffn_clip_down"], dtype=np.float32)
    wg = np.asarray(inputs["L0_ffn_w_gate"], dtype=np.float32)
    wu = np.asarray(inputs["L0_ffn_w_up"], dtype=np.float32)
    wd = np.asarray(inputs["L0_ffn_w_down"], dtype=np.float32)

    ec = np.zeros((16, 1024), dtype=np.float32)
    for c in range(NC):
        ec[2 * c, 128 * c:128 * c + 64] = 1.0
        ec[2 * c + 1, 128 * c + 64:128 * (c + 1)] = 1.0

    shared = {
        "wqT": np.ascontiguousarray(wq.T),
        "wkT": np.ascontiguousarray(wk.T),
        "wvT": np.ascontiguousarray(wv.T),
        "bq_pack": np.ascontiguousarray(bq.reshape(NC, 128).T),
        "bk_row": np.ascontiguousarray(bk.reshape(1, D)),
        "bv_row": np.ascontiguousarray(bv.reshape(1, D)),
        "gamma_pack": np.ascontiguousarray(gam.reshape(NC, 128).T),
        "beta_pack": np.ascontiguousarray(bet.reshape(NC, 128).T),
        "clip2": np.ascontiguousarray(
            np.stack([clip_x[0], clip_d[0]]).reshape(1, 2)),
        "wgT": np.ascontiguousarray(wg.T),
        "wuT": np.ascontiguousarray(wu.T),
        "wg_raw": np.ascontiguousarray(wg),
        "wu_raw": np.ascontiguousarray(wu),
        "wd_raw": np.ascontiguousarray(wd),
        "wdT": np.ascontiguousarray(wd.T),
        "ident": np.eye(128, dtype=np.float32),
        "ec": ec,
        "emask": np.kron(np.eye(2, dtype=np.float32),
                         np.ones((64, 64), np.float32)),
    }
    in_maps = []
    for b in range(NCORES):
        m = dict(shared)
        m["xT"] = np.ascontiguousarray(x[b].T)
        in_maps.append(m)
    return in_maps


def kernel(**inputs):
    in_maps = _prep_inputs(inputs)
    nc = build()
    res = run_bass_kernel_spmd(nc, in_maps, core_ids=list(range(NCORES)))
    out = np.empty((NCORES, NTOK, D), dtype=np.float32)
    for b in range(NCORES):
        r = np.asarray(res.results[b]["outT"])
        r4 = r.reshape(128, NG, NC, TG)
        feat_major = np.transpose(r4, (2, 0, 1, 3)).reshape(D, NTOK)
        out[b] = feat_major.T
    return out


# version 12 (final)
# speedup vs baseline: 1.0677x; 1.0040x over previous
import numpy as np
import ml_dtypes
from contextlib import ExitStack

import concourse.bass as bass
import concourse.mybir as mybir
from concourse.bass_utils import run_bass_kernel_spmd

F32 = mybir.dt.float32
BF16 = mybir.dt.bfloat16
I32 = mybir.dt.int32
AF = mybir.ActivationFunctionType
ALU = mybir.AluOpType
AX = mybir.AxisListType

D = 1024
NTOK = 4096
NG = 8
TG = 512
NC = 8
CI = 2816
NIC = 22
NH = 11
NTILE = 4
EPS_LN = 1e-5
EPS_DEN = 1e-6
NCORES = 8

ENGS = ("sync", "scalar", "vector", "tensor", "gpsimd")


class Sem:
    def __init__(self, h):
        self.h = h
        self.n = 0


class Prog:
    def __init__(self, nc, ctx):
        self.nc = nc
        self.ctx = ctx
        self.items = {e: [] for e in ENGS}

    def sem(self, name):
        return Sem(self.ctx.enter_context(self.nc.semaphore(name)))

    def pool(self, name, k):
        return [self.sem(f"{name}{i}") for i in range(k)]

    def op(self, eng, fn, sig=None):
        if sig is None:
            self.items[eng].append(fn)
            return None
        sig.n += 1
        self.items[eng].append(lambda e, f=fn, s=sig: f(e).then_inc(s.h, 1))
        return sig.n

    def dma(self, eng, pool, idx, fn):
        s = pool[idx % len(pool)]
        s.n += 16
        self.items[eng].append(lambda e, f=fn, ss=s: f(e).then_inc(ss.h, 16))
        return (s, s.n)

    def wait(self, eng, sem, n=None):
        v = sem.n if n is None else n
        if v > 0:
            self.items[eng].append(lambda e, s=sem, vv=v: e.wait_ge(s.h, vv))

    def wait_m(self, eng, mark):
        s, v = mark
        if v > 0:
            self.items[eng].append(lambda e, ss=s, vv=v: e.wait_ge(ss.h, vv))

    def snap(self, pool):
        return [(s, s.n) for s in pool]

    def wait_snap(self, eng, snap):
        for s, v in snap:
            if v > 0:
                self.items[eng].append(
                    lambda e, ss=s, vv=v: e.wait_ge(ss.h, vv))

    def drain(self, eng, pool):
        self.wait_snap(eng, self.snap(pool))

    def fence(self, eng, sem):
        self.op(eng, lambda ee: ee.nop(), sig=sem)
        self.wait(eng, sem, sem.n)

    def barrier(self, engsems):
        snap = []
        for e, s in engsems:
            if e != "tensor":
                self.op(e, lambda ee: ee.nop(), sig=s)
            snap.append((e, s, s.n))
        for e, _, _ in snap:
            for o, s, v in snap:
                if o != e:
                    self.wait(e, s, v)
        return [(s, v) for _, s, v in snap]




def _finalize(p, bar_engs):
    nc = p.nc
    with nc.Block() as block:
        block.sync(lambda e: [f(e) for f in p.items["sync"]])
        block.scalar(lambda e: [f(e) for f in p.items["scalar"]])
        block.vector(lambda e: [f(e) for f in p.items["vector"]])
        block.tensor(lambda e: [f(e) for f in p.items["tensor"]])
        block.gpsimd(lambda e: [f(e) for f in p.items["gpsimd"]])

def build(passes=("A", "B1", "B2", "C")):
    nc = bass.Bass("TRN2", target_bir_lowering=False,
                   detect_race_conditions=False)

    def din(name, shape, dtype=F32):
        return nc.dram_tensor(name, shape, dtype, kind="ExternalInput").ap()

    xT = din("xT", [D, NTOK])
    wqT_d = din("wqT", [D, D]); wkT_d = din("wkT", [D, D]); wvT_d = din("wvT", [D, D])
    bq_pack = din("bq_pack", [128, NC])
    bk_row = din("bk_row", [1, D]); bv_row = din("bv_row", [1, D])
    gamma_pack = din("gamma_pack", [128, NC]); beta_pack = din("beta_pack", [128, NC])
    clip2 = din("clip2", [1, 2])
    wgT_d = din("wgT", [D, CI]); wuT_d = din("wuT", [D, CI])
    wg_raw = din("wg_raw", [CI, D]); wu_raw = din("wu_raw", [CI, D])
    wd_raw = din("wd_raw", [D, CI])
    wdT_d = din("wdT", [CI, D])
    ident_d = din("ident", [128, 128])
    ec_d = din("ec", [16, 1024])
    emask_d = din("emask", [128, 128])
    identb_d = din("ident_bf", [128, 128], BF16)
    ones128f_d = din("ones128_f", [128, 128])
    ones128b_d = din("ones128_b", [128, 128], BF16)
    epscol_d = din("eps_col", [128, 1])

    outT = nc.dram_tensor("outT", [128, NC * NTOK], F32, kind="ExternalOutput").ap()

    _k = ("ExternalOutput" if ("DBG" in passes or "OUT" in passes)
          else "Internal")
    xn_s = nc.dram_tensor("xn_s", [128, NC * NTOK], F32, kind=_k).ap()
    xnh_s = nc.dram_tensor("xnh_s", [128, NC * NTOK], BF16, kind=_k).ap()
    xnl_s = nc.dram_tensor("xnl_s", [128, NC * NTOK], BF16, kind=_k).ap()
    s_s = nc.dram_tensor("s_s", [128, NC * NTOK], F32, kind=_k).ap()
    xq_s = nc.dram_tensor("xq_s", [128, NC * NTOK], BF16, kind=_k).ap()
    hp_s = nc.dram_tensor("hp_s", [128, NIC * NTOK], F32, kind=_k).ap()
    dbg_s = nc.dram_tensor("dbg_s", [128, 8192], F32, kind=_k).ap()
    dbg_s2 = nc.dram_tensor("dbg_s2", [128, 8192], F32, kind=_k).ap()
    rr_dram = nc.dram_tensor("rr_dram", [16, 128], F32, kind="Internal").ap()
    cc_in1 = nc.dram_tensor("cc_in1", [1, 1], F32, kind="Internal").ap()
    cc_out1 = nc.dram_tensor("cc_out1", [1, 1], F32, kind="Internal").ap()
    cc_in2 = nc.dram_tensor("cc_in2", [1, 1], F32, kind="Internal").ap()
    cc_out2 = nc.dram_tensor("cc_out2", [1, 1], F32, kind="Internal").ap()

    with ExitStack() as ctx:
        def sb(name, shape, dtype=F32):
            return ctx.enter_context(nc.sbuf_tensor(name, shape, dtype))

        p = Prog(nc, ctx)

        s_ident = sb("s_ident", [128, 128])
        s_identb = sb("s_identb", [128, 128], BF16)
        s_ones_f = sb("s_ones_f", [128, 128])
        s_ones_b = sb("s_ones_b", [128, 128], BF16)
        s_epsc = sb("s_epsc", [128, 1])
        s_ec = sb("s_ec", [16, 1024])
        ones_col = sb("ones_col", [128, 1])
        ones_col_b = sb("ones_col_b", [128, 1], BF16)
        ones_row = sb("ones_row", [1, 128])
        ones_row_b = sb("ones_row_b", [1, 128], BF16)
        s_bq = sb("s_bq", [128, NC])
        s_gamma = sb("s_gamma", [128, NC])
        s_beta = sb("s_beta", [128, NC])
        s_clip = sb("s_clip", [1, 2])
        s_sig = sb("s_sig", [1, 2])
        s_eps = sb("s_eps", [1, 1])
        smalls = sb("smalls", [1, 16])
        gmax1 = sb("gmax1", [1, 1]); gmax2 = sb("gmax2", [1, 1])
        mxcol = sb("mxcol", [128, 1]); mxcol2 = sb("mxcol2", [128, 1])
        rsx_bc = sb("rsx_bc", [128, 1]); rsh_bc = sb("rsh_bc", [128, 1])
        sx_bc = sb("sx_bc", [128, 1]); sh_bc = sb("sh_bc", [128, 1])
        kvblk = sb("kvblk", [128, NC * 128])
        ksum_fm = sb("ksum_fm", [128, NC])
        ksum_blk = sb("ksum_blk", [128, NC * 16])
        ksum_ex = sb("ksum_ex", [128, NC * 128])
        s_emask = sb("s_emask", [128, 128])
        k_acc = sb("k_acc", [128, D])
        swg = sb("swg", [128, NIC]); swu = sb("swu", [128, NIC])
        swd = sb("swd", [128, NC])
        dq_g = sb("dq_g", [128, NIC]); dq_u = sb("dq_u", [128, NIC])
        dq_d = sb("dq_d", [128, NC])
        row_a = sb("row_a", [1, TG]); row_b = sb("row_b", [1, TG])
        nm_t = sb("nm_t", [128, TG]); va_t = sb("va_t", [128, TG])
        sd_t = sb("sd_t", [128, TG]); rstd_t = sb("rstd_t", [128, TG])
        row_c = sb("row_c", [1, TG]); row_d = sb("row_d", [1, TG])

        if "SEMSHIFT" in passes:
            _dummy = [p.sem(f"dummy{i}") for i in range(10)]
        VE = p.sem("ve"); SC = p.sem("sc"); TE = p.sem("te"); GP = p.sem("gp")
        CC = p.sem("cc")
        lv_guard = [None]
        PCONST = p.pool("pconst", 1)
        PWLD = p.pool("pwld", 2)
        PLDX = p.pool("pldx", 4)
        PWR = p.pool("pwr", 2)
        PSTX = p.pool("pstx", 4)
        PXQ = p.pool("pxq", 2)
        PHP = p.pool("php", 4)
        PHPL = p.pool("phpl", 4)
        SWQ = p.pool("swq", 2)
        POUT = p.pool("pout", 4)
        SCC = p.pool("scc", 4)
        PRR = p.pool("prr", 2)

        bar_engs = (("scalar", SC), ("vector", VE), ("tensor", TE),
                    ("gpsimd", GP))

        for t_, s_ in ((ident_d, s_ident), (identb_d, s_identb),
                       (ones128f_d, s_ones_f), (ones128b_d, s_ones_b),
                       (epscol_d, s_epsc),
                       (ec_d, s_ec), (emask_d, s_emask),
                       (bq_pack, s_bq),
                       (gamma_pack, s_gamma), (beta_pack, s_beta),
                       (clip2, s_clip)):
            mconst = p.dma("sync", PCONST, 0, lambda e, t_=t_, s_=s_:
                           e.dma_start(out=s_[:], in_=t_))

        p.op("vector", lambda e: e.memset(ones_col[:], 1.0))
        p.op("vector", lambda e: e.memset(ones_col_b[:], 1.0))
        p.op("vector", lambda e: e.memset(ones_row[:], 1.0))
        p.op("vector", lambda e: e.memset(ones_row_b[:], 1.0))
        p.op("vector", lambda e: e.memset(s_eps[:], EPS_LN))
        p.op("vector", lambda e: e.memset(mxcol[:], 0.0))
        p.op("vector", lambda e: e.memset(ksum_blk[:], 0.0))
        p.op("vector", lambda e: e.memset(kvblk[:], 0.0))
        p.op("gpsimd", lambda e: e.memset(k_acc[:], 0.0), sig=GP)

        p.wait_m("scalar", mconst)
        p.op("scalar", lambda e: e.activation(out=s_sig[:], in_=s_clip[:],
                                              func=AF.Sigmoid), sig=SC)
        p.wait_m("vector", mconst)
        p.wait_m("tensor", mconst)

        _a_stage = "A"
        for _s in ("A0", "A1", "A2"):
            if _s in passes:
                _a_stage = _s
        _do_groups = _a_stage != "A0"
        _do_kvproj = _a_stage in ("A2", "A")
        _do_kv = _a_stage == "A"
        _ngroups = NG
        for _s in passes:
            if _s.startswith("G"):
                _ngroups = int(_s[1:])
        _no_store = "NOSTORE" in passes
        _no_stats = "NOSTATS" in passes
        _no_rowmax = "NORM" in passes
        with ExitStack() as actx:
            def sba(name, shape, dtype=F32):
                return actx.enter_context(nc.sbuf_tensor(name, shape, dtype))

            s_bkh = sba("s_bkh", [1, D], BF16); s_bkl = sba("s_bkl", [1, D], BF16)
            s_bvh = sba("s_bvh", [1, D], BF16); s_bvl = sba("s_bvl", [1, D], BF16)
            wk_h = sba("wk_h", [128, NC * D], BF16); wk_l = sba("wk_l", [128, NC * D], BF16)
            wv_h = sba("wv_h", [128, NC * D], BF16); wv_l = sba("wv_l", [128, NC * D], BF16)
            wld_t = sba("wld_t", [128, 2 * D])
            wraw_t = sba("wraw_t", [128, 2 * D])
            xg = sba("xg", [128, 2 * NC * TG])
            xh_t = sba("xh_t", [128, 2 * TG], BF16)
            xl_t = sba("xl_t", [128, 2 * TG], BF16)
            xsq_t = sba("xsq_t", [128, 2 * TG])
            xn_t = sba("xn_t", [128, 2 * TG])
            xnh_t = sba("xnh_t", [128, NC * TG], BF16)
            xnl_t = sba("xnl_t", [128, NC * TG], BF16)
            k_tile = sba("k_tile", [128, D]); v_tile = sba("v_tile", [128, D])

            ps_kv = actx.enter_context(nc.psum_tensor("ps_kv", [128, NC * 128], F32))
            ps_k = actx.enter_context(nc.psum_tensor("ps_k", [128, D], F32))
            ps_v = actx.enter_context(nc.psum_tensor("ps_v", [128, D], F32))
            ps_st = actx.enter_context(nc.psum_tensor("ps_st", [128, 2 * TG], F32))

            p.dma("sync", PWLD, 0, lambda e: e.dma_start(
                out=wld_t[0:1, 0:D], in_=bk_row))
            mbias = p.dma("sync", PWLD, 0, lambda e: e.dma_start(
                out=wld_t[0:1, D:2 * D], in_=bv_row))
            p.wait_m("vector", mbias)
            p.op("vector", lambda e: e.tensor_copy(out=s_bkh[:],
                                                   in_=wld_t[0:1, 0:D]))
            p.op("vector", lambda e: e.tensor_tensor(
                out=s_bkl[:], in0=wld_t[0:1, 0:D], in1=s_bkh[:],
                op=ALU.subtract))
            p.op("vector", lambda e: e.tensor_copy(out=s_bvh[:],
                                                   in_=wld_t[0:1, D:2 * D]))
            ve_bias = p.op("vector", lambda e: e.tensor_tensor(
                out=s_bvl[:], in0=wld_t[0:1, D:2 * D], in1=s_bvh[:],
                op=ALU.subtract), sig=VE)
            p.wait("sync", VE, ve_bias)

            wsplit_marks = []
            idx = 0
            for wi, (src, dh, dl) in enumerate(
                    ((wkT_d, wk_h, wk_l), (wvT_d, wv_h, wv_l))):
                for c in range(NC):
                    slot = idx % 2
                    if idx >= 2:
                        p.wait("sync", VE, wsplit_marks[idx - 2])
                    m = p.dma("sync", PWLD, idx, lambda e, src=src, c=c,
                              slot=slot: e.dma_start(
                                  out=wld_t[:, slot * D:(slot + 1) * D],
                                  in_=src[128 * c:128 * (c + 1), :]))
                    p.wait_m("vector", m)
                    p.op("vector", lambda e, dh=dh, c=c, slot=slot:
                         e.tensor_copy(
                             out=dh[:, D * c:D * (c + 1)],
                             in_=wld_t[:, slot * D:(slot + 1) * D]))
                    wsplit_marks.append(p.op(
                        "vector", lambda e, dh=dh, dl=dl, c=c, slot=slot:
                        e.tensor_tensor(
                            out=dl[:, D * c:D * (c + 1)],
                            in0=wld_t[:, slot * D:(slot + 1) * D],
                            in1=dh[:, D * c:D * (c + 1)], op=ALU.subtract),
                        sig=VE))
                    idx += 1

            rowmax_jobs = []
            for wi, (raw, dst) in enumerate(((wg_raw, swg), (wu_raw, swu))):
                for j in range(NIC):
                    rowmax_jobs.append((raw, dst, j))
            ve_rm_marks = []

            def emit_rowmax(job_idx):
                raw, dst, j = rowmax_jobs[job_idx]
                slot = job_idx % 2
                if job_idx >= 2:
                    p.wait("scalar", VE, ve_rm_marks[job_idx - 2])
                m = p.dma("scalar", PWR, job_idx, lambda e, raw=raw, j=j,
                          slot=slot: e.dma_start(
                              out=wraw_t[:, slot * D:(slot + 1) * D],
                              in_=raw[128 * j:128 * (j + 1), :]))
                p.wait_m("vector", m)
                ve_rm_marks.append(p.op(
                    "vector", lambda e, dst=dst, j=j, slot=slot:
                    e.reduce_max(out=dst[:, j:j + 1],
                                 in_=wraw_t[:, slot * D:(slot + 1) * D],
                                 axis=AX.X, apply_absolute_value=True),
                    sig=VE))

            ldx_marks = []
            bar_marks = [0] * (NG + 2)
            stx_snaps = [None] * (NG + 2)
            for g in range(_ngroups if _do_groups else 0):
                slot = g % 2
                gbase = slot * NC * TG
                if g >= 2:
                    p.wait_snap("sync", bar_marks[g - 2])
                for c in range(NC):
                    m = p.dma("sync", PLDX, g, lambda e, g=g, c=c, gbase=gbase:
                              e.dma_start(
                                  out=xg[:, gbase + TG * c:gbase + TG * (c + 1)],
                                  in_=xT[128 * c:128 * (c + 1),
                                         TG * g:TG * (g + 1)]))
                ldx_marks.append(m)
                if g >= 1:
                    p.wait_snap("scalar", stx_snaps[g - 1])
                    p.wait_snap("vector", stx_snaps[g - 1])

                lo = (len(rowmax_jobs) * g) // NG
                hi = (len(rowmax_jobs) * (g + 1)) // NG
                for ji in range(lo, hi):
                    if not _no_rowmax:
                        emit_rowmax(ji)

                p.wait_m("vector", ldx_marks[g])
                p.wait_m("scalar", ldx_marks[g])
                if _no_stats:
                    if "BARNOP" in passes:
                        for _e, _s in bar_engs:
                            p.op(_e, lambda ee: ee.nop())
                    elif "BARINC" in passes:
                        for _e, _s in bar_engs:
                            p.op(_e, lambda ee: ee.nop(), sig=_s)
                    elif any(_x.startswith("BW") for _x in passes):
                        sel = [_x for _x in passes if _x.startswith("BW")][0][2:]
                        snap = []
                        for _e, _s in bar_engs:
                            p.op(_e, lambda ee: ee.nop(), sig=_s)
                            snap.append((_e, _s, _s.n))
                        pairs = {"sv": ("scalar", "vector"),
                                 "st": ("scalar", "tensor"),
                                 "sg": ("scalar", "gpsimd"),
                                 "vs": ("vector", "scalar"),
                                 "vt": ("vector", "tensor"),
                                 "vg": ("vector", "gpsimd"),
                                 "ts": ("tensor", "scalar"),
                                 "tv": ("tensor", "vector"),
                                 "tg": ("tensor", "gpsimd"),
                                 "gs": ("gpsimd", "scalar"),
                                 "gv": ("gpsimd", "vector"),
                                 "gt": ("gpsimd", "tensor")}
                        for code in sel.split("-"):
                            we, oe = pairs[code]
                            for _e, _s, _v in snap:
                                if _e == oe:
                                    p.wait(we, _s, _v)
                    elif "NOBAR" not in passes:
                        bar_marks[g] = p.barrier(bar_engs)
                    stx_snaps[g] = p.snap(PSTX)
                    continue
                ve_sp = []
                sc_sq = []
                te_sum = []
                te_sq = []
                for c in range(NC):
                    cs = c % 2
                    if c >= 2:
                        p.wait("vector", TE, te_sum[c - 2])
                        p.wait("scalar", TE, te_sq[c - 2])
                    p.op("vector", lambda e, c=c, cs=cs, gbase=gbase:
                         e.tensor_copy(
                             out=xh_t[:, TG * cs:TG * (cs + 1)],
                             in_=xg[:, gbase + TG * c:gbase + TG * (c + 1)]))
                    ve_sp.append(p.op(
                        "vector", lambda e, c=c, cs=cs, gbase=gbase:
                        e.tensor_tensor(
                            out=xl_t[:, TG * cs:TG * (cs + 1)],
                            in0=xg[:, gbase + TG * c:gbase + TG * (c + 1)],
                            in1=xh_t[:, TG * cs:TG * (cs + 1)],
                            op=ALU.subtract), sig=VE))
                    sc_sq.append(p.op(
                        "scalar", lambda e, c=c, cs=cs, gbase=gbase:
                        e.activation(
                            out=xsq_t[:, TG * cs:TG * (cs + 1)],
                            in_=xg[:, gbase + TG * c:gbase + TG * (c + 1)],
                            func=AF.Square), sig=SC))
                    p.wait("tensor", VE, ve_sp[c])
                    p.op("tensor", lambda e, c=c, cs=cs: e.matmul(
                        ps_st[0:128, 0:TG], s_ones_b[:],
                        xh_t[:, TG * cs:TG * (cs + 1)],
                        start=(c == 0), stop=False))
                    te_sum.append(p.op("tensor", lambda e, c=c, cs=cs: e.matmul(
                        ps_st[0:128, 0:TG], s_ones_b[:],
                        xl_t[:, TG * cs:TG * (cs + 1)],
                        start=False, stop=(c == NC - 1)), sig=TE))
                    p.wait("tensor", SC, sc_sq[c])
                    te_sq.append(p.op("tensor", lambda e, c=c, cs=cs: e.matmul(
                        ps_st[0:128, TG:2 * TG], s_ones_f[:],
                        xsq_t[:, TG * cs:TG * (cs + 1)],
                        start=(c == 0), stop=(c == NC - 1)), sig=TE))
                te_stats = TE.n

                p.wait("vector", TE, te_stats)
                p.op("vector", lambda e: e.tensor_scalar(
                    out=nm_t[:], in0=ps_st[0:128, 0:TG], scalar1=-1.0 / D,
                    scalar2=None, op0=ALU.mult, op1=ALU.bypass))
                p.op("vector", lambda e: e.tensor_tensor(
                    out=sd_t[:], in0=nm_t[:], in1=nm_t[:], op=ALU.mult))
                ve_sm1 = p.op("vector", lambda e: e.scalar_tensor_tensor(
                    out=va_t[:], in0=ps_st[0:128, TG:2 * TG],
                    scalar=1.0 / D, in1=sd_t[:],
                    op0=ALU.mult, op1=ALU.subtract), sig=VE)
                p.wait("scalar", VE, ve_sm1)
                sc_sqrt = p.op("scalar", lambda e: e.activation(
                    out=sd_t[:], in_=va_t[:], func=AF.Sqrt,
                    bias=s_epsc[:, 0:1]), sig=SC)
                p.wait("vector", SC, sc_sqrt)
                p.op("vector", lambda e: e.reciprocal(
                    out=rstd_t[:], in_=sd_t[:]))

                ve_xn = []
                stx_marks = []
                for c in range(NC):
                    cs = c % 2
                    if c >= 2:
                        p.wait_m("vector", stx_marks[c - 2])
                    p.op("vector", lambda e, c=c, cs=cs, gbase=gbase:
                         e.tensor_tensor(
                             out=xsq_t[:, TG * cs:TG * (cs + 1)],
                             in0=xg[:, gbase + TG * c:gbase + TG * (c + 1)],
                             in1=nm_t[:], op=ALU.add))
                    p.op("vector", lambda e, cs=cs: e.tensor_tensor(
                        out=xn_t[:, TG * cs:TG * (cs + 1)],
                        in0=xsq_t[:, TG * cs:TG * (cs + 1)],
                        in1=rstd_t[:], op=ALU.mult))
                    p.op("vector", lambda e, c=c, cs=cs: e.tensor_copy(
                        out=xnh_t[:, TG * c:TG * (c + 1)],
                        in_=xn_t[:, TG * cs:TG * (cs + 1)]))
                    ve_xn.append(p.op(
                        "vector", lambda e, c=c, cs=cs: e.tensor_tensor(
                            out=xnl_t[:, TG * c:TG * (c + 1)],
                            in0=xn_t[:, TG * cs:TG * (cs + 1)],
                            in1=xnh_t[:, TG * c:TG * (c + 1)],
                            op=ALU.subtract), sig=VE))
                    if "ADBG" in passes and g == 3 and c == 0:
                        p.fence("vector", VE)
                        p.wait("scalar", VE, VE.n)
                        p.dma("scalar", PSTX, 3, lambda e: e.dma_start(
                            out=dbg_s[:, 0:TG], in_=nm_t[:]))
                        p.dma("scalar", PSTX, 3, lambda e: e.dma_start(
                            out=dbg_s[:, TG:2 * TG], in_=rstd_t[:]))
                        p.dma("scalar", PSTX, 3, lambda e: e.dma_start(
                            out=dbg_s[:, 2 * TG:3 * TG], in_=xn_t[:, 0:TG]))
                        _md = p.dma("scalar", PSTX, 3, lambda e: e.dma_start(
                            out=dbg_s[:, 3 * TG:4 * TG],
                            in_=xsq_t[:, 0:TG]))
                        p.wait_m("vector", _md)
                    p.op("vector", lambda e, cs=cs: e.reduce_max(
                        out=mxcol2[:], in_=xn_t[:, TG * cs:TG * (cs + 1)],
                        axis=AX.X, apply_absolute_value=True))
                    p.fence("vector", VE)
                    p.op("vector", lambda e: e.tensor_tensor(
                        out=mxcol[:], in0=mxcol[:], in1=mxcol2[:],
                        op=ALU.max))
                    p.fence("vector", VE)
                    def _emit_store(ca, g=g):
                        csa = ca % 2
                        cola = NC * TG * g + TG * ca
                        p.dma("scalar", PSTX, ca, lambda e, csa=csa,
                              cola=cola: e.dma_start(
                                  out=xn_s[:, cola:cola + TG],
                                  in_=xn_t[:, TG * csa:TG * (csa + 1)]))
                        p.dma("scalar", PSTX, ca, lambda e, ca=ca, cola=cola:
                              e.dma_start(out=xnh_s[:, cola:cola + TG],
                                          in_=xnh_t[:, TG * ca:TG * (ca + 1)]))
                        stx_marks.append(p.dma(
                            "scalar", PSTX, ca, lambda e, ca=ca, cola=cola:
                            e.dma_start(out=xnl_s[:, cola:cola + TG],
                                        in_=xnl_t[:, TG * ca:TG * (ca + 1)])))
                    p.wait("scalar", VE, ve_xn[c])
                    if not _no_store:
                        if c >= 1:
                            _emit_store(c - 1)
                        if c == NC - 1:
                            _emit_store(c)
                    else:
                        stx_marks.append((PSTX[0], 0))
                stx_snaps[g] = p.snap(PSTX)

                p.wait("tensor", VE, ve_xn[NC - 1])
                te_k = []; te_v = []; te_kv = []
                ve_kev = []; sc_vev = []; gp_acc = []
                for t in range(NTILE if _do_kvproj else 0):
                    toff = 128 * t
                    if t >= 1:
                        p.wait("tensor", VE, ve_kev[t - 1])
                    for nh in range(2):
                        first = True
                        for cc in range(NC):
                            for (la, rb) in ((xnh_t, wk_h), (xnh_t, wk_l),
                                             (xnl_t, wk_h)):
                                p.op("tensor", lambda e, cc=cc, la=la, rb=rb,
                                     nh=nh, toff=toff, first=first: e.matmul(
                                         ps_k[:, 512 * nh:512 * (nh + 1)],
                                         la[:, TG * cc + toff:TG * cc + toff + 128],
                                         rb[:, D * cc + 512 * nh:D * cc + 512 * (nh + 1)],
                                         start=first, stop=False))
                                first = False
                        p.op("tensor", lambda e, nh=nh: e.matmul(
                            ps_k[:, 512 * nh:512 * (nh + 1)], ones_row_b[:],
                            s_bkh[0:1, 512 * nh:512 * (nh + 1)],
                            start=False, stop=False))
                        if nh == 1:
                            te_k.append(p.op("tensor", lambda e, nh=nh:
                                             e.matmul(
                                                 ps_k[:, 512 * nh:512 * (nh + 1)],
                                                 ones_row_b[:],
                                                 s_bkl[0:1, 512 * nh:512 * (nh + 1)],
                                                 start=False, stop=True),
                                             sig=TE))
                        else:
                            p.op("tensor", lambda e, nh=nh: e.matmul(
                                ps_k[:, 512 * nh:512 * (nh + 1)], ones_row_b[:],
                                s_bkl[0:1, 512 * nh:512 * (nh + 1)],
                                start=False, stop=True))
                    if t >= 1:
                        p.wait("tensor", SC, sc_vev[t - 1])
                    for nh in range(2):
                        first = True
                        for cc in range(NC):
                            for (la, rb) in ((xnh_t, wv_h), (xnh_t, wv_l),
                                             (xnl_t, wv_h)):
                                p.op("tensor", lambda e, cc=cc, la=la, rb=rb,
                                     nh=nh, toff=toff, first=first: e.matmul(
                                         ps_v[:, 512 * nh:512 * (nh + 1)],
                                         la[:, TG * cc + toff:TG * cc + toff + 128],
                                         rb[:, D * cc + 512 * nh:D * cc + 512 * (nh + 1)],
                                         start=first, stop=False))
                                first = False
                        p.op("tensor", lambda e, nh=nh: e.matmul(
                            ps_v[:, 512 * nh:512 * (nh + 1)], ones_row_b[:],
                            s_bvh[0:1, 512 * nh:512 * (nh + 1)],
                            start=False, stop=False))
                        if nh == 1:
                            te_v.append(p.op("tensor", lambda e, nh=nh:
                                             e.matmul(
                                                 ps_v[:, 512 * nh:512 * (nh + 1)],
                                                 ones_row_b[:],
                                                 s_bvl[0:1, 512 * nh:512 * (nh + 1)],
                                                 start=False, stop=True),
                                             sig=TE))
                        else:
                            p.op("tensor", lambda e, nh=nh: e.matmul(
                                ps_v[:, 512 * nh:512 * (nh + 1)], ones_row_b[:],
                                s_bvl[0:1, 512 * nh:512 * (nh + 1)],
                                start=False, stop=True))

                    p.wait("vector", TE, te_k[t])
                    if te_kv:
                        p.wait("vector", TE, te_kv[-1])
                    if gp_acc:
                        p.wait("vector", GP, gp_acc[-1])
                    ve_kev.append(p.op("vector", lambda e: e.tensor_relu(
                        out=k_tile[:], in_=ps_k[:]), sig=VE))
                    p.wait("scalar", TE, te_v[t])
                    if te_kv:
                        p.wait("scalar", TE, te_kv[-1])
                    sc_vev.append(p.op("scalar", lambda e: e.activation(
                        out=v_tile[:], in_=ps_v[:], func=AF.Identity), sig=SC))

                    if _do_kv:
                        p.wait("tensor", VE, ve_kev[t])
                        p.wait("tensor", SC, sc_vev[t])
                        for cc in range(NC):
                            _sig = TE if cc == NC - 1 else None
                            _mk = p.op("tensor", lambda e, cc=cc, g=g, t=t:
                                       e.matmul(
                                           ps_kv[:, 128 * cc:128 * (cc + 1)],
                                           k_tile[:, 128 * cc:128 * (cc + 1)],
                                           v_tile[:, 128 * cc:128 * (cc + 1)],
                                           start=(g == 0 and t == 0
                                                  and cc % 4 == 0),
                                           stop=(g == NG - 1
                                                 and t == NTILE - 1
                                                 and cc % 4 == 3)), sig=_sig)
                        te_kv.append(_mk)
                        p.wait("gpsimd", VE, ve_kev[t])
                        gp_acc.append(p.op("gpsimd", lambda e: e.tensor_tensor(
                            out=k_acc[:], in0=k_acc[:], in1=k_tile[:],
                            op=ALU.add), sig=GP))

                bar_marks[g] = p.barrier(bar_engs)

            p.wait("tensor", GP, GP.n)
            te_ks = TE.n
            for c in range(NC if _do_kv else 0):
                te_ks = p.op("tensor", lambda e, c=c: e.matmul(
                    ps_st[0:128, c:c + 1], k_acc[:, 128 * c:128 * (c + 1)],
                    ones_col[:], start=(c == 0), stop=(c == NC - 1)),
                    sig=TE if c == NC - 1 else None) or te_ks
            p.wait("vector", TE, te_ks)
            if _do_kv:
                p.op("vector", lambda e: e.tensor_copy(out=ksum_fm[:],
                                                       in_=ps_st[0:128, 0:NC]))
                for c in range(NC):
                    p.op("vector", lambda e, c=c: e.tensor_copy(
                        out=ksum_blk[0:64, 16 * c + 2 * c:16 * c + 2 * c + 1],
                        in_=ps_st[0:64, c:c + 1]))
                    p.op("vector", lambda e, c=c: e.tensor_copy(
                        out=ksum_blk[64:128, 16 * c + 2 * c + 1:16 * c + 2 * c + 2],
                        in_=ps_st[64:128, c:c + 1]))
                for c in range(NC):
                    p.op("vector", lambda e, c=c: e.tensor_copy(
                        out=kvblk[0:64, 128 * c:128 * c + 64],
                        in_=ps_kv[0:64, 128 * c:128 * c + 64]))
                    p.op("vector", lambda e, c=c: e.tensor_copy(
                        out=kvblk[64:128, 128 * c + 64:128 * (c + 1)],
                        in_=ps_kv[64:128, 128 * c + 64:128 * (c + 1)]))
            if _do_kv:
                p.fence("vector", VE)
                for c in range(NC):
                    p.op("vector", lambda e, c=c: e.tensor_scalar(
                        out=ksum_ex[:, 128 * c:128 * (c + 1)], in0=s_emask[:],
                        scalar1=ksum_fm[:, c:c + 1], scalar2=None,
                        op0=ALU.mult, op1=ALU.bypass))
            ve_fin = p.op("vector", lambda e: e.nop(), sig=VE)

            if not _do_groups:
                p.op("vector", lambda e: e.memset(smalls[0:1, 0:1], 1.0))
            te_gm = p.op("tensor", lambda e: e.matmul(
                ps_st[0:1, TG:TG + 128], mxcol[:], s_ident[:],
                start=True, stop=True), sig=TE)
            p.wait("vector", TE, te_gm)
            p.op("vector", lambda e: e.tensor_copy(
                out=row_a[0:1, 0:128], in_=ps_st[0:1, TG:TG + 128]))
            p.fence("vector", VE)
            if "DBG" in passes:
                p.dma("scalar", PSTX, 3, lambda e: e.dma_start(
                    out=dbg_s[0:1, 4200:4328], in_=row_a[0:1, 0:128]))
                p.dma("scalar", PSTX, 3, lambda e: e.dma_start(
                    out=dbg_s[:, 4330:4331], in_=mxcol[:],
                    allow_slow_non_contiguous=True))
            p.op("vector", lambda e: e.reduce_max(
                out=smalls[0:1, 0:1], in_=row_a[0:1, 0:128], axis=AX.X))
            ve_gm = p.fence("vector", VE) or VE.n
            if "DBG" in passes:
                p.wait("scalar", VE, VE.n)
                p.dma("scalar", PSTX, 3, lambda e: e.dma_start(
                    out=dbg_s[0:1, 4340:4341], in_=smalls[0:1, 0:1]))
            p.wait("gpsimd", VE, VE.n)
            mcc = p.dma("gpsimd", SCC, 0, lambda e: e.dma_start(
                out=cc_in1, in_=smalls[0:1, 0:1]))
            p.wait_m("gpsimd", mcc)
            p.op("gpsimd", lambda e: e.collective_compute(
                "AllReduce", ALU.max, replica_groups=[list(range(NCORES))],
                ins=[cc_in1.opt()], outs=[cc_out1.opt()]), sig=CC)

            if "DBG" in passes:
                p.wait("scalar", VE, VE.n)
                p.dma("scalar", PSTX, 0, lambda e: e.dma_start(
                    out=dbg_s[:, 0:NC * 128], in_=kvblk[:]))
                p.dma("scalar", PSTX, 0, lambda e: e.dma_start(
                    out=dbg_s[:, 1024:1024 + NC], in_=ksum_fm[:]))
                p.dma("scalar", PSTX, 0, lambda e: e.dma_start(
                    out=dbg_s[:, 1040:1041], in_=mxcol[:],
                    allow_slow_non_contiguous=True))
                p.dma("scalar", PSTX, 0, lambda e: e.dma_start(
                    out=dbg_s[0:1, 1050:1066], in_=smalls[:]))
                p.dma("scalar", PSTX, 0, lambda e: e.dma_start(
                    out=dbg_s[:, 1100:1100 + NC * 16], in_=ksum_blk[:]))
            p.drain("sync", PWLD)
            p.drain("sync", PLDX)
            p.drain("scalar", PWR)
            p.drain("scalar", PSTX)
            p.wait("vector", VE, ve_fin)
            _bs = p.barrier(bar_engs)
            p.wait_snap("sync", _bs)

        if "B1" not in passes:
            _finalize(p, bar_engs)
            return nc
        with ExitStack() as bctx:
            def sbb(name, shape, dtype=F32):
                return bctx.enter_context(nc.sbuf_tensor(name, shape, dtype))

            wq_h = sbb("wq_h", [128, NC * D], BF16)
            wq_l = sbb("wq_l", [128, NC * D], BF16)
            wld_b = sbb("wld_b", [128, 2 * D])
            xng = sbb("xng", [128, 2 * NC * TG])
            xnh_b = sbb("xnh_b", [128, NC * TG], BF16)
            xnl_b = sbb("xnl_b", [128, NC * TG], BF16)
            qT_t = sbb("qT_t", [128, NC * TG])
            y_t = sbb("y_t", [128, NC * TG])

            rbc_t = sbb("rbc_t", [128, 2 * TG])
            t2_t = sbb("t2_t", [128, 2 * TG])
            s_t = sbb("s_t", [128, 2 * TG])
            ysq_t = sbb("ysq_t", [128, 2 * TG])
            wraw_d = sbb("wraw_d", [128, 2 * CI])
            dbg2_s = (sbb("dbg2_s", [128, 2 * TG]) if "DBG" in passes else None)

            ps_q = bctx.enter_context(nc.psum_tensor("ps_q", [128, 2 * TG], F32))
            ps_dbc = bctx.enter_context(nc.psum_tensor("ps_dbc", [128, TG], F32))
            ps_ctx = bctx.enter_context(nc.psum_tensor("ps_ctx", [128, TG], F32))
            ps_stb = bctx.enter_context(nc.psum_tensor("ps_stb", [128, 2 * TG], F32))

            wq_marks = []
            for c in range(NC):
                slot = c % 2
                if c >= 2:
                    p.wait("sync", VE, wq_marks[c - 2])
                m = p.dma("sync", PWLD, c, lambda e, c=c, slot=slot:
                          e.dma_start(out=wld_b[:, slot * D:(slot + 1) * D],
                                      in_=wqT_d[128 * c:128 * (c + 1), :]))
                p.wait_m("vector", m)
                p.op("vector", lambda e, c=c, slot=slot: e.tensor_copy(
                    out=wq_h[:, D * c:D * (c + 1)],
                    in_=wld_b[:, slot * D:(slot + 1) * D]))
                wq_marks.append(p.op(
                    "vector", lambda e, c=c, slot=slot: e.tensor_tensor(
                        out=wq_l[:, D * c:D * (c + 1)],
                        in0=wld_b[:, slot * D:(slot + 1) * D],
                        in1=wq_h[:, D * c:D * (c + 1)], op=ALU.subtract),
                    sig=VE))
            ve_wq = VE.n

            wd_marks = []

            def emit_wd_rowmax(j):
                slot = j % 2
                if j >= 2:
                    p.wait("scalar", VE, wd_marks[j - 2])
                m = p.dma("scalar", PWR, j, lambda e, j=j, slot=slot:
                          e.dma_start(
                              out=wraw_d[:, slot * CI:(slot + 1) * CI],
                              in_=wd_raw[128 * j:128 * (j + 1), :]))
                p.wait_m("vector", m)
                wd_marks.append(p.op(
                    "vector", lambda e, j=j, slot=slot: e.reduce_max(
                        out=swd[:, j:j + 1],
                        in_=wraw_d[:, slot * CI:(slot + 1) * CI],
                        axis=AX.X, apply_absolute_value=True), sig=VE))

            bar_marks_b = [0] * (NG + 2)
            stx_snaps_b = [None] * (NG + 2)
            ldb_marks = []
            for g in range(NG):
                slot = g % 2
                gbase = slot * NC * TG
                if g >= 2:
                    p.wait_snap("sync", bar_marks_b[g - 2])
                p.dma("sync", PLDX, g, lambda e, g=g, gbase=gbase: e.dma_start(
                    out=xng[:, gbase:gbase + NC * TG],
                    in_=xn_s[:, NC * TG * g:NC * TG * (g + 1)]))
                if g >= 1:
                    p.wait_snap("sync", bar_marks_b[g - 1])
                p.dma("sync", PLDX, g, lambda e, g=g: e.dma_start(
                    out=xnh_b[:], in_=xnh_s[:, NC * TG * g:NC * TG * (g + 1)]))
                m = p.dma("sync", PLDX, g, lambda e, g=g: e.dma_start(
                    out=xnl_b[:], in_=xnl_s[:, NC * TG * g:NC * TG * (g + 1)]))
                ldb_marks.append(m)
                if g >= 1:
                    p.wait_snap("scalar", stx_snaps_b[g - 1])

                if g < NC:
                    emit_wd_rowmax(g)

                p.wait_m("tensor", ldb_marks[g])
                if g == 0:
                    p.wait("tensor", VE, ve_wq)
                sc_qe = []
                te_qm = TE.n
                for mch in range(NC):
                    qb = (mch % 2) * TG
                    if mch >= 2:
                        p.wait("tensor", SC, sc_qe[mch - 2])
                    first = True
                    for cc in range(NC):
                        for pi, (la, rb) in enumerate(
                                ((xnh_b, wq_h), (xnh_b, wq_l),
                                 (xnl_b, wq_h))):
                            last = (cc == NC - 1 and pi == 2)
                            te_qm = p.op("tensor", lambda e, cc=cc, la=la,
                                          rb=rb, mch=mch, qb=qb, first=first,
                                          last=last: e.matmul(
                                              ps_q[:, qb:qb + TG],
                                              rb[:, D * cc + 128 * mch:D * cc + 128 * (mch + 1)],
                                              la[:, TG * cc:TG * (cc + 1)],
                                              start=first, stop=last),
                                          sig=TE if last else None) or te_qm
                            first = False
                    p.wait("scalar", TE, te_qm)
                    sc_qe.append(p.op("scalar", lambda e, mch=mch, qb=qb:
                                      e.activation(
                                          out=qT_t[:, TG * mch:TG * (mch + 1)],
                                          in_=ps_q[:, qb:qb + TG],
                                          func=AF.Relu,
                                          bias=s_bq[:, mch:mch + 1]), sig=SC))

                p.wait_m("vector", ldb_marks[g])
                ve_y = []
                sc_ysq = []
                te_ysq = []
                for cc in range(NC):
                    cs = cc % 2
                    p.wait("tensor", SC, sc_qe[cc])
                    if cc >= 1:
                        p.wait("tensor", VE, ve_y[cc - 1])
                    te_dbc = p.op("tensor", lambda e, cc=cc: e.matmul(
                        ps_dbc[:], ksum_ex[:, 128 * cc:128 * (cc + 1)],
                        qT_t[:, TG * cc:TG * (cc + 1)], start=True, stop=True),
                        sig=TE)
                    te_ctx = p.op("tensor", lambda e, cc=cc: e.matmul(
                        ps_ctx[:], kvblk[:, 128 * cc:128 * (cc + 1)],
                        qT_t[:, TG * cc:TG * (cc + 1)], start=True, stop=True),
                        sig=TE)
                    p.wait("vector", TE, te_dbc)
                    p.op("vector", lambda e, cs=cs: e.reciprocal(
                        out=rbc_t[:, TG * cs:TG * (cs + 1)], in_=ps_dbc[:]))
                    p.wait("vector", TE, te_ctx)
                    if cc >= 2:
                        p.wait("vector", TE, te_ysq[cc - 2])
                    if "DBG" in passes and g in (0, 1) and cc in (0, 3):
                        _off = 4928 + (g * 2 + (0 if cc == 0 else 1)) * TG
                        p.op("vector", lambda e: e.tensor_copy(
                            out=dbg2_s[:, 0:TG], in_=ps_ctx[:]))
                        p.op("vector", lambda e, cs=cs: e.tensor_copy(
                            out=dbg2_s[:, TG:2 * TG],
                            in_=rbc_t[:, TG * cs:TG * (cs + 1)]))
                        p.fence("vector", VE)
                        p.wait("scalar", VE, VE.n)
                        _m1 = p.dma("scalar", PSTX, 3, lambda e, _off=_off:
                                    e.dma_start(out=dbg_s[:, _off:_off + TG],
                                                in_=dbg2_s[:, 0:TG]))
                        _m2 = p.dma("scalar", PSTX, 3, lambda e, _off=_off:
                                    e.dma_start(out=dbg_s2[:, _off:_off + TG],
                                                in_=dbg2_s[:, TG:2 * TG]))
                        p.wait_m("vector", _m1)
                        p.wait_m("vector", _m2)
                    p.op("vector", lambda e, cs=cs: e.scalar_tensor_tensor(
                        out=ysq_t[:, TG * cs:TG * (cs + 1)],
                        in0=rbc_t[:, TG * cs:TG * (cs + 1)],
                        scalar=1.0 / EPS_DEN, in1=ps_ctx[:],
                        op0=ALU.min, op1=ALU.mult))
                    ve_y.append(p.op(
                        "vector", lambda e, cc=cc, cs=cs, gbase=gbase:
                        e.tensor_tensor(
                            out=y_t[:, TG * cc:TG * (cc + 1)],
                            in0=ysq_t[:, TG * cs:TG * (cs + 1)],
                            in1=xng[:, gbase + TG * cc:gbase + TG * (cc + 1)],
                            op=ALU.add), sig=VE))
                    p.wait("tensor", VE, ve_y[cc])
                    p.op("tensor", lambda e, cc=cc: e.matmul(
                        ps_stb[0:128, 0:TG], s_ones_f[:],
                        y_t[:, TG * cc:TG * (cc + 1)],
                        start=(cc == 0), stop=(cc == NC - 1)))
                    p.wait("scalar", VE, ve_y[cc])
                    sc_ysq.append(p.op(
                        "scalar", lambda e, cc=cc, cs=cs: e.activation(
                            out=ysq_t[:, TG * cs:TG * (cs + 1)],
                            in_=y_t[:, TG * cc:TG * (cc + 1)],
                            func=AF.Square), sig=SC))
                    p.wait("tensor", SC, sc_ysq[cc])
                    te_ysq.append(p.op(
                        "tensor", lambda e, cc=cc, cs=cs: e.matmul(
                            ps_stb[0:128, TG:2 * TG], s_ones_f[:],
                            ysq_t[:, TG * cs:TG * (cs + 1)],
                            start=(cc == 0), stop=(cc == NC - 1)), sig=TE))
                te_yst = TE.n
                if "DBG" in passes and g == 1:
                    p.wait("scalar", VE, VE.n)
                    p.dma("scalar", PSTX, 3, lambda e: e.dma_start(
                        out=dbg_s2[:, 4096:4096 + NC * TG // 8],
                        in_=y_t[:, 0:NC * TG // 8]))
                    p.dma("scalar", PSTX, 3, lambda e: e.dma_start(
                        out=dbg_s2[:, 4608:4608 + NC * TG // 8],
                        in_=y_t[:, 7 * TG:7 * TG + NC * TG // 8]))

                p.wait("vector", TE, te_yst)
                p.op("vector", lambda e: e.tensor_scalar(
                    out=nm_t[:], in0=ps_stb[0:128, 0:TG], scalar1=-1.0 / D,
                    scalar2=None, op0=ALU.mult, op1=ALU.bypass))
                p.op("vector", lambda e: e.tensor_tensor(
                    out=sd_t[:], in0=nm_t[:], in1=nm_t[:], op=ALU.mult))
                ve_ysm = p.op("vector", lambda e: e.scalar_tensor_tensor(
                    out=va_t[:], in0=ps_stb[0:128, TG:2 * TG],
                    scalar=1.0 / D, in1=sd_t[:],
                    op0=ALU.mult, op1=ALU.subtract), sig=VE)
                p.wait("scalar", VE, ve_ysm)
                sc_ysqrt = p.op("scalar", lambda e: e.activation(
                    out=sd_t[:], in_=va_t[:], func=AF.Sqrt,
                    bias=s_epsc[:, 0:1]), sig=SC)
                p.wait("vector", SC, sc_ysqrt)
                p.op("vector", lambda e: e.reciprocal(
                    out=rstd_t[:], in_=sd_t[:]))

                sc_s = []
                sts_marks = []
                for cc in range(NC):
                    cs = cc % 2
                    if cc >= 2:
                        p.wait("vector", SC, sc_s[cc - 2])
                    p.op("vector", lambda e, cc=cc, cs=cs: e.tensor_tensor(
                        out=ysq_t[:, TG * cs:TG * (cs + 1)],
                        in0=y_t[:, TG * cc:TG * (cc + 1)],
                        in1=nm_t[:], op=ALU.add))
                    ve_t2 = p.op("vector", lambda e, cs=cs: e.tensor_tensor(
                        out=t2_t[:, TG * cs:TG * (cs + 1)],
                        in0=ysq_t[:, TG * cs:TG * (cs + 1)],
                        in1=rstd_t[:], op=ALU.mult), sig=VE)
                    p.wait("scalar", VE, ve_t2)
                    if cc >= 2:
                        p.wait_m("scalar", sts_marks[cc - 2])
                    sc_s.append(p.op("scalar", lambda e, cc=cc, cs=cs:
                                     e.activation(
                                         out=s_t[:, TG * cs:TG * (cs + 1)],
                                         in_=t2_t[:, TG * cs:TG * (cs + 1)],
                                         func=AF.Identity,
                                         scale=s_gamma[:, cc:cc + 1],
                                         bias=s_beta[:, cc:cc + 1]), sig=SC))
                    col = NC * TG * g + TG * cc
                    sts_marks.append(p.dma(
                        "scalar", PSTX, cc, lambda e, cs=cs, col=col:
                        e.dma_start(out=s_s[:, col:col + TG],
                                    in_=s_t[:, TG * cs:TG * (cs + 1)])))
                stx_snaps_b[g] = p.snap(PSTX)

                bar_marks_b[g] = p.barrier(bar_engs)

            if "DBG" in passes:
                p.wait("scalar", VE, VE.n)
                p.wait("scalar", SC, SC.n)
                p.dma("scalar", PSTX, 1, lambda e: e.dma_start(
                    out=dbg_s[:, 1300:1300 + TG], in_=qT_t[:, 0:TG]))
                p.dma("scalar", PSTX, 1, lambda e: e.dma_start(
                    out=dbg_s[:, 3000:3000 + NC * TG // 8],
                    in_=y_t[:, 0:NC * TG // 8]))
            p.drain("sync", PWLD)
            p.drain("sync", PLDX)
            p.drain("scalar", PWR)
            p.drain("scalar", PSTX)
            _bs = p.barrier(bar_engs)
            p.wait_snap("sync", _bs)

        if "B2" not in passes:
            _finalize(p, bar_engs)
            return nc
        with ExitStack() as cctx:
            def sbc(name, shape, dtype=F32):
                return cctx.enter_context(nc.sbuf_tensor(name, shape, dtype))

            lvl_g = sbc("lvl_g", [128, NC * NH * 128], BF16)
            lvl_u = sbc("lvl_u", [128, NC * NH * 128], BF16)
            rq_gu = sbc("rq_gu", [128, 4 * NIC])
            wq32_t = sbc("wq32_t", [128, D], I32)
            lvT_t = sbc("lvT_t", [128, 2 * D], BF16)
            wlv_t = sbc("wlv_t", [128, 2 * D])
            xng_c = sbc("xng_c", [128, 2 * NC * TG])
            xq_t = sbc("xq_t", [128, NC * TG], BF16)
            xqi_t = sbc("xqi_t", [128, 2 * TG], I32)
            sg_t = sbc("sg_t", [128, 2 * TG])
            up_t = sbc("up_t", [128, 2 * TG])
            hp_t = sbc("hp_t", [128, 4 * TG])

            ps_g = cctx.enter_context(nc.psum_tensor("ps_g", [128, 2 * TG], F32))
            ps_u = cctx.enter_context(nc.psum_tensor("ps_u", [128, 2 * TG], F32))
            ps_b = cctx.enter_context(nc.psum_tensor("ps_b", [128, 128], F32))
            ps_wt = cctx.enter_context(nc.psum_tensor("ps_wt", [128, 1024], F32))

            p.op("vector", lambda e: e.memset(mxcol[:], 0.0))

            p.wait("gpsimd", CC, 1)
            mgf = p.dma("gpsimd", SCC, 1, lambda e: e.dma_start(
                out=gmax1[:], in_=cc_out1))
            p.wait_m("vector", mgf)
            p.op("vector", lambda e: e.tensor_tensor(
                out=smalls[0:1, 0:1], in0=s_sig[0:1, 0:1], in1=gmax1[:],
                op=ALU.mult))
            p.fence("vector", VE)
            p.op("vector", lambda e: e.tensor_scalar(
                out=smalls[0:1, 1:2], in0=smalls[0:1, 0:1], scalar1=1.0 / 127.0,
                scalar2=None, op0=ALU.mult, op1=ALU.bypass))
            p.fence("vector", VE)
            p.op("vector", lambda e: e.reciprocal(
                out=smalls[0:1, 2:3], in_=smalls[0:1, 1:2]))
            p.fence("vector", VE)
            p.fence("vector", VE)
            ve_sx = VE.n
            p.wait("tensor", VE, ve_sx)
            p.op("tensor", lambda e: e.matmul(
                ps_b[:, 0:1], ones_row[:], smalls[0:1, 2:3],
                start=True, stop=True))
            te_bc = p.op("tensor", lambda e: e.matmul(
                ps_b[:, 1:2], ones_row[:], smalls[0:1, 1:2],
                start=True, stop=True), sig=TE)
            p.wait("vector", TE, te_bc)
            p.op("vector", lambda e: e.tensor_copy(out=rsx_bc[:],
                                                   in_=ps_b[:, 0:1]))
            p.op("vector", lambda e: e.tensor_copy(out=sx_bc[:],
                                                   in_=ps_b[:, 1:2]))
            p.fence("vector", VE)
            p.op("vector", lambda e: e.tensor_scalar(
                out=dq_g[:], in0=swg[:], scalar1=sx_bc[:, 0:1],
                scalar2=1.0 / 127.0, op0=ALU.mult, op1=ALU.mult))
            ve_dq = p.op("vector", lambda e: e.tensor_scalar(
                out=dq_u[:], in0=swu[:], scalar1=sx_bc[:, 0:1],
                scalar2=1.0 / 127.0, op0=ALU.mult, op1=ALU.mult), sig=VE)
            p.fence("vector", VE)
            if "DBG" in passes:
                p.wait("scalar", VE, VE.n)
                p.dma("scalar", PSTX, 2, lambda e: e.dma_start(
                    out=dbg_s[0:1, 4000:4001], in_=gmax1[:]))
                p.dma("scalar", PSTX, 2, lambda e: e.dma_start(
                    out=dbg_s[:, 4010:4011], in_=rsx_bc[:],
                    allow_slow_non_contiguous=True))
                p.dma("scalar", PSTX, 2, lambda e: e.dma_start(
                    out=dbg_s[:, 4020:4021], in_=sx_bc[:],
                    allow_slow_non_contiguous=True))
                p.dma("scalar", PSTX, 2, lambda e: e.dma_start(
                    out=dbg_s[:, 4030:4030 + NIC], in_=dq_g[:]))
                p.dma("scalar", PSTX, 2, lambda e: e.dma_start(
                    out=dbg_s[:, 4060:4060 + NIC], in_=swg[:]))
                p.dma("scalar", PSTX, 2, lambda e: e.dma_start(
                    out=dbg_s[:, 4090:4090 + NIC], in_=swu[:]))
                p.dma("scalar", PSTX, 2, lambda e: e.dma_start(
                    out=dbg_s[0:1, 4120:4122], in_=s_sig[:]))

            bar_marks_c = [0] * (2 * NG + 4)
            xq_snaps = [None] * (NG + 2)
            hp_snaps = [None] * (2 * NG + 4)
            gidx = 0
            for half in range(2):
                ic0 = half * NH
                if half == 0:
                    p.op("vector", lambda e: e.tensor_scalar(
                        out=rq_gu[:, 2 * NIC:3 * NIC], in0=swg[:],
                        scalar1=1.0 / 127.0, scalar2=None,
                        op0=ALU.mult, op1=ALU.bypass))
                    p.op("vector", lambda e: e.tensor_scalar(
                        out=rq_gu[:, 3 * NIC:4 * NIC], in0=swu[:],
                        scalar1=1.0 / 127.0, scalar2=None,
                        op0=ALU.mult, op1=ALU.bypass))
                    p.fence("vector", VE)
                    p.op("vector", lambda e: e.reciprocal(
                        out=rq_gu[:, 0:NIC], in_=rq_gu[:, 2 * NIC:3 * NIC]))
                    p.op("vector", lambda e: e.reciprocal(
                        out=rq_gu[:, NIC:2 * NIC], in_=rq_gu[:, 3 * NIC:4 * NIC]))
                    p.fence("vector", VE)
                for wi, (rqc, lvl, wraw) in enumerate(
                        ((0, lvl_g, wg_raw), (NIC, lvl_u, wu_raw))):
                    ve_int = []
                    ve_lvs = []
                    te_trm = []
                    sc_ev = []
                    p.wait("tensor", SC, SC.n)
                    for jj in range(NH + 1):
                        if jj < NH:
                            q = ic0 + jj
                            sl = (jj % 2) * D
                            if jj >= 2:
                                p.wait("sync", VE, ve_int[jj - 2])
                            else:
                                p.wait("sync", VE, VE.n)
                            m = p.dma("sync", SWQ, jj, lambda e, q=q, sl=sl,
                                      wraw=wraw: e.dma_start(
                                          out=wlv_t[:, sl:sl + D],
                                          in_=wraw[128 * q:128 * (q + 1), :]))
                            p.wait_m("vector", m)
                            if jj >= 2:
                                p.wait_m("vector", te_trm[jj - 2])
                            elif jj == 0 and lv_guard[0] is not None:
                                p.wait_m("vector", lv_guard[0])
                            ve_int.append(p.op(
                                "vector", lambda e, q=q, sl=sl, rqc=rqc:
                                e.tensor_scalar(
                                    out=wq32_t[:], in0=wlv_t[:, sl:sl + D],
                                    scalar1=rq_gu[:, rqc + q:rqc + q + 1],
                                    scalar2=None, op0=ALU.mult,
                                    op1=ALU.bypass), sig=VE))
                            ve_lvs.append(p.op(
                                "vector", lambda e, sl=sl: e.tensor_scalar(
                                    out=lvT_t[:, sl:sl + D], in0=wq32_t[:],
                                    scalar1=-128, scalar2=127, op0=ALU.max,
                                    op1=ALU.min), sig=VE))
                        if jj < 1:
                            continue
                        j2 = jj - 1
                        sl2 = (j2 % 2) * D
                        p.wait("tensor", VE, ve_lvs[j2])
                        te_last = None
                        for dc in range(NC):
                            s4 = ((j2 * NC + dc) % 2) * 512
                            if len(sc_ev) >= 2:
                                p.wait("tensor", SC, sc_ev[-2])
                            te_last = p.op(
                                "tensor", lambda e, sl2=sl2, dc=dc, s4=s4:
                                e.matmul(
                                    ps_wt[:, s4:s4 + 128],
                                    lvT_t[:, sl2 + 128 * dc:
                                           sl2 + 128 * (dc + 1)],
                                    s_identb[:], start=True, stop=True),
                                sig=TE)
                            p.wait("scalar", TE, te_last)
                            sc_ev.append(p.op(
                                "scalar", lambda e, dc=dc, j2=j2, s4=s4,
                                lvl=lvl: e.activation(
                                    out=lvl[:, NH * 128 * dc + 128 * j2:
                                            NH * 128 * dc + 128 * (j2 + 1)],
                                    in_=ps_wt[:, s4:s4 + 128],
                                    func=AF.Identity), sig=SC))
                        te_trm.append((TE, te_last))
                    lv_guard[0] = te_trm[-1]
                sc_wq_lvls = SC.n

                for g in range(NG):
                    slot = g % 2
                    gbase = slot * NC * TG
                    if gidx >= 2:
                        p.wait_snap("sync", bar_marks_c[gidx - 2])
                    if half == 0:
                        mld = p.dma("sync", PLDX, gidx, lambda e, g=g,
                                    gbase=gbase: e.dma_start(
                                        out=xng_c[:, gbase:gbase + NC * TG],
                                        in_=xn_s[:, NC * TG * g:NC * TG * (g + 1)]))
                    else:
                        if gidx >= 1:
                            p.wait_snap("sync", bar_marks_c[gidx - 1])
                        if xq_snaps[g] is not None:
                            p.wait_snap("sync", xq_snaps[g])
                        mld = p.dma("sync", PLDX, gidx, lambda e, g=g:
                                    e.dma_start(
                                        out=xq_t[:],
                                        in_=xq_s[:, NC * TG * g:NC * TG * (g + 1)]))
                    if gidx >= 1:
                        p.wait_snap("vector", hp_snaps[gidx - 1])

                    if half == 0:
                        p.wait_m("vector", mld)
                        if gidx == 0:
                            p.wait("vector", VE, ve_dq)
                        if g >= 1 and xq_snaps[g - 1] is not None:
                            p.wait_snap("vector", xq_snaps[g - 1])
                        ve_xq = []
                        for cc in range(NC):
                            cs = cc % 2
                            p.op("vector", lambda e, cc=cc, cs=cs,
                                 gbase=gbase: e.tensor_scalar(
                                     out=xqi_t[:, TG * cs:TG * (cs + 1)],
                                     in0=xng_c[:, gbase + TG * cc:gbase + TG * (cc + 1)],
                                     scalar1=rsx_bc[:, 0:1], scalar2=None,
                                     op0=ALU.mult, op1=ALU.bypass))
                            ve_xq.append(p.op(
                                "vector", lambda e, cc=cc, cs=cs:
                                e.tensor_scalar(
                                    out=xq_t[:, TG * cc:TG * (cc + 1)],
                                    in0=xqi_t[:, TG * cs:TG * (cs + 1)],
                                    scalar1=-128, scalar2=127,
                                    op0=ALU.max, op1=ALU.min), sig=VE))
                            p.wait("scalar", VE, ve_xq[cc])
                            for ca in ([cc - 1] if cc < NC - 1 else
                                       [cc - 1, cc]):
                                if ca < 0:
                                    continue
                                cola = NC * TG * g + TG * ca
                                p.dma("scalar", PXQ, g, lambda e, ca=ca,
                                      cola=cola: e.dma_start(
                                          out=xq_s[:, cola:cola + TG],
                                          in_=xq_t[:, TG * ca:TG * (ca + 1)]))
                        xq_snaps[g] = p.snap(PXQ)
                        ve_xq_all = ve_xq[-1]
                        p.wait("tensor", VE, ve_xq_all)
                    else:
                        p.wait_m("tensor", mld)
                    if gidx == 0 or (half == 1 and g == 0):
                        p.wait("tensor", SC, sc_wq_lvls)

                    sc_ge = []; sc_ue = []; ve_hp = []
                    hp_marks = []
                    for jj in range(NH):
                        ic = ic0 + jj
                        gb = (jj % 2) * TG
                        if jj >= 2:
                            p.wait("tensor", SC, sc_ge[jj - 2])
                            p.wait("tensor", SC, sc_ue[jj - 2])
                        first = True
                        te_g = TE.n
                        for cc in range(NC):
                            te_g = p.op("tensor", lambda e, cc=cc, jj=jj,
                                        gb=gb, first=first, lvl=lvl_g:
                                        e.matmul(
                                            ps_g[:, gb:gb + TG],
                                            lvl[:, NH * 128 * cc + 128 * jj:
                                                NH * 128 * cc + 128 * (jj + 1)],
                                            xq_t[:, TG * cc:TG * (cc + 1)],
                                            start=first, stop=(cc == NC - 1)),
                                        sig=TE if cc == NC - 1 else None) or te_g
                            first = False
                        first = True
                        te_u = TE.n
                        for cc in range(NC):
                            te_u = p.op("tensor", lambda e, cc=cc, jj=jj,
                                        gb=gb, first=first, lvl=lvl_u:
                                        e.matmul(
                                            ps_u[:, gb:gb + TG],
                                            lvl[:, NH * 128 * cc + 128 * jj:
                                                NH * 128 * cc + 128 * (jj + 1)],
                                            xq_t[:, TG * cc:TG * (cc + 1)],
                                            start=first, stop=(cc == NC - 1)),
                                        sig=TE if cc == NC - 1 else None) or te_u
                            first = False
                        p.wait("scalar", TE, te_g)
                        if jj >= 2:
                            p.wait("scalar", VE, ve_hp[jj - 2])
                        sc_ge.append(p.op(
                            "scalar", lambda e, ic=ic, gb=gb: e.activation(
                                out=sg_t[:, gb:gb + TG],
                                in_=ps_g[:, gb:gb + TG],
                                func=AF.Silu, scale=dq_g[:, ic:ic + 1]),
                            sig=SC))
                        p.wait("scalar", TE, te_u)
                        sc_ue.append(p.op(
                            "scalar", lambda e, ic=ic, gb=gb: e.activation(
                                out=up_t[:, gb:gb + TG],
                                in_=ps_u[:, gb:gb + TG],
                                func=AF.Identity, scale=dq_u[:, ic:ic + 1]),
                            sig=SC))
                        hb = (jj % 4) * TG
                        p.wait("vector", SC, sc_ge[jj])
                        p.wait("vector", SC, sc_ue[jj])
                        if jj >= 4:
                            p.wait_m("vector", hp_marks[jj - 4])
                        ve_hp.append(p.op(
                            "vector", lambda e, gb=gb, hb=hb: e.tensor_tensor(
                                out=hp_t[:, hb:hb + TG],
                                in0=sg_t[:, gb:gb + TG],
                                in1=up_t[:, gb:gb + TG], op=ALU.mult),
                            sig=VE))
                        p.op("vector", lambda e, hb=hb: e.reduce_max(
                            out=mxcol2[:], in_=hp_t[:, hb:hb + TG],
                            axis=AX.X, apply_absolute_value=True))
                        p.fence("vector", VE)
                        p.op("vector", lambda e: e.tensor_tensor(
                            out=mxcol[:], in0=mxcol[:], in1=mxcol2[:],
                            op=ALU.max))
                        p.fence("vector", VE)
                        p.wait("scalar", VE, ve_hp[jj])
                        for ja in ([jj - 1] if jj < NH - 1 else
                                   [jj - 1, jj]):
                            if ja < 0:
                                continue
                            hba = (ja % 4) * TG
                            cola = NIC * TG * g + TG * (ic0 + ja)
                            hp_marks.append(p.dma(
                                "scalar", PHP, ja, lambda e, hba=hba,
                                cola=cola: e.dma_start(
                                    out=hp_s[:, cola:cola + TG],
                                    in_=hp_t[:, hba:hba + TG])))
                    hp_snaps[gidx] = p.snap(PHP)

                    bar_marks_c[gidx] = p.barrier(bar_engs)
                    gidx += 1

            te_hm = p.op("tensor", lambda e: e.matmul(
                ps_b[0:1, 0:128], mxcol[:], s_ident[:], start=True, stop=True),
                sig=TE)
            p.wait("vector", TE, te_hm)
            p.op("vector", lambda e: e.tensor_copy(
                out=row_a[0:1, 0:128], in_=ps_b[0:1, 0:128]))
            p.fence("vector", VE)
            p.op("vector", lambda e: e.reduce_max(
                out=smalls[0:1, 3:4], in_=row_a[0:1, 0:128], axis=AX.X))
            p.fence("vector", VE)
            p.wait("gpsimd", VE, VE.n)
            mcc2 = p.dma("gpsimd", SCC, 2, lambda e: e.dma_start(
                out=cc_in2, in_=smalls[0:1, 3:4]))
            p.wait_m("gpsimd", mcc2)
            p.op("gpsimd", lambda e: e.collective_compute(
                "AllReduce", ALU.max, replica_groups=[list(range(NCORES))],
                ins=[cc_in2.opt()], outs=[cc_out2.opt()]), sig=CC)

            p.drain("sync", PLDX)
            p.drain("sync", SWQ)
            p.drain("scalar", PXQ)
            p.drain("scalar", PHP)
            p.drain("scalar", PWR)
            _bs = p.barrier(bar_engs)
            p.wait_snap("sync", _bs)

        if "C" not in passes:
            _finalize(p, bar_engs)
            return nc
        with ExitStack() as dctx:
            def sbd(name, shape, dtype=F32):
                return dctx.enter_context(nc.sbuf_tensor(name, shape, dtype))

            lvl_d = sbd("lvl_d", [128, NIC * D], BF16)
            rq_d = sbd("rq_d", [128, 2 * NC])
            wq32_d = sbd("wq32_d", [128, CI // 2], I32)
            lvT_d = sbd("lvT_d", [128, CI], BF16)
            wlv_d = sbd("wlv_d", [128, CI])
            sgrp = sbd("sgrp", [128, NC * TG])
            xgrp = sbd("xgrp", [128, NC * TG])
            hg_t = sbd("hg_t", [128, 2 * TG])
            hi_t = sbd("hi_t", [128, 2 * TG], I32)
            hl_t = sbd("hl_t", [128, NIC * TG], BF16)
            g_t = sbd("g_t", [128, 2 * TG])
            o_t = sbd("o_t", [128, 4 * TG])

            ps_d = dctx.enter_context(nc.psum_tensor("ps_d", [128, 2 * TG], F32))
            ps_bd = dctx.enter_context(nc.psum_tensor("ps_bd", [128, 128], F32))
            ps_dt = dctx.enter_context(nc.psum_tensor("ps_dt", [128, 1024], F32))

            p.op("vector", lambda e: e.tensor_scalar(
                out=rq_d[:, NC:2 * NC], in0=swd[:], scalar1=1.0 / 127.0,
                scalar2=None, op0=ALU.mult, op1=ALU.bypass))
            p.fence("vector", VE)
            p.op("vector", lambda e: e.reciprocal(
                out=rq_d[:, 0:NC], in_=rq_d[:, NC:2 * NC]))
            p.fence("vector", VE)
            ve_int_d = []
            te_trm_d = []
            sc_ev_d = []
            p.wait("tensor", SC, SC.n)
            HCI = CI // 2
            ve_lvs_d = []
            for idx in range(2 * NC + 1):
                if idx < 2 * NC:
                    q, hh = idx // 2, idx % 2
                    sl = (idx % 2) * HCI
                    if idx >= 2:
                        p.wait("sync", VE, ve_int_d[idx - 2])
                    else:
                        p.wait("sync", VE, VE.n)
                    m = p.dma("sync", SWQ, idx, lambda e, q=q, hh=hh, sl=sl:
                              e.dma_start(
                                  out=wlv_d[:, sl:sl + HCI],
                                  in_=wd_raw[128 * q:128 * (q + 1),
                                             HCI * hh:HCI * (hh + 1)]))
                    p.wait_m("vector", m)
                    if idx >= 2:
                        p.wait_m("vector", te_trm_d[idx - 2])
                    ve_int_d.append(p.op(
                        "vector", lambda e, q=q, sl=sl: e.tensor_scalar(
                            out=wq32_d[:], in0=wlv_d[:, sl:sl + HCI],
                            scalar1=rq_d[:, q:q + 1], scalar2=None,
                            op0=ALU.mult, op1=ALU.bypass), sig=VE))
                    ve_lvs_d.append(p.op(
                        "vector", lambda e, sl=sl: e.tensor_scalar(
                            out=lvT_d[:, sl:sl + HCI], in0=wq32_d[:],
                            scalar1=-128, scalar2=127, op0=ALU.max,
                            op1=ALU.min), sig=VE))
                if idx < 1:
                    continue
                i2 = idx - 1
                q2, hh2 = i2 // 2, i2 % 2
                sl2 = (i2 % 2) * HCI
                p.wait("tensor", VE, ve_lvs_d[i2])
                te_last_d = None
                for icl in range(NH):
                    ic = NH * hh2 + icl
                    s4 = ((i2 * NH + icl) % 2) * 512
                    if len(sc_ev_d) >= 2:
                        p.wait("tensor", SC, sc_ev_d[-2])
                    te_last_d = p.op(
                        "tensor", lambda e, sl2=sl2, icl=icl, s4=s4: e.matmul(
                            ps_dt[:, s4:s4 + 128],
                            lvT_d[:, sl2 + 128 * icl:sl2 + 128 * (icl + 1)],
                            s_identb[:], start=True, stop=True), sig=TE)
                    p.wait("scalar", TE, te_last_d)
                    sc_ev_d.append(p.op(
                        "scalar", lambda e, ic=ic, q2=q2, s4=s4:
                        e.activation(
                            out=lvl_d[:, D * ic + 128 * q2:
                                      D * ic + 128 * (q2 + 1)],
                            in_=ps_dt[:, s4:s4 + 128],
                            func=AF.Identity), sig=SC))
                te_trm_d.append((TE, te_last_d))
            sc_dlvl = SC.n

            p.wait("gpsimd", CC, 2)
            mgf2 = p.dma("gpsimd", SCC, 3, lambda e: e.dma_start(
                out=gmax2[:], in_=cc_out2))
            p.wait_m("vector", mgf2)
            p.op("vector", lambda e: e.tensor_tensor(
                out=smalls[0:1, 4:5], in0=s_sig[0:1, 1:2], in1=gmax2[:],
                op=ALU.mult))
            p.fence("vector", VE)
            p.op("vector", lambda e: e.tensor_scalar(
                out=smalls[0:1, 5:6], in0=smalls[0:1, 4:5],
                scalar1=1.0 / 127.0,
                scalar2=None, op0=ALU.mult, op1=ALU.bypass))
            p.fence("vector", VE)
            p.op("vector", lambda e: e.reciprocal(
                out=smalls[0:1, 6:7], in_=smalls[0:1, 5:6]))
            p.fence("vector", VE)
            p.fence("vector", VE)
            ve_sh = VE.n
            p.wait("tensor", VE, ve_sh)
            p.op("tensor", lambda e: e.matmul(
                ps_bd[:, 0:1], ones_row[:], smalls[0:1, 6:7],
                start=True, stop=True))
            te_bc2 = p.op("tensor", lambda e: e.matmul(
                ps_bd[:, 1:2], ones_row[:], smalls[0:1, 5:6],
                start=True, stop=True), sig=TE)
            p.wait("vector", TE, te_bc2)
            p.op("vector", lambda e: e.tensor_copy(out=rsh_bc[:],
                                                   in_=ps_bd[:, 0:1]))
            p.op("vector", lambda e: e.tensor_copy(out=sh_bc[:],
                                                   in_=ps_bd[:, 1:2]))
            p.fence("vector", VE)
            ve_dqd = p.op("vector", lambda e: e.tensor_scalar(
                out=dq_d[:], in0=swd[:], scalar1=sh_bc[:, 0:1],
                scalar2=1.0 / 127.0, op0=ALU.mult, op1=ALU.mult), sig=VE)
            p.fence("vector", VE)


            bar_marks_d = [0] * (NG + 2)
            out_snaps = [None] * (NG + 2)
            for g in range(NG):
                if g >= 1:
                    p.wait_snap("sync", bar_marks_d[g - 1])
                    p.wait_snap("gpsimd", out_snaps[g - 1])
                mlds = p.dma("sync", PLDX, g, lambda e, g=g: e.dma_start(
                    out=sgrp[:], in_=s_s[:, NC * TG * g:NC * TG * (g + 1)]))
                mldx = mlds
                for c in range(NC):
                    mldx = p.dma("sync", PLDX, g, lambda e, g=g, c=c:
                                 e.dma_start(
                                     out=xgrp[:, TG * c:TG * (c + 1)],
                                     in_=xT[128 * c:128 * (c + 1),
                                            TG * g:TG * (g + 1)]))
                ve_hl = []
                for ic in range(NIC):
                    cs = ic % 2
                    col = NIC * TG * g + TG * ic
                    if ic >= 2:
                        p.wait("sync", VE, ve_hl[ic - 2])
                    mh = p.dma("sync", PHPL, ic, lambda e, cs=cs, col=col:
                               e.dma_start(
                                   out=hg_t[:, TG * cs:TG * (cs + 1)],
                                   in_=hp_s[:, col:col + TG]))
                    p.wait_m("vector", mh)
                    if g == 0 and ic == 0:
                        p.wait("vector", VE, ve_dqd)
                    p.op("vector", lambda e, cs=cs: e.tensor_scalar(
                        out=hi_t[:, TG * cs:TG * (cs + 1)],
                        in0=hg_t[:, TG * cs:TG * (cs + 1)],
                        scalar1=rsh_bc[:, 0:1], scalar2=None,
                        op0=ALU.mult, op1=ALU.bypass))
                    ve_hl.append(p.op(
                        "vector", lambda e, ic=ic, cs=cs: e.tensor_scalar(
                            out=hl_t[:, TG * ic:TG * (ic + 1)],
                            in0=hi_t[:, TG * cs:TG * (cs + 1)],
                            scalar1=-128, scalar2=127,
                            op0=ALU.max, op1=ALU.min), sig=VE))
                ve_hl_all = ve_hl[-1]

                p.wait("tensor", VE, ve_hl_all)
                if g == 0:
                    p.wait("tensor", SC, sc_dlvl)
                sc_g = []; ve_o = []; gp_o = []; out_marks = []
                for mch in range(NC):
                    db = (mch % 2) * TG
                    if mch >= 2:
                        p.wait("tensor", SC, sc_g[mch - 2])
                    first = True
                    te_d = TE.n
                    for ic in range(NIC):
                        te_d = p.op("tensor", lambda e, ic=ic, mch=mch, db=db,
                                    first=first: e.matmul(
                                        ps_d[:, db:db + TG],
                                        lvl_d[:, D * ic + 128 * mch:D * ic + 128 * (mch + 1)],
                                        hl_t[:, TG * ic:TG * (ic + 1)],
                                        start=first, stop=(ic == NIC - 1)),
                                    sig=TE if ic == NIC - 1 else None) or te_d
                        first = False
                    p.wait("scalar", TE, te_d)
                    if g == 0 and mch == 0:
                        p.wait("scalar", VE, ve_dqd)
                    if mch >= 2:
                        p.wait("scalar", VE, ve_o[mch - 2])
                    sc_g.append(p.op("scalar", lambda e, mch=mch, db=db:
                                     e.activation(
                                         out=g_t[:, db:db + TG],
                                         in_=ps_d[:, db:db + TG],
                                         func=AF.Silu,
                                         scale=dq_d[:, mch:mch + 1]),
                                     sig=SC))
                    ob = (mch % 2) * 2 * TG
                    p.wait("vector", SC, sc_g[mch])
                    p.wait_m("vector", mlds)
                    if mch >= 2:
                        p.wait("vector", GP, gp_o[mch - 2])
                    ve_o.append(p.op(
                        "vector", lambda e, mch=mch, db=db, ob=ob:
                        e.tensor_tensor(
                            out=o_t[:, ob:ob + TG],
                            in0=sgrp[:, TG * mch:TG * (mch + 1)],
                            in1=g_t[:, db:db + TG], op=ALU.mult), sig=VE))
                    p.wait("gpsimd", VE, ve_o[mch])
                    p.wait_m("gpsimd", mldx)
                    if mch >= 2:
                        p.wait_m("gpsimd", out_marks[mch - 2])
                    gp_o.append(p.op(
                        "gpsimd", lambda e, mch=mch, ob=ob: e.tensor_tensor(
                            out=o_t[:, ob + TG:ob + 2 * TG],
                            in0=o_t[:, ob:ob + TG],
                            in1=xgrp[:, TG * mch:TG * (mch + 1)],
                            op=ALU.add), sig=GP))
                    for ma in ([mch - 1] if mch < NC - 1 else
                               [mch - 1, mch]):
                        if ma < 0:
                            continue
                        oba = (ma % 2) * 2 * TG
                        cola = NC * TG * g + TG * ma
                        out_marks.append(p.dma(
                            "gpsimd", POUT, ma, lambda e, oba=oba, cola=cola:
                            e.dma_start(out=outT[:, cola:cola + TG],
                                        in_=o_t[:, oba + TG:oba + 2 * TG])))
                out_snaps[g] = p.snap(POUT)

                bar_marks_d[g] = p.barrier(bar_engs)

            p.drain("sync", PLDX)
            p.drain("sync", PHPL)
            p.drain("sync", SWQ)
            p.drain("gpsimd", SCC)
            p.drain("gpsimd", POUT)

        with nc.Block() as block:
            block.sync(lambda e: [f(e) for f in p.items["sync"]])
            block.scalar(lambda e: [f(e) for f in p.items["scalar"]])
            block.vector(lambda e: [f(e) for f in p.items["vector"]])
            block.tensor(lambda e: [f(e) for f in p.items["tensor"]])
            block.gpsimd(lambda e: [f(e) for f in p.items["gpsimd"]])

    return nc


def _prep_inputs(inputs):
    x = np.ascontiguousarray(np.asarray(inputs["L0_attn_x"], dtype=np.float32))
    wq = np.asarray(inputs["L0_attn_Wq"], dtype=np.float32)
    wk = np.asarray(inputs["L0_attn_Wk"], dtype=np.float32)
    wv = np.asarray(inputs["L0_attn_Wv"], dtype=np.float32)
    bq = np.asarray(inputs["L0_attn_bq"], dtype=np.float32)
    bk = np.asarray(inputs["L0_attn_bk"], dtype=np.float32)
    bv = np.asarray(inputs["L0_attn_bv"], dtype=np.float32)
    gam = np.asarray(inputs["L0_attn_gamma"], dtype=np.float32)
    bet = np.asarray(inputs["L0_attn_beta"], dtype=np.float32)
    clip_x = np.asarray(inputs["L0_ffn_clip_x"], dtype=np.float32)
    clip_d = np.asarray(inputs["L0_ffn_clip_down"], dtype=np.float32)
    wg = np.asarray(inputs["L0_ffn_w_gate"], dtype=np.float32)
    wu = np.asarray(inputs["L0_ffn_w_up"], dtype=np.float32)
    wd = np.asarray(inputs["L0_ffn_w_down"], dtype=np.float32)

    ec = np.zeros((16, 1024), dtype=np.float32)
    for c in range(NC):
        ec[2 * c, 128 * c:128 * c + 64] = 1.0
        ec[2 * c + 1, 128 * c + 64:128 * (c + 1)] = 1.0

    shared = {
        "wqT": np.ascontiguousarray(wq.T),
        "wkT": np.ascontiguousarray(wk.T),
        "wvT": np.ascontiguousarray(wv.T),
        "bq_pack": np.ascontiguousarray(bq.reshape(NC, 128).T),
        "bk_row": np.ascontiguousarray(bk.reshape(1, D)),
        "bv_row": np.ascontiguousarray(bv.reshape(1, D)),
        "gamma_pack": np.ascontiguousarray(gam.reshape(NC, 128).T),
        "beta_pack": np.ascontiguousarray(bet.reshape(NC, 128).T),
        "clip2": np.ascontiguousarray(
            np.stack([clip_x[0], clip_d[0]]).reshape(1, 2)),
        "wgT": np.ascontiguousarray(wg.T),
        "wuT": np.ascontiguousarray(wu.T),
        "wg_raw": np.ascontiguousarray(wg),
        "wu_raw": np.ascontiguousarray(wu),
        "wd_raw": np.ascontiguousarray(wd),
        "wdT": np.ascontiguousarray(wd.T),
        "ident": np.eye(128, dtype=np.float32),
        "ec": ec,
        "emask": np.kron(np.eye(2, dtype=np.float32),
                         np.ones((64, 64), np.float32)),
        "ident_bf": np.eye(128).astype(ml_dtypes.bfloat16),
        "ones128_f": np.ones((128, 128), np.float32),
        "ones128_b": np.ones((128, 128)).astype(ml_dtypes.bfloat16),
        "eps_col": np.full((128, 1), 1e-5, np.float32),
    }
    in_maps = []
    for b in range(NCORES):
        m = dict(shared)
        m["xT"] = np.ascontiguousarray(x[b].T)
        in_maps.append(m)
    return in_maps


def kernel(**inputs):
    in_maps = _prep_inputs(inputs)
    nc = build()
    res = run_bass_kernel_spmd(nc, in_maps, core_ids=list(range(NCORES)))
    out = np.empty((NCORES, NTOK, D), dtype=np.float32)
    for b in range(NCORES):
        r = np.asarray(res.results[b]["outT"])
        r4 = r.reshape(128, NG, NC, TG)
        feat_major = np.transpose(r4, (2, 0, 1, 3)).reshape(D, NTOK)
        out[b] = feat_major.T
    return out
